# version 10
# baseline (speedup 1.0000x reference)
import numpy as np
import ml_dtypes
import concourse.bass as bass
import concourse.mybir as mybir
from concourse.bass_utils import run_bass_kernel_spmd

F32 = mybir.dt.float32
BF16 = mybir.dt.bfloat16
U8 = mybir.dt.uint8
AF = mybir.ActivationFunctionType
ALU = mybir.AluOpType
AX = mybir.AxisListType
bfloat16 = ml_dtypes.bfloat16

B, S, D = 4, 2048, 4096
TOK = 1024
NIC = 32
NTB = 8
SCALE_A = float(192.0 ** -0.5)
SCALE_B = float(128.0 ** -0.5)
NEG_BIG = -1e30
EPS = 1e-5
ENGINES = ("tensor", "vector", "scalar", "gpsimd", "sync")


class Em:

    def __init__(self, nc):
        self.nc = nc
        self.ops = {e: [] for e in ENGINES}
        self.meta = {}
        self.counts = {}
        self.hwm = {e: {} for e in ENGINES}
        self.semh = {}

    def sem(self, name):
        if name not in self.counts:
            self.counts[name] = 0
            self.semh[name] = self.nc.alloc_semaphore(name)
        return self.semh[name]

    OWN = {"tensor": "sPE", "vector": "sDV", "scalar": "sAC",
           "gpsimd": "sGP"}

    def emit(self, eng, fn, waits=(), inc=None, inc_by=1, manual=False):
        own = self.OWN.get(eng)
        if own is not None:
            self.sem(own)
            waits = list(waits) + [(own, self.counts[own])]
            if inc is None:
                inc = own
        w = []
        flat = []
        for t in waits:
            if isinstance(t, list):
                flat.extend(t)
            else:
                flat.append(t)
        for t in flat:
            if t is None:
                continue
            s, v = t
            if v <= 0 or v <= self.hwm[eng].get(s, 0):
                continue
            self.hwm[eng][s] = v
            w.append((s, v))
        tok = None
        if inc is not None:
            self.sem(inc)
            self.counts[inc] += inc_by
            tok = (inc, self.counts[inc])
        self.ops[eng].append((fn, w, None if (inc is None or manual)
                              else (inc, inc_by),
                              getattr(fn, "__code__", None)
                              and fn.__code__.co_firstlineno))
        self.meta.setdefault(eng, []).append(
            (tok, getattr(fn, "__code__", None) and fn.__code__.co_firstlineno))
        return tok


class Slot:
    __slots__ = ("buf", "sem", "uses", "prod_tok", "cons_toks")

    def __init__(self, buf, sem):
        self.buf = buf
        self.sem = sem
        self.uses = 0
        self.prod_tok = None
        self.cons_toks = []


class Ring:
    def __init__(self, em, name, bufs, dma=True, init_cons=None):
        self.em = em
        for i in range(len(bufs)):
            em.sem(f"{name}{i}")
        self.slots = [Slot(b, f"{name}{i}") for i, b in enumerate(bufs)]
        if init_cons:
            for sl in self.slots:
                sl.cons_toks = list(init_cons)
        self.k = 0
        self.dma = dma

    def acquire(self):
        s = self.slots[self.k % len(self.slots)]
        self.k += 1
        return s

    def produce(self, s, eng, fn, extra_waits=(), n_dma=1):
        waits = list(s.cons_toks) + list(extra_waits)
        s.cons_toks = []
        s.uses += 1
        if self.dma and n_dma > 1:
            tok = self.em.emit(eng, fn, waits=waits, inc=s.sem,
                               inc_by=16 * n_dma, manual=True)
        else:
            tok = self.em.emit(eng, fn, waits=waits, inc=s.sem,
                               inc_by=16 if self.dma else 1)
        s.prod_tok = tok
        return tok

    def consume(self, s, eng, fn, extra_waits=(), inc=None, inc_by=1):
        tok = self.em.emit(eng, fn, waits=[s.prod_tok] + list(extra_waits),
                           inc=inc, inc_by=inc_by)
        if tok is not None:
            s.cons_toks.append(tok)
        return tok


def _u16(a):
    return np.ascontiguousarray(np.asarray(a).astype(bfloat16)).view(np.uint16)


def build(phases="ABCD", a_stop=9):
    nc = bass.Bass("TRN2", target_bir_lowering=False)
    em = Em(nc)
    SEMH = em.semh

    def inp(name, shape):
        return nc.dram_tensor(name, list(shape), F32, kind="ExternalInput").ap()

    xf = inp("xf", (S, D))
    xo = inp("xo", (TOK, D))
    wq_my = inp("wq_my", (D, 2048))
    wk_my = inp("wk_my", (D, 2048))
    wv_my = inp("wv_my", (D, 2048))
    wo = inp("wo", (D, D))
    qT = inp("qT", (D, D))
    kT = inp("kT", (D, 1024))
    vT = inp("vT", (D, 1024))
    oT = inp("oT", (D, D))
    fT = inp("fT", (D, D))
    gwT = inp("gwT", (D, 8))
    ics = inp("ics", (8, D))
    ocs = inp("ocs", (8, D))
    fbias = inp("fbias", (1, D))
    qN = inp("qN", (D, D))
    kN = inp("kN", (1024, D))
    vN = inp("vN", (1024, D))
    oN = inp("oN", (D, D))
    qscg = inp("qscg", (128, 32))
    kscg = inp("kscg", (128, 8))
    vscg = inp("vscg", (128, 8))
    oscg = inp("oscg", (128, 32))
    y = nc.dram_tensor("y", [TOK, D], F32, kind="ExternalOutput").ap()

    Qa = nc.dram_tensor("Qa", [2048, S], BF16).ap()
    Ka = nc.dram_tensor("Ka", [2048, S], BF16).ap()
    Va = nc.dram_tensor("Va", [S, 2048], BF16).ap()
    attn_ctr = nc.dram_tensor("attn_ctr", [2048, S], BF16).ap()
    attn_gth = nc.dram_tensor("attn_gth", [4096, S], BF16).ap()
    Qb = nc.dram_tensor("Qb", [D, TOK], BF16).ap()
    Kc = nc.dram_tensor("Kc", [1024, TOK], BF16).ap()
    Vc = nc.dram_tensor("Vc", [TOK, 1024], BF16).ap()
    Kg = nc.dram_tensor("Kg", [2048, TOK], BF16).ap()
    Vg = nc.dram_tensor("Vg", [2048, 1024], BF16).ap()
    Ab = nc.dram_tensor("Ab", [D, TOK], BF16).ap()

    identB_d = nc.inline_tensor(_u16(np.eye(128)), name="identB_c")
    identF_d = nc.inline_tensor(np.eye(128, dtype=np.float32), name="identF_c")
    maskT_d = nc.inline_tensor(
        _u16(np.tril(np.full((128, 128), NEG_BIG, np.float32), -1)),
        name="maskT_c")
    onesr_d = nc.inline_tensor(_u16(np.ones((1, 128))), name="onesr_c")
    onesc_d = nc.inline_tensor(_u16(np.ones((128, 1))), name="onesc_c")

    res_t = nc.alloc_sbuf_tensor("res", [128, NTB * D], BF16)
    big1_t = nc.alloc_sbuf_tensor("big1", [128, 65536], U8)
    scr_t = nc.alloc_sbuf_tensor("scr", [128, 73728], U8)
    identB = nc.alloc_sbuf_tensor("identB", [128, 128], BF16).ap()
    identF = nc.alloc_sbuf_tensor("identF", [128, 128], F32).ap()
    maskT = nc.alloc_sbuf_tensor("maskT", [128, 128], BF16).ap()
    onesr = nc.alloc_sbuf_tensor("onesr", [1, 128], BF16).ap()
    onesc = nc.alloc_sbuf_tensor("onesc", [128, 1], BF16).ap()
    res = res_t.ap()
    big1 = big1_t.ap()
    scr = scr_t.ap()

    def resv(tb):
        return res[:, tb * D:(tb + 1) * D]

    def view(base, off, cols, dt=BF16, parts=128):
        nb = cols * (4 if dt == F32 else 2)
        return base[0:parts, off:off + nb].bitcast(dt)

    BIG = view(big1, 0, 32 * TOK)

    def bigv(ic, c0, c1):
        return BIG[:, ic * TOK + c0:ic * TOK + c1]

    SLOT0, SGN0, STAT0, PH0 = 0, 8192, 12288, 28672
    wring = Ring(em, "w", [view(scr, SLOT0 + i * 1024, 512) for i in range(8)])
    sgnring = Ring(em, "g", [view(scr, SGN0 + i * 1024, 512) for i in range(4)],
                   dma=False)

    so = [STAT0]

    def stat(cols, dt=F32, parts=128):
        nb = cols * (4 if dt == F32 else 2)
        off = so[0]
        so[0] += (nb + 31) // 32 * 32
        assert so[0] <= PH0, "stat overflow"
        return scr[0:parts, off:off + nb].bitcast(dt)

    t_sums = stat(1)
    t_sumsq = stat(1)
    t_m = stat(1)
    t_v1 = stat(1)
    t_v2 = stat(1)
    t_std = stat(1)
    t_r = stat(1)
    t_negmr = stat(1)
    t_mx = stat(1)
    t_mxc = stat(4)
    t_nsx = stat(1)
    t_sumsP = [stat(1), stat(1)]
    t_rc = [stat(1), stat(1)]
    t_row = stat(2048, BF16, parts=1)
    t_srowb = stat(512, BF16, parts=1)
    sg_raw = {"q": stat(32), "k": stat(8), "v": stat(8), "o": stat(32)}
    sg_fin = {"q": stat(32), "k": stat(8), "v": stat(8), "o": stat(32)}
    scg_sb = {"q": stat(32), "k": stat(8), "v": stat(8), "o": stat(32)}
    t_red = stat(1)
    lnt = stat(4096, BF16)

    psA = nc.alloc_psum_tensor("psA", [128, 2048], F32).ap()
    psB = nc.alloc_psum_tensor("psB", [128, 2048], F32).ap()

    def bank(i):
        src = psA if i < 4 else psB
        j = i % 4
        return src[:, j * 512:(j + 1) * 512]

    def fn_consts(g):
        g.dma_start(identB[:], identB_d.ap().bitcast(BF16)).then_inc(SEMH["qC"], 16)
        g.dma_start(identF[:], identF_d.ap()).then_inc(SEMH["qC"], 16)
        g.dma_start(maskT[:], maskT_d.ap().bitcast(BF16)).then_inc(SEMH["qC"], 16)
        g.dma_start(onesr[:], onesr_d.ap().bitcast(BF16)).then_inc(SEMH["qC"], 16)
        g.dma_start(onesc[:], onesc_d.ap().bitcast(BF16)).then_inc(SEMH["qC"], 16)
        return None
    em.sem("qC")
    tok_const = em.emit("gpsimd", fn_consts, inc="qC", inc_by=16 * 5,
                        manual=True)

    em.sem("qRES")
    res_toks = []
    for tb in range(NTB):
        def fn_res(g, tb=tb):
            return g.dma_start(resv(tb), xo[tb * 128:(tb + 1) * 128, :])
        res_toks.append(em.emit("gpsimd", fn_res, inc="qRES", inc_by=16))
    tok_res_all = res_toks[-1]


    class StoreRing:

        def __init__(self, name, n=2):
            self.names = [f"{name}{i}" for i in range(n)]
            for nm in self.names:
                em.sem(nm)

        def store(self, k, fn, waits=()):
            nm = self.names[k % len(self.names)]
            return em.emit("sync", fn, waits=waits, inc=nm, inc_by=16)

        def all_toks(self):
            return [(nm, em.counts[nm]) for nm in self.names]

    def fence():
        return [(nm, em.counts.get(nm, 0))
                for nm in ("sPE", "sDV", "sAC", "sGP")]

    rowsem_n = [0]

    def rowsem():
        rowsem_n[0] += 1
        name = f"qR{rowsem_n[0]}"
        em.sem(name)
        return name

    p0_nat = Ring(em, "p0n", [view(scr, 67584 + i * 1024, 512)
                              for i in range(4)])
    p0_marks = {}

    def p0_gen():
        def s_units(WN, key, nblk):
            grid = sg_raw[key]

            def consume(u):
                sl, ob, piece = u

                def fn_red(e, sl=sl):
                    return e.tensor_reduce(t_red[:], sl.buf, axis=AX.X,
                                           op=ALU.add,
                                           apply_absolute_value=True)
                rt = p0_nat.consume(sl, "vector", fn_red)

                def fn_acc(e, ob=ob, piece=piece, grid=grid):
                    if piece == 0:
                        return e.tensor_copy(grid[:, ob:ob + 1], t_red[:])
                    return e.tensor_tensor(grid[:, ob:ob + 1],
                                           grid[:, ob:ob + 1],
                                           t_red[:], op=ALU.add)
                em.emit("vector", fn_acc, waits=[rt])

            pend = []
            for ob in range(nblk):
                for piece in range(8):
                    sl = p0_nat.acquire()
                    p0_nat.produce(sl, "gpsimd",
                                   (lambda e, WN=WN, ob=ob, piece=piece,
                                    sl=sl: e.dma_start(
                                        sl.buf,
                                        WN[ob * 128:(ob + 1) * 128,
                                           piece * 512:(piece + 1) * 512])))
                    pend.append((sl, ob, piece))
                    if len(pend) >= 3:
                        consume(pend.pop(0))
                    yield 1
            while pend:
                consume(pend.pop(0))

        for key, WN, nblk in (("q", qN, 32), ("k", kN, 8), ("v", vN, 8),
                              ("o", oN, 32)):
            yield from s_units(WN, key, nblk)
        p0_marks["s"] = [("sDV", em.counts["sDV"])]

    _p0 = p0_gen()
    p0_active = [True]

    def tick(n=1):
        for _ in range(n):
            if not p0_active[0]:
                return
            if next(_p0, None) is None:
                p0_active[0] = False
                return

    def p0_drain():
        while p0_active[0]:
            tick()
        rsemg = rowsem()

        def fn_g(g):
            g.dma_start(scg_sb["q"], qscg[:, :]).then_inc(SEMH[rsemg], 16)
            g.dma_start(scg_sb["k"], kscg[:, :]).then_inc(SEMH[rsemg], 16)
            g.dma_start(scg_sb["v"], vscg[:, :]).then_inc(SEMH[rsemg], 16)
            g.dma_start(scg_sb["o"], oscg[:, :]).then_inc(SEMH[rsemg], 16)
            return None
        gt = em.emit("sync", fn_g, inc=rsemg, inc_by=64, manual=True)
        fin = {}
        for key in ("q", "k", "v", "o"):
            t1 = em.emit("vector",
                         (lambda v, key=key: v.tensor_scalar(
                             sg_fin[key][:], sg_raw[key][:], 1.0 / D, 1e-8,
                             op0=ALU.mult, op1=ALU.max)),
                         waits=p0_marks["s"] + [gt])
            fin[key] = em.emit("vector",
                               (lambda v, key=key: v.tensor_tensor(
                                   sg_fin[key][:], sg_fin[key][:],
                                   scg_sb[key][:], op=ALU.mult)),
                               waits=[t1])
            if key == "q":
                fin[key] = em.emit("vector",
                                   (lambda v: v.tensor_scalar(
                                       sg_fin["q"][:], sg_fin["q"][:],
                                       SCALE_B, None, op0=ALU.mult)),
                                   waits=[fin[key]])
        p0_marks["fin"] = [fin[k] for k in fin]
        return p0_marks["s"] + p0_marks["fin"]

    pack_cons = [None, None]
    ln_tp_last = [None]

    def emit_ln_block(src_ap, tbcol, extra_w=(), psum_guard=()):
        tw = [t for t in extra_w if t is not None]
        tok1 = em.emit("vector",
                       lambda v: v.reduce_sum(t_sums[:], src_ap, axis=AX.X),
                       waits=tw)
        tok2 = em.emit("scalar",
                       lambda s: s.activation(lnt[:], src_ap, AF.Square,
                                              accum_out=t_sumsq[:]),
                       waits=tw + [tok1, ln_tp_last[0]])
        s1 = em.emit("vector", lambda v: v.tensor_scalar(
            t_m[:], t_sums[:], 1.0 / D, None, op0=ALU.mult), waits=[tok1])
        s2 = em.emit("vector", lambda v: v.tensor_scalar(
            t_v1[:], t_sumsq[:], 1.0 / D, EPS, op0=ALU.mult, op1=ALU.add),
            waits=[tok2])
        s3 = em.emit("vector", lambda v: v.tensor_tensor(
            t_v2[:], t_m[:], t_m[:], op=ALU.mult), waits=[s1])
        s4 = em.emit("vector", lambda v: v.scalar_tensor_tensor(
            t_v1[:], t_v2[:], -1.0, t_v1[:], op0=ALU.mult, op1=ALU.add),
            waits=[s2, s3])
        s5 = em.emit("scalar", lambda s: s.activation(
            t_std[:], t_v1[:], AF.Sqrt), waits=[s4])
        s6 = em.emit("vector", lambda v: v.reciprocal(t_r[:], t_std[:]),
                     waits=[s5])
        s7 = em.emit("vector", lambda v: v.scalar_tensor_tensor(
            t_negmr[:], t_m[:], -1.0, t_r[:], op0=ALU.mult, op1=ALU.mult),
            waits=[s6])
        tok5 = em.emit("vector", lambda v: v.tensor_scalar(
            lnt[:], src_ap, t_r[:, 0:1], t_negmr[:, 0:1],
            op0=ALU.mult, op1=ALU.add), waits=[s7, tok2])

        last_copy = None
        last_tp = None
        for grp in range(8):
            pk = grp % 2
            dstp = psB[:, pk * 512:(pk + 1) * 512].bitcast(BF16)

            def fn_tp(t, grp=grp, dstp=dstp):
                out = []
                for j in range(4):
                    ic = grp * 4 + j
                    out.append(t.transpose(dstp[:, j * 128:(j + 1) * 128],
                                           lnt[:, ic * 128:(ic + 1) * 128],
                                           identB))
                return out
            pw = [tok5, tok_const, pack_cons[pk]]
            if grp < 2:
                pw += [t for t in psum_guard if t is not None]
            tok_tp = em.emit("tensor", fn_tp, waits=pw, inc_by=4)
            last_tp = tok_tp

            def fn_cp(s, grp=grp, dstp=dstp, tbcol=tbcol):
                d = BIG.rearrange("p (c t) -> p c t", c=NIC)[
                    :, grp * 4:(grp + 1) * 4, tbcol * 128:(tbcol + 1) * 128]
                return s.copy(d, dstp[:, 0:512].rearrange(
                    "p (c t) -> p c t", c=4))
            last_copy = em.emit("scalar", fn_cp, waits=[tok_tp])
            pack_cons[pk] = last_copy
        ln_tp_last[0] = last_tp
        return last_copy

    a_end = res_toks

    def attention_pipe(pb_, PT, rbc_sb, scale, causal, bufs_state):
        st = bufs_state
        st["pend"] = None

        def emit_tail(p):
            qb, span, ex, rc = p["qb"], p["span"], p["ex"], p["rc"]
            qbl = qb % 4

            def fn_rT(t, qbl=qbl, pk2=qb % 2):
                return t.transpose(
                    psB[0:1, 1024 + qbl * 128:1024 + (qbl + 1) * 128],
                    t_rc[pk2][:, 0:1], identF)
            rT = em.emit("tensor", fn_rT, waits=[rc, st["row_free"]],
                         inc="sPE")
            st["rT_prev"][qb % 2] = rT
            st["rows"].append(rT)
            nch = (span + 127) // 128
            pb = pb_[qb % 2]
            last_pt = None
            for p0 in range(0, nch, 4):
                n = min(4, nch - p0)
                pk = (p0 // 4) % 2
                bankc = 0 if pk == 0 else 1536

                def fn_pt(t, p0=p0, n=n, pb=pb, bankc=bankc):
                    return [t.transpose(
                        psB[:, bankc:bankc + 512].bitcast(BF16)[
                            :, j * 128:(j + 1) * 128],
                        pb[:, (p0 + j) * 128:(p0 + j + 1) * 128],
                        identB) for j in range(n)]
                tpk = em.emit("tensor", fn_pt,
                              waits=[ex, st["pt_pack_free"][pk]],
                              inc="sPE", inc_by=n)
                last_pt = tpk

                if pk == 0:
                    def fn_ptc(sc_, p0=p0, n=n, qbl=qbl, bankc=bankc, PT=PT):
                        d = PT.rearrange("p (c t) -> p c t", c=16)[
                            :, p0:p0 + n, qbl * 128:(qbl + 1) * 128]
                        srcv = psB[:, bankc:bankc + 512].bitcast(BF16)[
                            :, 0:n * 128].rearrange("p (c t) -> p c t", c=n)
                        return sc_.copy(d, srcv)
                    ptc = em.emit("scalar", fn_ptc, waits=[tpk], inc="sAC")
                else:
                    def fn_ptc(v_, p0=p0, n=n, qbl=qbl, bankc=bankc, PT=PT):
                        d = PT.rearrange("p (c t) -> p c t", c=16)[
                            :, p0:p0 + n, qbl * 128:(qbl + 1) * 128]
                        srcv = psB[:, bankc:bankc + 512].bitcast(BF16)[
                            :, 0:n * 128].rearrange("p (c t) -> p c t", c=n)
                        return v_.tensor_copy(d, srcv)
                    ptc = em.emit("vector", fn_ptc, waits=[tpk], inc="sDV")
                st["pt_pack_free"][pk] = ptc
                st["pt_tile"].append(ptc)
            st["p_free"][qb % 2] = last_pt

        def emit_tile(te, tile):
            def fn_rowcp(v, tile=tile):
                return v.tensor_copy(
                    t_row[0:1, (tile % 4) * 512:(tile % 4 + 1) * 512],
                    psB[0:1, 1024:1536])
            rowcp = em.emit("vector", fn_rowcp,
                            waits=st["rows"] + [st["trow_free"]], inc="sDV")

            def fn_rbc(t, tile=tile):
                return t.matmul(
                    psB[:, 1024:1536], onesr,
                    t_row[0:1, (tile % 4) * 512:(tile % 4 + 1) * 512],
                    start=True, stop=True)
            rbct = em.emit("tensor", fn_rbc,
                           waits=[rowcp, st["rbc_free"]], inc="sPE")
            st["trow_free"] = rbct
            rbccp = em.emit("vector",
                            (lambda v, rbc_sb=rbc_sb: v.tensor_copy(
                                rbc_sb, psB[:, 1024:1536])),
                            waits=[rbct, st["rbc_free"]], inc="sDV")
            st["row_free"] = rbccp
            nchT = te["nchT"]
            vsl = te["vsl"]

            def fn_av(t, nchT=nchT, vsl=vsl, PT=PT):
                return [t.matmul(
                    psB[:, 512:1024],
                    vsl.buf[:, c * 128:(c + 1) * 128],
                    PT.rearrange("p (c t) -> p c t", c=16)[:, c, :],
                    start=(c == 0), stop=(c == nchT - 1))
                    for c in range(nchT)]
            av = em.emit("tensor", fn_av,
                         waits=[vsl.prod_tok, st["ptzero"], st["av_prev"]]
                         + st["pt_tile"], inc="sPE", inc_by=nchT)
            vsl.cons_toks.append(av)
            avev = te["evac"](av, rbccp, rbc_sb)
            st["av_prev"] = avev
            st["rbc_free"] = avev
            st["rows"] = []
            st["pt_tile"] = []

        def push(task):
            if task.get("zero") is not None:
                tile = task["zero"]

                def fn_z(v, tile=tile, PT=PT):
                    return [v.memset(
                        PT.rearrange("p (c t) -> p c t", c=16)[
                            :, tile * 4 + cc:tile * 4 + cc + 1,
                            0:cc * 128], 0.0) for cc in range(1, 4)]
                st["ptzero"] = em.emit("vector", fn_z,
                                       waits=[st["av_prev"]], inc="sDV",
                                       inc_by=3)
                return
            qb, span = task["qb"], task["span"]
            ksl, qsl, qcol = task["ksl"], task["qsl"], task["qcol"]
            nfull, rem = span // 512, span % 512
            dc = qb // 4

            def fn_sc(t, qb=qb, nfull=nfull, rem=rem, dc=dc,
                      ksl=ksl, qsl=qsl, qcol=qcol, causal=causal):
                lhs = qsl.buf[:, qcol * 128:(qcol + 1) * 128]
                out = []
                for c in range(nfull):
                    out.append(t.matmul(
                        psA[:, c * 512:(c + 1) * 512], lhs,
                        ksl.buf[:, c * 512:(c + 1) * 512],
                        start=True,
                        stop=(not causal) or (c != dc)))
                if rem:
                    out.append(t.matmul(
                        psA[:, nfull * 512:nfull * 512 + rem], lhs,
                        ksl.buf[:, nfull * 512:nfull * 512 + rem],
                        start=True, stop=(nfull != dc)))
                if causal:
                    out.append(t.matmul(psA[:, qb * 128:qb * 128 + 128],
                                        maskT, identB, start=False,
                                        stop=True))
                return out
            n_sc = nfull + (1 if rem else 0) + (1 if causal else 0)
            sc = em.emit("tensor", fn_sc,
                         waits=[ksl.prod_tok, qsl.prod_tok,
                                st["exp_prev"], tok_const]
                         + task.get("extra", []),
                         inc="sPE", inc_by=n_sc)
            ksl.cons_toks.append(sc)
            qsl.cons_toks.append(sc)
            nchunks = nfull + (1 if rem else 0)
            mx_toks = []
            for c in range(nchunks):
                c0, c1 = c * 512, min((c + 1) * 512, span)
                ctok = (sc[0], sc[1] - (n_sc - 1 - c))
                if causal and c == qb // 4:
                    ctok = sc
                mx_toks.append(em.emit(
                    "vector",
                    (lambda v, c=c, c0=c0, c1=c1: v.reduce_max(
                        t_mxc[:, c:c + 1], psA[:, c0:c1], axis=AX.X)),
                    waits=[ctok, st["exp_prev"]]))
            mx = em.emit("vector",
                         (lambda v, nchunks=nchunks: (
                             v.reduce_max(t_nsx[:], t_mxc[:, 0:nchunks],
                                          axis=AX.X, negate=True))),
                         waits=mx_toks + [st["exp_prev"]])
            pb = pb_[qb % 2]

            pk2 = qb % 2

            def fn_exp(s_, pb=pb, span=span, pk2=pk2):
                return s_.activation(pb[:, 0:span], psA[:, 0:span],
                                     AF.Exp, bias=t_nsx[:, 0:1],
                                     accum_out=t_sumsP[pk2][:])
            ex = em.emit("scalar", fn_exp,
                         waits=[mx, st["p_free"][pk2], st["rc_prev"][pk2]],
                         inc="sAC")
            st["exp_prev"] = ex
            rc = em.emit("vector",
                         (lambda v, pk2=pk2: v.reciprocal(
                             t_rc[pk2][:], t_sumsP[pk2][:])),
                         waits=[ex, st["rT_prev"][pk2]], inc="sDV")
            st["rc_prev"][pk2] = rc
            if st["pend"] is not None:
                emit_tail(st["pend"])
                if st["pend"]["tile_end"] is not None:
                    emit_tile(st["pend"]["tile_end"], st["pend"]["tile"])
            st["pend"] = {"qb": qb, "span": span, "ex": ex, "rc": rc,
                          "tile_end": task.get("tile_end"),
                          "tile": task.get("tile", 0)}

        def flush():
            if st["pend"] is not None:
                emit_tail(st["pend"])
                if st["pend"]["tile_end"] is not None:
                    emit_tile(st["pend"]["tile_end"], st["pend"]["tile"])
                st["pend"] = None
        return push, flush


    if "A" in phases:
        xbufs = [view(scr, PH0 + i * 8192, 4096) for i in range(2)]
        qk_stg = [view(scr, PH0 + 16384 + i * 4096, 2048) for i in range(2)]
        v_stg = [view(scr, PH0 + 24576 + i * 1024, 512) for i in range(2)]
        xring = Ring(em, "xa", xbufs)
        stQK = StoreRing("stQK")
        stV = StoreRing("stV")
        stAT = StoreRing("stAT")

        qk_psum_free = [None, None]
        qk_stg_free = [None, None]
        v_bank_free = [None] * 8
        v_stg_free = [None, None]

        for half in range(2):
            ln_last = None
            for tbl in range(8):
                tbg = half * 8 + tbl
                sl = xring.acquire()
                xring.produce(sl, "gpsimd",
                              (lambda g, tbg=tbg, sl=sl: g.dma_start(
                                  sl.buf, xf[tbg * 128:(tbg + 1) * 128, :])))
                guard = ()
                if half == 1 and tbl == 0:
                    guard = qk_psum_free + [t for t in v_bank_free
                                            if t is not None]
                tok = emit_ln_block(sl.buf, tbl, extra_w=[sl.prod_tok],
                                    psum_guard=guard)
                sl.cons_toks.append(tok)
                ln_last = tok
            lnfm_rdy = ln_last

            for W, dstD, qscl in ((wq_my, Qa, SCALE_A), (wk_my, Ka, None)):
                for fp in range(8):
                    pk = fp % 2
                    ps = psA if pk == 0 else psB
                    last_mm = None
                    for ic in range(NIC):
                        sl = wring.acquire()
                        wring.produce(sl, "gpsimd",
                                      (lambda g, W=W, fp=fp, ic=ic, sl=sl:
                                       g.dma_start(sl.buf[:, 0:256],
                                                   W[ic * 128:(ic + 1) * 128,
                                                     fp * 256:(fp + 1) * 256])))
                        pw = [lnfm_rdy]
                        if ic == 0:
                            pw.append(qk_psum_free[pk])
                            pw += [t for t in v_bank_free if t is not None]

                        def fn_mm(t, ic=ic, sl=sl, ps=ps):
                            out = []
                            for fc in range(2):
                                for tt in range(2):
                                    out.append(t.matmul(
                                        ps[:, fc * 1024 + tt * 512:
                                           fc * 1024 + (tt + 1) * 512],
                                        sl.buf[:, fc * 128:(fc + 1) * 128],
                                        bigv(ic, tt * 512, (tt + 1) * 512),
                                        start=(ic == 0), stop=(ic == NIC - 1)))
                            return out
                        last_mm = wring.consume(sl, "tensor", fn_mm,
                                                extra_waits=pw, inc="sPE",
                                                inc_by=4)
                        tick(2)
                    stg = qk_stg[pk]

                    def fn_ev(s, ps=ps, stg=stg, qscl=qscl):
                        if qscl is None:
                            return s.copy(stg, ps[:, 0:2048])
                        return s.activation(stg, ps[:, 0:2048], AF.Copy,
                                            scale=qscl)
                    ev = em.emit("scalar", fn_ev,
                                 waits=[last_mm, qk_stg_free[pk]], inc="sAC")
                    qk_psum_free[pk] = ev

                    def fn_st(sy, dstD=dstD, fp=fp, half=half, stg=stg):
                        return sy.dma_start(
                            dstD.rearrange("(a p) t -> p a t", p=128)[
                                :, 2 * fp:2 * fp + 2,
                                half * 1024:(half + 1) * 1024],
                            stg.rearrange("p (a t) -> p a t", a=2))
                    qk_stg_free[pk] = stQK.store(pk, fn_st, waits=[ev])

            for vc in range(4):
                last_mm = None
                for ic in range(NIC):
                    sl = wring.acquire()
                    wring.produce(sl, "gpsimd",
                                  (lambda g, vc=vc, ic=ic, sl=sl:
                                   g.dma_start(sl.buf,
                                               wv_my[ic * 128:(ic + 1) * 128,
                                                     vc * 512:(vc + 1) * 512])))
                    pw = [lnfm_rdy]
                    if ic == 0:
                        pw += [t for t in v_bank_free if t is not None]
                        pw += qk_psum_free

                    def fn_mm(t, ic=ic, sl=sl):
                        out = []
                        for tb in range(8):
                            out.append(t.matmul(
                                bank(tb), bigv(ic, tb * 128, (tb + 1) * 128),
                                sl.buf, start=(ic == 0),
                                stop=(ic == NIC - 1)))
                        return out
                    last_mm = wring.consume(sl, "tensor", fn_mm,
                                            extra_waits=pw, inc="sPE",
                                            inc_by=8)
                    tick()
                for tb in range(8):
                    pk = tb % 2
                    stg = v_stg[pk]

                    def fn_ev(v, tb=tb, stg=stg):
                        return v.tensor_copy(stg, bank(tb))
                    ev = em.emit("vector", fn_ev,
                                 waits=[last_mm, v_stg_free[pk]], inc="sDV")
                    v_bank_free[tb] = ev

                    def fn_st(sy, tb=tb, vc=vc, half=half, stg=stg):
                        r0 = half * 1024 + tb * 128
                        return sy.dma_start(
                            Va[r0:r0 + 128, vc * 512:(vc + 1) * 512], stg)
                    v_stg_free[pk] = stV.store(pk, fn_st, waits=[ev])

        if a_stop < 2:
            a_end = res_toks
        kb = [view(big1, i * 4096, 2048) for i in range(2)]
        qb_ = [view(big1, 8192 + i * 4096, 2048) for i in range(2)]
        vb = [view(big1, 16384 + i * 4096, 2048) for i in range(2)]
        pb_ = [view(big1, 24576 + i * 4096, 2048) for i in range(2)]
        PT = view(big1, 32768, 8192)
        astg = [view(big1, 49152 + i * 1024, 512) for i in range(2)]
        rbc_sb = view(big1, 51200, 512, F32)
        fA3 = fence()
        kring = Ring(em, "ka", kb, init_cons=fA3)
        qring = Ring(em, "qa", qb_, init_cons=fA3)
        vring = Ring(em, "va", vb, init_cons=fA3)
        all_qkv_l = stQK.all_toks() + stV.all_toks()

        pt_pack_free = [fA3, fA3]
        p_free = [fA3, fA3]
        exp_prev = fA3
        av_prev = fA3
        avst_free = [None, None]
        rbc_free = fA3
        row_free = fA3
        trow_free = fA3
        ptzero = fA3

        st_at = dict(pt_pack_free=[fA3, fA3], p_free=[fA3, fA3],
                     exp_prev=fA3, av_prev=fA3, rbc_free=fA3,
                     row_free=fA3, trow_free=fA3, ptzero=fA3,
                     rT_prev=[fA3, fA3], rc_prev=[fA3, fA3],
                     rows=[], pt_tile=[])
        push_a, flush_a = attention_pipe(pb_, PT, rbc_sb, SCALE_A, True,
                                         st_at)

        def mk_evac_a(h, tile):
            def evac(av, rbccp, rbc_sb):
                stg = astg[tile % 2]

                def fn_avev(v, stg=stg, rbc_sb=rbc_sb):
                    return v.tensor_tensor(stg, psB[:, 512:1024], rbc_sb,
                                           op=ALU.mult)
                avev = em.emit("vector", fn_avev,
                               waits=[av, rbccp, avst_free[tile % 2]],
                               inc="sDV")

                def fn_avst(sy, h=h, tile=tile, stg=stg):
                    return sy.dma_start(
                        attn_ctr[h * 128:(h + 1) * 128,
                                 tile * 512:(tile + 1) * 512], stg)
                avst_free[tile % 2] = stAT.store(tile % 2, fn_avst,
                                                 waits=[avev])
                return avev
            return evac

        for h in range(16 if a_stop >= 2 else 0):
            ksl = kring.acquire()
            kring.produce(ksl, "sync",
                          (lambda sy, h=h, ksl=ksl: sy.dma_start(
                              ksl.buf, Ka[h * 128:(h + 1) * 128, :])),
                          extra_waits=all_qkv_l)
            qsl = qring.acquire()
            qring.produce(qsl, "sync",
                          (lambda sy, h=h, qsl=qsl: sy.dma_start(
                              qsl.buf, Qa[h * 128:(h + 1) * 128, :])),
                          extra_waits=all_qkv_l)
            vsl = vring.acquire()

            def fn_vl(sy, h=h, vsl=vsl):
                for q4 in range(4):
                    sy.dma_start(
                        vsl.buf.rearrange("p (c d) -> p c d", c=16)[
                            :, q4 * 4:(q4 + 1) * 4, :],
                        Va.rearrange("(c p) v -> p c v", p=128)[
                            :, q4 * 4:(q4 + 1) * 4, h * 128:(h + 1) * 128]
                    ).then_inc(SEMH[vsl.sem], 16)
                return None
            vring.produce(vsl, "sync", fn_vl, extra_waits=all_qkv_l, n_dma=4)

            for tile in range(4):
                for qbl in range(4):
                    qb = tile * 4 + qbl
                    t = {"qb": qb, "span": (qb + 1) * 128, "ksl": ksl,
                         "qsl": qsl, "qcol": qb, "tile": tile, "zero": None}
                    if qbl == 3:
                        t["tile_end"] = {"nchT": (tile + 1) * 4, "vsl": vsl,
                                         "evac": mk_evac_a(h, tile)}
                    else:
                        t["tile_end"] = None
                    push_a(t)
                    tick(2)
                    if qbl == 0:
                        push_a({"zero": tile})
        if a_stop >= 2:
            flush_a()

        def fn_cc_a(g):
            return g.collective_compute(
                "AllGather", ALU.bypass,
                replica_groups=[[0, 1], [2, 3], [4, 5], [6, 7]],
                ins=[attn_ctr[0:1024, :].opt()],
                outs=[attn_gth[0:2048, :].opt()])

        def fn_cc_b(g):
            return g.collective_compute(
                "AllGather", ALU.bypass,
                replica_groups=[[0, 1], [2, 3], [4, 5], [6, 7]],
                ins=[attn_ctr[1024:2048, :].opt()],
                outs=[attn_gth[2048:4096, :].opt()])
        cc_tok = None
        cc_tok_a = None
        if a_stop >= 3:
            cc_tok_a = em.emit("gpsimd", fn_cc_a, waits=stAT.all_toks(),
                               inc="sCC")
            cc_tok = em.emit("gpsimd", fn_cc_b, waits=stAT.all_toks(),
                             inc="sCC")

        em.sem("qAFMa")
        em.sem("qAFMb")

        def fn_afm_a(g):
            cid = g.compute_val(nc.partition_id(engines=[mybir.EngineType.Pool]))
            off = g.snap((cid % 2) * 1024, min_val=0, max_val=1024)
            src = attn_gth.rearrange("(c p) t -> p c t", p=128)
            dst = BIG.rearrange("p (c t) -> p c t", c=NIC)
            g.dma_start(dst[:, 0:8, :],
                        src[:, 0:8, bass.ds(off, 1024)]
                        ).then_inc(SEMH["qAFMa"], 16)
            g.dma_start(dst[:, 16:24, :],
                        src[:, 8:16, bass.ds(off, 1024)]
                        ).then_inc(SEMH["qAFMa"], 16)
            return None

        def fn_afm_b(g):
            cid = g.compute_val(nc.partition_id(engines=[mybir.EngineType.Pool]))
            off = g.snap((cid % 2) * 1024, min_val=0, max_val=1024)
            src = attn_gth.rearrange("(c p) t -> p c t", p=128)
            dst = BIG.rearrange("p (c t) -> p c t", c=NIC)
            g.dma_start(dst[:, 8:16, :],
                        src[:, 16:24, bass.ds(off, 1024)]
                        ).then_inc(SEMH["qAFMb"], 16)
            g.dma_start(dst[:, 24:32, :],
                        src[:, 24:32, bass.ds(off, 1024)]
                        ).then_inc(SEMH["qAFMb"], 16)
            return None
        afm_tok = None
        afm_tok_a = None
        if a_stop >= 4:
            afm_tok_a = em.emit("gpsimd", fn_afm_a,
                                waits=[cc_tok_a] + fence(),
                                inc="qAFMa", inc_by=16 * 2, manual=True)
            afm_tok = em.emit("gpsimd", fn_afm_b, waits=[cc_tok],
                              inc="qAFMb", inc_by=16 * 2, manual=True)

        oev = [None] * 8
        oev[0] = fence()
        for oc in range(8 if a_stop >= 5 else 0):
            last_mm = None
            fcs = (list(range(0, 8)) + list(range(16, 24))
                   + list(range(8, 16)) + list(range(24, 32)))
            for fi, fc in enumerate(fcs):
                sl = wring.acquire()
                wring.produce(sl, "gpsimd",
                              (lambda g, oc=oc, fc=fc, sl=sl:
                               g.dma_start(sl.buf,
                                           wo[fc * 128:(fc + 1) * 128,
                                              oc * 512:(oc + 1) * 512])))
                pw = [afm_tok_a if fi < 16 else afm_tok, tok_res_all]
                if fi == 0:
                    pw += [t for t in oev if t is not None]

                def fn_mm(t, fc=fc, fi=fi, sl=sl):
                    return [t.matmul(bank(tb),
                                     bigv(fc, tb * 128, (tb + 1) * 128),
                                     sl.buf, start=(fi == 0),
                                     stop=(fi == NIC - 1))
                            for tb in range(8)]
                last_mm = wring.consume(sl, "tensor", fn_mm, extra_waits=pw,
                                        inc="sPE", inc_by=8)
                tick(2)
            for tb in range(8):
                def fn_ev(v, tb=tb, oc=oc):
                    d = resv(tb)[:, oc * 512:(oc + 1) * 512]
                    return v.tensor_tensor(d, bank(tb), d, op=ALU.add)
                oev[tb] = em.emit("vector", fn_ev, waits=[last_mm], inc="sDV")
        a_end = [t for t in oev if t is not None]
        if a_stop < 5:
            a_end = res_toks + ([cc_tok] if cc_tok else
                                ([avst_free[0], avst_free[1]]
                                 if a_stop >= 2 else []))

    out_stg_free = [None, None]
    stOUT = StoreRing("stOUT")

    def emit_ffn(prev_toks, to_output):
        gwt_sb = view(scr, PH0, 256)
        ics_sb = view(scr, PH0 + 512, 4096, parts=8)
        ocs_sb = view(scr, PH0 + 8704, 4096, parts=8)
        bias_bc = view(scr, PH0 + 16896, 4096)
        routT = view(scr, PH0 + 25088, 1024, parts=8)
        routf = view(scr, PH0 + 27136, 8, F32)
        routb = view(scr, PH0 + 27168, 8)
        os_sb = [view(scr, PH0 + 27200 + i * 1024, 512) for i in range(8)]
        tu_sb = [view(scr, PH0 + 35392 + i * 2048, 512, F32)
                 for i in range(2)]
        outst = [view(scr, PH0 + 39488 + i * 2048, 512, F32)
                 for i in range(2)]

        rsem = rowsem()
        fF = fence()

        def fn_rows(g):
            g.dma_start(gwt_sb.rearrange("p (c e) -> p c e", c=NIC),
                        gwT.rearrange("(c p) e -> p c e", p=128)
                        ).then_inc(SEMH[rsem], 16)
            g.dma_start(ics_sb, ics[:, :]).then_inc(SEMH[rsem], 16)
            g.dma_start(ocs_sb, ocs[:, :]).then_inc(SEMH[rsem], 16)
            g.dma_start(bias_bc[0:1, :], fbias[:, :]).then_inc(SEMH[rsem], 16)
            return None
        rows_tok = em.emit("gpsimd", fn_rows, waits=list(prev_toks),
                           inc=rsem, inc_by=64, manual=True)

        bb_prev = fF
        bb_toks = []
        for j in range(8):
            def fn_bb(t, j=j):
                return t.matmul(psB[:, 1024:1536], onesr,
                                bias_bc[0:1, j * 512:(j + 1) * 512],
                                start=True, stop=True)
            t1 = em.emit("tensor", fn_bb, waits=[rows_tok, tok_const, bb_prev],
                         inc="sPE")

            def fn_bbc(s, j=j):
                return s.copy(bias_bc[:, j * 512:(j + 1) * 512],
                                     psB[:, 1024:1536])
            bb_prev = em.emit("scalar", fn_bbc, waits=[t1], inc="sAC")
            bb_toks.append(bb_prev)
        bias_rdy = bb_toks[-1]

        ln_toks = []
        for tb in range(NTB):
            ln_toks.append(emit_ln_block(resv(tb), tb,
                                         extra_w=list(prev_toks)))
        lnfm_rdy = ln_toks[-1]

        rt_prev = fF
        lg_free = fF
        rt_toks = []
        for tb in range(NTB):
            def fn_lg(t, tb=tb):
                return [t.matmul(
                    psB[:, 1024:1032],
                    bigv(ic, tb * 128, (tb + 1) * 128),
                    gwt_sb.rearrange("p (c e) -> p c e", c=NIC)[:, ic, :],
                    start=(ic == 0), stop=(ic == NIC - 1))
                    for ic in range(NIC)]
            lg = em.emit("tensor", fn_lg,
                         waits=[lnfm_rdy, rows_tok, bias_rdy, lg_free],
                         inc="sPE", inc_by=NIC)

            sm0 = em.emit("vector",
                          lambda v: v.reduce_max(t_mx[:], psB[:, 1024:1032],
                                                 axis=AX.X), waits=[lg])
            sm = em.emit("vector",
                         lambda v: v.tensor_scalar(t_nsx[:], t_mx[:], -1.0,
                                                   None, op0=ALU.mult),
                         waits=[sm0])

            def fn_ex(s):
                return s.activation(routf[:, 0:8], psB[:, 1024:1032], AF.Exp,
                                    bias=t_nsx[:, 0:1],
                                    accum_out=t_sumsP[0][:])
            ex = em.emit("scalar", fn_ex, waits=[sm], inc="sAC")
            lg_free = ex

            nr0 = em.emit("vector",
                          lambda v: v.reciprocal(t_rc[0][:], t_sumsP[0][:]),
                          waits=[ex])
            nr = em.emit("vector",
                         lambda v: v.tensor_scalar(routb[:, 0:8],
                                                   routf[:, 0:8],
                                                   t_rc[0][:, 0:1], None,
                                                   op0=ALU.mult),
                         waits=[nr0])

            def fn_rt(t):
                d = psB[:, 1536:2048].bitcast(BF16)[0:8, 0:128]
                return t.transpose(d, routb[:, 0:8], identB)
            rt = em.emit("tensor", fn_rt, waits=[nr, rt_prev], inc="sPE")

            def fn_rtc(s, tb=tb):
                return s.copy(routT[0:8, tb * 128:(tb + 1) * 128],
                              psB[:, 1536:2048].bitcast(BF16)[0:8, 0:128])
            rt_prev = em.emit("scalar", fn_rtc, waits=[rt], inc="sAC")
            rt_toks.append(rt_prev)
        routT_rdy = rt_toks[-1]

        is_free = [fF, fF]
        xs_last = None
        for ic in range(NIC):
            for tt in range(2):
                pk = (ic * 2 + tt) % 2
                ps = psA[:, pk * 512:(pk + 1) * 512]

                def fn_is(t, ic=ic, tt=tt, ps=ps):
                    return t.matmul(ps, ics_sb[:, ic * 128:(ic + 1) * 128],
                                    routT[0:8, tt * 512:(tt + 1) * 512],
                                    start=True, stop=True)
                ist = em.emit("tensor", fn_is,
                              waits=[routT_rdy, is_free[pk]], inc="sPE")

                def fn_xs(v, ic=ic, tt=tt, ps=ps):
                    d = bigv(ic, tt * 512, (tt + 1) * 512)
                    return v.tensor_tensor(d, ps, d, op=ALU.mult)
                xs_last = em.emit("vector", fn_xs, waits=[ist], inc="sDV")
                is_free[pk] = xs_last
        xs_rdy = xs_last

        badd = []
        for tb in range(NTB):
            def fn_ba(g, tb=tb):
                return g.tensor_tensor(resv(tb), resv(tb), bias_bc,
                                       op=ALU.add)
            badd.append(em.emit("gpsimd", fn_ba,
                                waits=[bias_rdy, ln_toks[tb]], inc="sGP"))

        ev_done = [None] * 8
        ev_done[0] = fF
        os_free = [fF] * 8
        tu_free = [fF, fF]
        out_toks = []
        for oc in range(8):
            os_toks = []
            for tb in range(NTB):
                pk = tb % 2
                ps = psB[:, 1024 + pk * 512:1536 + pk * 512]

                def fn_os(t, tb=tb, oc=oc, ps=ps):
                    return t.matmul(ps, routT[0:8, tb * 128:(tb + 1) * 128],
                                    ocs_sb[:, oc * 512:(oc + 1) * 512],
                                    start=True, stop=True)
                pw = [routT_rdy, ev_done[6 + pk], xs_rdy]
                if len(os_toks) >= 2:
                    pw.append(os_toks[-2][1])
                ot = em.emit("tensor", fn_os, waits=pw, inc="sPE")

                def fn_osc(s, tb=tb, ps=ps):
                    return s.copy(os_sb[tb], ps)
                oc_t = em.emit("scalar", fn_osc,
                               waits=[ot, os_free[tb]], inc="sAC")
                os_toks.append((ot, oc_t))
            os_all = os_toks[-1][1]

            last_mm = None
            for ic in range(NIC):
                sl = wring.acquire()
                wring.produce(sl, "gpsimd",
                              (lambda g, oc=oc, ic=ic, sl=sl:
                               g.dma_start(sl.buf,
                                           fT[ic * 128:(ic + 1) * 128,
                                              oc * 512:(oc + 1) * 512])))
                gs = sgnring.acquire()

                def fn_sgn(s_, sl=sl, gs=gs):
                    return s_.activation(gs.buf, sl.buf, AF.Sign)
                sgnring.produce(gs, "scalar", fn_sgn,
                                extra_waits=[sl.prod_tok] + list(prev_toks))
                sl.cons_toks.append(gs.prod_tok)
                pw = [xs_rdy, os_all]
                if ic == 0:
                    pw += [t for t in ev_done if t is not None]

                def fn_mm(t, ic=ic, gs=gs):
                    return [t.matmul(bank(tb),
                                     bigv(ic, tb * 128, (tb + 1) * 128),
                                     gs.buf, start=(ic == 0),
                                     stop=(ic == NIC - 1))
                            for tb in range(8)]
                last_mm = sgnring.consume(gs, "tensor", fn_mm,
                                          extra_waits=pw, inc="sPE",
                                          inc_by=8)
            for tb in range(8):
                pk = tb % 2
                tu = tu_sb[pk]

                def fn_e1(v, tb=tb, tu=tu):
                    return v.tensor_tensor(tu, bank(tb), os_sb[tb],
                                           op=ALU.mult)
                e1 = em.emit("vector", fn_e1,
                             waits=[last_mm, os_toks[tb][1], tu_free[pk]],
                             inc="sDV")
                os_free[tb] = e1
                if not to_output:
                    def fn_e2(g, tb=tb, oc=oc, tu=tu):
                        d = resv(tb)[:, oc * 512:(oc + 1) * 512]
                        return g.tensor_tensor(d, tu, d, op=ALU.add)
                    e2 = em.emit("gpsimd", fn_e2, waits=[e1, badd[tb]],
                                 inc="sGP")
                    tu_free[pk] = e2
                    ev_done[tb] = e2
                else:
                    ost = outst[pk]

                    def fn_e2(g, tb=tb, oc=oc, ost=ost, tu=tu):
                        return g.tensor_tensor(
                            ost, tu, resv(tb)[:, oc * 512:(oc + 1) * 512],
                            op=ALU.add)
                    e2 = em.emit("gpsimd", fn_e2,
                                 waits=[e1, badd[tb], out_stg_free[pk]],
                                 inc="sGP")
                    tu_free[pk] = e2

                    def fn_yst(sy, tb=tb, oc=oc, ost=ost):
                        return sy.dma_start(
                            y[tb * 128:(tb + 1) * 128,
                              oc * 512:(oc + 1) * 512], ost)
                    yt = stOUT.store(pk, fn_yst, waits=[e2])
                    out_stg_free[pk] = yt
                    out_toks.append(yt)
                    ev_done[tb] = e2
        return out_toks if to_output else [t for t in ev_done
                                           if t is not None]

    a_end = list(a_end) + p0_drain()
    if "B" in phases:
        b_end = emit_ffn(a_end, to_output=False)
    else:
        b_end = a_end

    if "C" in phases:
        stKV = StoreRing("stKV")
        stQB = StoreRing("stQB")
        stAB = StoreRing("stAB")
        sgnC = Ring(em, "gc", [view(scr, 55296 + i * 1024, 512)
                               for i in range(4)], dma=False)
        stg2 = [view(scr, SGN0 + i * 2048, 1024) for i in range(2)]
        svbc = view(scr, PH0 + 40960, 512)
        tuv2 = [view(scr, SGN0 + i * 2048, 512, F32)
                for i in range(2)]

        ln_toks = []
        for tb in range(NTB):
            ln_toks.append(emit_ln_block(resv(tb), tb, extra_w=list(b_end)))
        lnfm_rdy = ln_toks[-1]

        fC = fence()
        psum8_guard = list(fC)

        def str_ps(fc):
            src = psA if fc < 2 else psB
            return src[:, (fc % 2) * 1024:(fc % 2 + 1) * 1024]

        stg_free = [fC, fC]

        def grid_to_bcast(key, ob0):
            tg = []
            for j in range(4):
                def fn_tg(t, key=key, ob0=ob0, j=j):
                    return t.transpose(
                        psB[0:1, 1536 + j * 128:1536 + (j + 1) * 128],
                        sg_fin[key][:, ob0 + j:ob0 + j + 1], identF)
                tg.append(em.emit("tensor", fn_tg,
                                  waits=[tok_const] + psum8_guard
                                  + p0_marks["fin"], inc="sPE"))
            cb = em.emit("vector",
                         lambda v: v.tensor_copy(t_srowb[0:1, :],
                                                 psB[0:1, 1536:2048]),
                         waits=tg)
            bm = em.emit("tensor",
                         lambda t: t.matmul(psB[:, 1024:1536], onesr,
                                            t_srowb[0:1, :],
                                            start=True, stop=True),
                         waits=[cb, tok_const] + psum8_guard, inc="sPE")
            cc = em.emit("vector",
                         lambda v: v.tensor_copy(svbc, psB[:, 1024:1536]),
                         waits=[bm])
            psum8_guard.extend([cb, cc])
            return cc

        def feat_strips(SD, nst, key, Dst, stR):
            for st in range(nst):
                last_mm = None
                for ic in range(NIC):
                    sl = wring.acquire()
                    wring.produce(sl, "gpsimd",
                                  (lambda g, SD=SD, st=st, ic=ic, sl=sl:
                                   g.dma_start(sl.buf,
                                               SD[ic * 128:(ic + 1) * 128,
                                                  st * 512:(st + 1) * 512])))
                    gs = sgnC.acquire()

                    def fn_sgn(s_, sl=sl, gs=gs):
                        return s_.activation(gs.buf, sl.buf, AF.Sign)
                    sgnC.produce(gs, "scalar", fn_sgn,
                                 extra_waits=[sl.prod_tok] + list(b_end))
                    sl.cons_toks.append(gs.prod_tok)
                    pw = [lnfm_rdy]
                    if ic == 0:
                        pw = pw + psum8_guard

                    def fn_mm(t, ic=ic, gs=gs):
                        return [t.matmul(
                            str_ps(fc)[:, tt * 512:(tt + 1) * 512],
                            gs.buf[:, fc * 128:(fc + 1) * 128],
                            bigv(ic, tt * 512, (tt + 1) * 512),
                            start=(ic == 0), stop=(ic == NIC - 1))
                            for fc in range(4) for tt in range(2)]
                    last_mm = sgnC.consume(gs, "tensor", fn_mm,
                                              extra_waits=pw, inc="sPE",
                                              inc_by=8)
                for fc in range(4):
                    pk = fc % 2
                    stg = stg2[pk]

                    def fn_ev(e, fc=fc, st=st, key=key, stg=stg):
                        return e.activation(
                            stg, str_ps(fc), AF.Copy,
                            scale=sg_fin[key][:, st * 4 + fc:st * 4 + fc + 1])
                    ev = em.emit("scalar", fn_ev,
                                 waits=[last_mm, stg_free[pk]]
                                 + p0_marks["fin"], inc="sAC")
                    psum8_guard.append(ev)

                    def fn_st(sy, st=st, fc=fc, stg=stg, Dst=Dst):
                        r0 = st * 512 + fc * 128
                        return sy.dma_start(Dst[r0:r0 + 128, :], stg)
                    stg_free[pk] = stR.store(pk, fn_st, waits=[ev])

        feat_strips(kT, 2, "k", Kc, stKV)

        v_bank_free2 = [None] * 4
        vst_free = stg_free
        for st in range(2):
            svc = grid_to_bcast("v", st * 4)
            for tbp in range(2):
                last_mm = None
                for ic in range(NIC):
                    sl = wring.acquire()
                    wring.produce(sl, "gpsimd",
                                  (lambda g, st=st, ic=ic, sl=sl:
                                   g.dma_start(sl.buf,
                                               vT[ic * 128:(ic + 1) * 128,
                                                  st * 512:(st + 1) * 512])))
                    gs = sgnC.acquire()

                    def fn_sgn(s_, sl=sl, gs=gs):
                        return s_.activation(gs.buf, sl.buf, AF.Sign)
                    sgnC.produce(gs, "scalar", fn_sgn,
                                 extra_waits=[sl.prod_tok] + list(b_end))
                    sl.cons_toks.append(gs.prod_tok)
                    pw = [lnfm_rdy]
                    if ic == 0:
                        pw = pw + psum8_guard \
                            + [t for t in v_bank_free2 if t is not None]

                    def fn_mm(t, ic=ic, tbp=tbp, gs=gs):
                        return [t.matmul(
                            psA[:, j * 512:(j + 1) * 512],
                            bigv(ic, (tbp * 4 + j) * 128,
                                 (tbp * 4 + j + 1) * 128),
                            gs.buf,
                            start=(ic == 0), stop=(ic == NIC - 1))
                            for j in range(4)]
                    last_mm = sgnC.consume(gs, "tensor", fn_mm,
                                              extra_waits=pw, inc="sPE",
                                              inc_by=4)
                for j in range(4):
                    tb = tbp * 4 + j
                    pk = tb % 2
                    stg = stg2[pk]

                    def fn_ev(v, j=j, stg=stg):
                        return v.tensor_tensor(stg[:, 0:512],
                                               psA[:, j * 512:(j + 1) * 512],
                                               svbc, op=ALU.mult)
                    ev = em.emit("vector", fn_ev,
                                 waits=[last_mm, svc, vst_free[pk]])
                    v_bank_free2[j] = ev
                    psum8_guard.append(ev)

                    def fn_st(sy, tb=tb, st=st, stg=stg):
                        return sy.dma_start(Vc[tb * 128:(tb + 1) * 128,
                                               st * 512:(st + 1) * 512],
                                            stg[:, 0:512])
                    vst_free[pk] = stKV.store(pk, fn_st, waits=[ev])

        def fn_cc2(g):
            return g.collective_compute(
                "AllGather", ALU.bypass,
                replica_groups=[[0, 1], [2, 3], [4, 5], [6, 7]],
                ins=[Kc.opt()], outs=[Kg.opt()])
        cck = em.emit("gpsimd", fn_cc2, waits=stKV.all_toks(),
                      inc="sCC")

        def fn_cc3(g):
            return g.collective_compute(
                "AllGather", ALU.bypass,
                replica_groups=[[0, 1], [2, 3], [4, 5], [6, 7]],
                ins=[Vc.opt()], outs=[Vg.opt()])
        cc2 = em.emit("gpsimd", fn_cc3, waits=stKV.all_toks(), inc="sCC")

        feat_strips(qT, 8, "q", Qb, stQB)

        kb = [view(big1, i * 4096, 2048) for i in range(2)]
        vb = [view(big1, 8192 + i * 4096, 2048) for i in range(2)]
        qb2 = [view(big1, 16384 + i * 2048, 1024) for i in range(2)]
        pb_ = [view(big1, 20480 + i * 4096, 2048) for i in range(2)]
        PT = view(big1, 28672, 8192)
        astg = [view(big1, 45056 + i * 1024, 512) for i in range(2)]
        rbc_sb = view(big1, 47104, 512, F32)
        fC4 = fence()
        kringb = Ring(em, "kb", kb, init_cons=fC4)
        vringb = Ring(em, "vb", vb, init_cons=fC4)
        qringb = Ring(em, "qb", qb2, init_cons=fC4)
        all_qb_l = stQB.all_toks()

        pt_pack_free = [fC4, fC4]
        p_free = [fC4, fC4]
        exp_prev = fC4
        av_prev = fC4
        ab_stg_free = [None, None]
        rbc_free = fC4
        row_free = fC4
        trow_free = fC4

        st_cb = dict(pt_pack_free=[fC4, fC4], p_free=[fC4, fC4],
                     exp_prev=fC4, av_prev=fC4, rbc_free=fC4,
                     row_free=fC4, trow_free=fC4, ptzero=fC4,
                     rT_prev=[fC4, fC4], rc_prev=[fC4, fC4],
                     rows=[], pt_tile=[])
        push_b, flush_b = attention_pipe(pb_, PT, rbc_sb, SCALE_B, False,
                                         st_cb)

        ab_stg_free = [None, None]

        def mk_evac_b(hq, tile):
            def evac(av, rbccp, rbc_sb):
                stg = astg[tile % 2]

                def fn_avev(v, stg=stg, rbc_sb=rbc_sb):
                    return v.tensor_tensor(stg, psB[:, 512:1024], rbc_sb,
                                           op=ALU.mult)
                avev = em.emit("vector", fn_avev,
                               waits=[av, rbccp, ab_stg_free[tile % 2]],
                               inc="sDV")

                def fn_ast(sy, hq=hq, tile=tile, stg=stg):
                    return sy.dma_start(
                        Ab[hq * 128:(hq + 1) * 128,
                           tile * 512:(tile + 1) * 512], stg)
                ab_stg_free[tile % 2] = stAB.store(tile % 2, fn_ast,
                                                   waits=[avev])
                return avev
            return evac

        for kvh in range(8):
            ksl = kringb.acquire()

            def fn_kl(sy, kvh=kvh, ksl=ksl):
                sy.dma_start(ksl.buf[:, 0:1024],
                             Kg[kvh * 128:(kvh + 1) * 128, :]
                             ).then_inc(SEMH[ksl.sem], 16)
                sy.dma_start(ksl.buf[:, 1024:2048],
                             Kg[1024 + kvh * 128:1024 + (kvh + 1) * 128, :]
                             ).then_inc(SEMH[ksl.sem], 16)
                return None
            kringb.produce(ksl, "sync", fn_kl, extra_waits=[cck], n_dma=2)
            vsl = vringb.acquire()

            def fn_vl(sy, kvh=kvh, vsl=vsl):
                for q4 in range(4):
                    sy.dma_start(
                        vsl.buf.rearrange("p (c d) -> p c d", c=16)[
                            :, q4 * 4:(q4 + 1) * 4, :],
                        Vg.rearrange("(c p) v -> p c v", p=128)[
                            :, q4 * 4:(q4 + 1) * 4,
                            kvh * 128:(kvh + 1) * 128]
                    ).then_inc(SEMH[vsl.sem], 16)
                return None
            vringb.produce(vsl, "sync", fn_vl, extra_waits=[cc2], n_dma=4)

            for qh in range(4):
                hq = kvh * 4 + qh
                qsl = qringb.acquire()
                qringb.produce(qsl, "sync",
                               (lambda sy, hq=hq, qsl=qsl: sy.dma_start(
                                   qsl.buf, Qb[hq * 128:(hq + 1) * 128, :])),
                               extra_waits=all_qb_l)
                for tile in range(2):
                    for qbl in range(4):
                        qb = tile * 4 + qbl
                        t = {"qb": qb, "span": 2048, "ksl": ksl,
                             "qsl": qsl, "qcol": qb, "tile": tile,
                             "zero": None, "tile_end": None}
                        if qbl == 3:
                            t["tile_end"] = {"nchT": 16, "vsl": vsl,
                                             "evac": mk_evac_b(hq, tile)}
                        push_b(t)
        flush_b()

        em.sem("qABM")

        def fn_abm(sy):
            for fc in range(NIC):
                sy.dma_start(BIG[:, fc * TOK:(fc + 1) * TOK],
                             Ab[fc * 128:(fc + 1) * 128, :]
                             ).then_inc(SEMH["qABM"], 16)
            return None
        abm_tok = em.emit("sync", fn_abm, waits=stAB.all_toks(),
                          inc="qABM", inc_by=16 * NIC, manual=True)

        fC5 = fence()
        psum8_guard.extend(list(fC5))
        oev = [None] * 8
        oev[0] = fC5
        o_bank_free = [None] * 4
        tuv2_init = stQB.all_toks() + stKV.all_toks() + list(fC5)
        tu_free2 = [tuv2_init, tuv2_init]
        for oc in range(8):
            svc = grid_to_bcast("o", oc * 4)
            for tbp in range(2):
                last_mm = None
                for ic in range(NIC):
                    sl = wring.acquire()
                    wring.produce(sl, "gpsimd",
                                  (lambda g, oc=oc, ic=ic, sl=sl:
                                   g.dma_start(sl.buf,
                                               oT[ic * 128:(ic + 1) * 128,
                                                  oc * 512:(oc + 1) * 512])))
                    gs = sgnC.acquire()

                    def fn_sgn(s_, sl=sl, gs=gs):
                        return s_.activation(gs.buf, sl.buf, AF.Sign)
                    sgnC.produce(gs, "scalar", fn_sgn,
                                 extra_waits=[sl.prod_tok] + list(b_end))
                    sl.cons_toks.append(gs.prod_tok)
                    pw = [abm_tok]
                    if ic == 0:
                        pw = pw + psum8_guard \
                            + [t for t in o_bank_free if t is not None]

                    def fn_mm(t, ic=ic, tbp=tbp, gs=gs):
                        return [t.matmul(
                            psA[:, j * 512:(j + 1) * 512],
                            bigv(ic, (tbp * 4 + j) * 128,
                                 (tbp * 4 + j + 1) * 128),
                            gs.buf,
                            start=(ic == 0), stop=(ic == NIC - 1))
                            for j in range(4)]
                    last_mm = sgnC.consume(gs, "tensor", fn_mm,
                                              extra_waits=pw, inc="sPE",
                                              inc_by=4)
                for j in range(4):
                    tb = tbp * 4 + j
                    pk = tb % 2
                    tu = tuv2[pk]

                    def fn_e1(v, j=j, tu=tu):
                        return v.tensor_tensor(tu,
                                               psA[:, j * 512:(j + 1) * 512],
                                               svbc, op=ALU.mult)
                    e1 = em.emit("vector", fn_e1,
                                 waits=[last_mm, svc, tu_free2[pk]])
                    o_bank_free[j] = e1
                    psum8_guard.append(e1)

                    def fn_e2(g, tb=tb, oc=oc, tu=tu):
                        d = resv(tb)[:, oc * 512:(oc + 1) * 512]
                        return g.tensor_tensor(d, tu, d, op=ALU.add)
                    e2 = em.emit("gpsimd", fn_e2, waits=[e1])
                    tu_free2[pk] = [e2]
                    oev[tb] = e2
        c_end = [t for t in oev if t is not None]
    else:
        c_end = b_end

    if "D" in phases:
        emit_ffn(c_end, to_output=True)
        final_l = stOUT.all_toks()
    else:
        em.sem("qDUMP")
        def fn_dump(g):
            for tb in range(NTB):
                g.dma_start(y[tb * 128:(tb + 1) * 128, :],
                            resv(tb)).then_inc(SEMH["qDUMP"], 16)
            return None
        em.emit("gpsimd", fn_dump, waits=list(c_end) + [tok_res_all],
                inc="qDUMP", inc_by=16 * NTB, manual=True)
        final_l = [("qDUMP", em.counts["qDUMP"])]

    em.emit("sync", lambda sy: None, waits=final_l)

    with nc.Block() as blk:
        def runner(eng_name):
            def go(eng):
                for idx, (fn, waits, inc, lineno) in enumerate(em.ops[eng_name]):
                    try:
                        for s, v in waits:
                            eng.wait_ge(SEMH[s], v)
                        last = fn(eng)
                        for ins_ in (last if isinstance(last, list)
                                     else ([last] if last is not None else [])):
                            try:
                                ins_.annotate(f"{eng_name}:{idx}:L{lineno}")
                            except Exception:
                                pass
                        if inc is not None:
                            name, by = inc
                            if isinstance(last, list):
                                assert len(last) == by, \
                                    f"inc_by {by} != {len(last)} insts"
                                for ins_ in last:
                                    ins_.then_inc(SEMH[name], 1)
                            else:
                                assert last is not None, \
                                    f"no inst for inc {name}"
                                last.then_inc(SEMH[name], by)
                    except Exception as ex:
                        import traceback
                        co = fn.__code__
                        raise RuntimeError(
                            f"step {idx} on {eng_name} "
                            f"({co.co_filename}:{co.co_firstlineno}): {ex}"
                        ) from ex
            return go
        blk.tensor(runner("tensor"))
        blk.vector(runner("vector"))
        blk.scalar(runner("scalar"))
        blk.gpsimd(runner("gpsimd"))
        blk.sync(runner("sync"))
    nc._em = em
    return nc


def kernel(**inputs):
    x = np.asarray(inputs["L0_attn_a_hidden_states"], dtype=np.float32)
    wq = np.asarray(inputs["L0_attn_a_wq"], dtype=np.float32)
    wkv = np.asarray(inputs["L0_attn_a_wkv"], dtype=np.float32)
    wo_ = np.asarray(inputs["L0_attn_a_wo"], dtype=np.float32)
    C = np.ascontiguousarray
    qT = C(np.asarray(inputs["L0_attn_b_q_weight"], dtype=np.float32).T)
    kT = C(np.asarray(inputs["L0_attn_b_k_weight"], dtype=np.float32).T)
    vT = C(np.asarray(inputs["L0_attn_b_v_weight"], dtype=np.float32).T)
    oT = C(np.asarray(inputs["L0_attn_b_o_weight"], dtype=np.float32).T)
    fT = C(np.asarray(inputs["L0_ffn_weight"], dtype=np.float32).T)
    gwT = C(np.asarray(inputs["L0_ffn_gate_w"], dtype=np.float32).T)
    icsv = np.asarray(inputs["L0_ffn_in_channel_scale"], dtype=np.float32)
    ocsv = np.asarray(inputs["L0_ffn_out_channel_scale"], dtype=np.float32)
    fb = np.asarray(inputs["L0_ffn_bias"], dtype=np.float32).reshape(1, -1)
    qN = C(np.asarray(inputs["L0_attn_b_q_weight"], dtype=np.float32))
    kN = C(np.asarray(inputs["L0_attn_b_k_weight"], dtype=np.float32))
    vN = C(np.asarray(inputs["L0_attn_b_v_weight"], dtype=np.float32))
    oN = C(np.asarray(inputs["L0_attn_b_o_weight"], dtype=np.float32))
    qscg = C(np.asarray(inputs["L0_attn_b_q_scale"],
                        dtype=np.float32).reshape(32, 128).T)
    kscg = C(np.asarray(inputs["L0_attn_b_k_scale"],
                        dtype=np.float32).reshape(8, 128).T)
    vscg = C(np.asarray(inputs["L0_attn_b_v_scale"],
                        dtype=np.float32).reshape(8, 128).T)
    oscg = C(np.asarray(inputs["L0_attn_b_o_scale"],
                        dtype=np.float32).reshape(32, 128).T)

    in_maps = []
    for core in range(8):
        b, g = core // 2, core % 2
        qcols = np.concatenate([np.arange(h * 192, h * 192 + 128)
                                for h in range(g * 16, (g + 1) * 16)])
        in_maps.append({
            "xf": C(x[b]),
            "xo": C(x[b, g * TOK:(g + 1) * TOK]),
            "wq_my": C(wq[:, qcols]),
            "wk_my": C(wkv[:, g * 2048:(g + 1) * 2048]),
            "wv_my": C(wkv[:, 4096 + g * 2048:4096 + (g + 1) * 2048]),
            "wo": wo_, "qT": qT, "kT": kT, "vT": vT, "oT": oT, "fT": fT,
            "gwT": gwT, "ics": icsv, "ocs": ocsv, "fbias": fb,
            "qN": qN, "kN": kN, "vN": vN, "oN": oN,
            "qscg": qscg, "kscg": kscg, "vscg": vscg, "oscg": oscg,
        })

    nc = build()
    r = run_bass_kernel_spmd(nc, in_maps, core_ids=list(range(8)))
    out = np.empty((B, S, D), dtype=np.float32)
    for core in range(8):
        b, g = core // 2, core % 2
        out[b, g * TOK:(g + 1) * TOK] = np.asarray(r.results[core]["y"])
    return out


# version 11
# speedup vs baseline: 1.1211x; 1.1211x over previous
import numpy as np
import ml_dtypes
import concourse.bass as bass
import concourse.mybir as mybir
from concourse.bass_utils import run_bass_kernel_spmd

F32 = mybir.dt.float32
BF16 = mybir.dt.bfloat16
U8 = mybir.dt.uint8
AF = mybir.ActivationFunctionType
ALU = mybir.AluOpType
AX = mybir.AxisListType
bfloat16 = ml_dtypes.bfloat16

B, S, D = 4, 2048, 4096
TOK = 1024
NIC = 32
NTB = 8
SCALE_A = float(192.0 ** -0.5)
SCALE_B = float(128.0 ** -0.5)
NEG_BIG = -1e30
EPS = 1e-5
ENGINES = ("tensor", "vector", "scalar", "gpsimd", "sync")


class Em:

    def __init__(self, nc):
        self.nc = nc
        self.ops = {e: [] for e in ENGINES}
        self.meta = {}
        self.counts = {}
        self.hwm = {e: {} for e in ENGINES}
        self.semh = {}

    def sem(self, name):
        if name not in self.counts:
            self.counts[name] = 0
            self.semh[name] = self.nc.alloc_semaphore(name)
        return self.semh[name]

    OWN = {"tensor": "sPE", "vector": "sDV", "scalar": "sAC",
           "gpsimd": "sGP"}

    def emit(self, eng, fn, waits=(), inc=None, inc_by=1, manual=False):
        own = self.OWN.get(eng)
        if own is not None:
            self.sem(own)
            waits = list(waits) + [(own, self.counts[own])]
            if inc is None:
                inc = own
        w = []
        flat = []
        for t in waits:
            if isinstance(t, list):
                flat.extend(t)
            else:
                flat.append(t)
        for t in flat:
            if t is None:
                continue
            s, v = t
            if v <= 0 or v <= self.hwm[eng].get(s, 0):
                continue
            self.hwm[eng][s] = v
            w.append((s, v))
        tok = None
        if inc is not None:
            self.sem(inc)
            self.counts[inc] += inc_by
            tok = (inc, self.counts[inc])
        self.ops[eng].append((fn, w, None if (inc is None or manual)
                              else (inc, inc_by),
                              getattr(fn, "__code__", None)
                              and fn.__code__.co_firstlineno))
        self.meta.setdefault(eng, []).append(
            (tok, getattr(fn, "__code__", None) and fn.__code__.co_firstlineno))
        return tok


class Slot:
    __slots__ = ("buf", "sem", "uses", "prod_tok", "cons_toks")

    def __init__(self, buf, sem):
        self.buf = buf
        self.sem = sem
        self.uses = 0
        self.prod_tok = None
        self.cons_toks = []


class Ring:
    def __init__(self, em, name, bufs, dma=True, init_cons=None):
        self.em = em
        for i in range(len(bufs)):
            em.sem(f"{name}{i}")
        self.slots = [Slot(b, f"{name}{i}") for i, b in enumerate(bufs)]
        if init_cons:
            for sl in self.slots:
                sl.cons_toks = list(init_cons)
        self.k = 0
        self.dma = dma

    def acquire(self):
        s = self.slots[self.k % len(self.slots)]
        self.k += 1
        return s

    def produce(self, s, eng, fn, extra_waits=(), n_dma=1):
        waits = list(s.cons_toks) + list(extra_waits)
        s.cons_toks = []
        s.uses += 1
        if self.dma and n_dma > 1:
            tok = self.em.emit(eng, fn, waits=waits, inc=s.sem,
                               inc_by=16 * n_dma, manual=True)
        else:
            tok = self.em.emit(eng, fn, waits=waits, inc=s.sem,
                               inc_by=16 if self.dma else 1)
        s.prod_tok = tok
        return tok

    def consume(self, s, eng, fn, extra_waits=(), inc=None, inc_by=1):
        tok = self.em.emit(eng, fn, waits=[s.prod_tok] + list(extra_waits),
                           inc=inc, inc_by=inc_by)
        if tok is not None:
            s.cons_toks.append(tok)
        return tok


def _u16(a):
    return np.ascontiguousarray(np.asarray(a).astype(bfloat16)).view(np.uint16)


def build(phases="ABCD", a_stop=9):
    nc = bass.Bass("TRN2", target_bir_lowering=False)
    em = Em(nc)
    SEMH = em.semh

    def inp(name, shape):
        return nc.dram_tensor(name, list(shape), F32, kind="ExternalInput").ap()

    xf = inp("xf", (S, D))
    xo = inp("xo", (TOK, D))
    wq_my = inp("wq_my", (D, 2048))
    wk_my = inp("wk_my", (D, 2048))
    wv_my = inp("wv_my", (D, 2048))
    wo = inp("wo", (D, D))
    qT = inp("qT", (D, D))
    kT = inp("kT", (D, 1024))
    vT = inp("vT", (D, 1024))
    oT = inp("oT", (D, D))
    fT = inp("fT", (D, D))
    gwT = inp("gwT", (D, 8))
    ics = inp("ics", (8, D))
    ocs = inp("ocs", (8, D))
    fbias = inp("fbias", (1, D))
    qN = inp("qN", (D, D))
    kN = inp("kN", (1024, D))
    vN = inp("vN", (1024, D))
    oN = inp("oN", (D, D))
    qscg = inp("qscg", (128, 32))
    kscg = inp("kscg", (128, 8))
    vscg = inp("vscg", (128, 8))
    oscg = inp("oscg", (128, 32))
    y = nc.dram_tensor("y", [TOK, D], F32, kind="ExternalOutput").ap()

    Qa = nc.dram_tensor("Qa", [2048, S], BF16).ap()
    Ka = nc.dram_tensor("Ka", [2048, S], BF16).ap()
    Va = nc.dram_tensor("Va", [S, 2048], BF16).ap()
    attn_ctr = nc.dram_tensor("attn_ctr", [2048, S], BF16).ap()
    attn_gth = nc.dram_tensor("attn_gth", [4096, S], BF16).ap()
    Qb = nc.dram_tensor("Qb", [D, TOK], BF16).ap()
    Kc = nc.dram_tensor("Kc", [1024, TOK], BF16).ap()
    Vc = nc.dram_tensor("Vc", [TOK, 1024], BF16).ap()
    Kg = nc.dram_tensor("Kg", [2048, TOK], BF16).ap()
    Vg = nc.dram_tensor("Vg", [2048, 1024], BF16).ap()
    Ab = nc.dram_tensor("Ab", [D, TOK], BF16).ap()

    identB_d = nc.inline_tensor(_u16(np.eye(128)), name="identB_c")
    identF_d = nc.inline_tensor(np.eye(128, dtype=np.float32), name="identF_c")
    maskT_d = nc.inline_tensor(
        _u16(np.tril(np.full((128, 128), NEG_BIG, np.float32), -1)),
        name="maskT_c")
    onesr_d = nc.inline_tensor(_u16(np.ones((1, 128))), name="onesr_c")
    onesc_d = nc.inline_tensor(_u16(np.ones((128, 1))), name="onesc_c")

    res_t = nc.alloc_sbuf_tensor("res", [128, NTB * D], BF16)
    big1_t = nc.alloc_sbuf_tensor("big1", [128, 65536], U8)
    scr_t = nc.alloc_sbuf_tensor("scr", [128, 73728], U8)
    identB = nc.alloc_sbuf_tensor("identB", [128, 128], BF16).ap()
    identF = nc.alloc_sbuf_tensor("identF", [128, 128], F32).ap()
    maskT = nc.alloc_sbuf_tensor("maskT", [128, 128], BF16).ap()
    onesr = nc.alloc_sbuf_tensor("onesr", [1, 128], BF16).ap()
    onesc = nc.alloc_sbuf_tensor("onesc", [128, 1], BF16).ap()
    res = res_t.ap()
    big1 = big1_t.ap()
    scr = scr_t.ap()

    def resv(tb):
        return res[:, tb * D:(tb + 1) * D]

    def view(base, off, cols, dt=BF16, parts=128):
        nb = cols * (4 if dt == F32 else 2)
        return base[0:parts, off:off + nb].bitcast(dt)

    BIG = view(big1, 0, 32 * TOK)

    def bigv(ic, c0, c1):
        return BIG[:, ic * TOK + c0:ic * TOK + c1]

    SLOT0, SGN0, STAT0, PH0 = 0, 8192, 12288, 28672
    wring = Ring(em, "w", [view(scr, SLOT0 + i * 1024, 512) for i in range(8)])
    sgnring = Ring(em, "g", [view(scr, SGN0 + i * 1024, 512) for i in range(4)],
                   dma=False)

    so = [STAT0]

    def stat(cols, dt=F32, parts=128):
        nb = cols * (4 if dt == F32 else 2)
        off = so[0]
        so[0] += (nb + 31) // 32 * 32
        assert so[0] <= PH0, "stat overflow"
        return scr[0:parts, off:off + nb].bitcast(dt)

    t_sums = stat(1)
    t_sumsq = stat(1)
    t_m = stat(1)
    t_v1 = stat(1)
    t_v2 = stat(1)
    t_std = stat(1)
    t_r = stat(1)
    t_negmr = stat(1)
    t_mx = stat(1)
    t_mxc = stat(4)
    t_nsx = stat(1)
    t_sumsP = [stat(1), stat(1)]
    t_rc = [stat(1), stat(1)]
    t_row = stat(2048, BF16, parts=1)
    t_srowb = stat(512, BF16, parts=1)
    sg_raw = {"q": stat(32), "k": stat(8), "v": stat(8), "o": stat(32)}
    sg_fin = {"q": stat(32), "k": stat(8), "v": stat(8), "o": stat(32)}
    scg_sb = {"q": stat(32), "k": stat(8), "v": stat(8), "o": stat(32)}
    t_red = stat(1)
    lnt = stat(4096, BF16)

    psA = nc.alloc_psum_tensor("psA", [128, 2048], F32).ap()
    psB = nc.alloc_psum_tensor("psB", [128, 2048], F32).ap()

    def bank(i):
        src = psA if i < 4 else psB
        j = i % 4
        return src[:, j * 512:(j + 1) * 512]

    def fn_consts(g):
        g.dma_start(identB[:], identB_d.ap().bitcast(BF16)).then_inc(SEMH["qC"], 16)
        g.dma_start(identF[:], identF_d.ap()).then_inc(SEMH["qC"], 16)
        g.dma_start(maskT[:], maskT_d.ap().bitcast(BF16)).then_inc(SEMH["qC"], 16)
        g.dma_start(onesr[:], onesr_d.ap().bitcast(BF16)).then_inc(SEMH["qC"], 16)
        g.dma_start(onesc[:], onesc_d.ap().bitcast(BF16)).then_inc(SEMH["qC"], 16)
        return None
    em.sem("qC")
    tok_const = em.emit("gpsimd", fn_consts, inc="qC", inc_by=16 * 5,
                        manual=True)

    em.sem("qRES")
    res_toks = []
    for tb in range(NTB):
        def fn_res(g, tb=tb):
            return g.dma_start(resv(tb), xo[tb * 128:(tb + 1) * 128, :])
        res_toks.append(em.emit("gpsimd", fn_res, inc="qRES", inc_by=16))
    tok_res_all = res_toks[-1]


    class StoreRing:

        def __init__(self, name, n=2):
            self.names = [f"{name}{i}" for i in range(n)]
            for nm in self.names:
                em.sem(nm)

        def store(self, k, fn, waits=()):
            nm = self.names[k % len(self.names)]
            return em.emit("sync", fn, waits=waits, inc=nm, inc_by=16)

        def all_toks(self):
            return [(nm, em.counts[nm]) for nm in self.names]

    def fence():
        return [(nm, em.counts.get(nm, 0))
                for nm in ("sPE", "sDV", "sAC", "sGP")]

    rowsem_n = [0]

    def rowsem():
        rowsem_n[0] += 1
        name = f"qR{rowsem_n[0]}"
        em.sem(name)
        return name

    p0_nat = Ring(em, "p0n", [view(scr, 67584 + i * 1024, 512)
                              for i in range(4)])
    p0_marks = {}

    def p0_gen():
        def s_units(WN, key, nblk):
            grid = sg_raw[key]

            def consume(u):
                sl, ob, piece = u

                def fn_red(e, sl=sl):
                    return e.tensor_reduce(t_red[:], sl.buf, axis=AX.X,
                                           op=ALU.add,
                                           apply_absolute_value=True)
                rt = p0_nat.consume(sl, "vector", fn_red)

                def fn_acc(e, ob=ob, piece=piece, grid=grid):
                    if piece == 0:
                        return e.tensor_copy(grid[:, ob:ob + 1], t_red[:])
                    return e.tensor_tensor(grid[:, ob:ob + 1],
                                           grid[:, ob:ob + 1],
                                           t_red[:], op=ALU.add)
                em.emit("vector", fn_acc, waits=[rt])

            pend = []
            for ob in range(nblk):
                for piece in range(8):
                    sl = p0_nat.acquire()
                    p0_nat.produce(sl, "gpsimd",
                                   (lambda e, WN=WN, ob=ob, piece=piece,
                                    sl=sl: e.dma_start(
                                        sl.buf,
                                        WN[ob * 128:(ob + 1) * 128,
                                           piece * 512:(piece + 1) * 512])))
                    pend.append((sl, ob, piece))
                    if len(pend) >= 3:
                        consume(pend.pop(0))
                    yield 1
            while pend:
                consume(pend.pop(0))

        for key, WN, nblk in (("q", qN, 32), ("k", kN, 8), ("v", vN, 8),
                              ("o", oN, 32)):
            yield from s_units(WN, key, nblk)
        p0_marks["s"] = [("sDV", em.counts["sDV"])]

    _p0 = p0_gen()
    p0_active = [True]

    def tick(n=1):
        for _ in range(n):
            if not p0_active[0]:
                return
            if next(_p0, None) is None:
                p0_active[0] = False
                return

    def p0_drain():
        while p0_active[0]:
            tick()
        rsemg = rowsem()

        def fn_g(g):
            g.dma_start(scg_sb["q"], qscg[:, :]).then_inc(SEMH[rsemg], 16)
            g.dma_start(scg_sb["k"], kscg[:, :]).then_inc(SEMH[rsemg], 16)
            g.dma_start(scg_sb["v"], vscg[:, :]).then_inc(SEMH[rsemg], 16)
            g.dma_start(scg_sb["o"], oscg[:, :]).then_inc(SEMH[rsemg], 16)
            return None
        gt = em.emit("sync", fn_g, inc=rsemg, inc_by=64, manual=True)
        fin = {}
        for key in ("q", "k", "v", "o"):
            t1 = em.emit("vector",
                         (lambda v, key=key: v.tensor_scalar(
                             sg_fin[key][:], sg_raw[key][:], 1.0 / D, 1e-8,
                             op0=ALU.mult, op1=ALU.max)),
                         waits=p0_marks["s"] + [gt])
            fin[key] = em.emit("vector",
                               (lambda v, key=key: v.tensor_tensor(
                                   sg_fin[key][:], sg_fin[key][:],
                                   scg_sb[key][:], op=ALU.mult)),
                               waits=[t1])
            if key == "q":
                fin[key] = em.emit("vector",
                                   (lambda v: v.tensor_scalar(
                                       sg_fin["q"][:], sg_fin["q"][:],
                                       SCALE_B, None, op0=ALU.mult)),
                                   waits=[fin[key]])
        p0_marks["fin"] = [fin[k] for k in fin]
        return p0_marks["s"] + p0_marks["fin"]

    pack_cons = [None, None]
    ln_tp_last = [None]

    def emit_ln_block(src_ap, tbcol, extra_w=(), psum_guard=()):
        tw = [t for t in extra_w if t is not None]
        tok1 = em.emit("vector",
                       lambda v: v.reduce_sum(t_sums[:], src_ap, axis=AX.X),
                       waits=tw)
        tok2 = em.emit("scalar",
                       lambda s: s.activation(lnt[:], src_ap, AF.Square,
                                              accum_out=t_sumsq[:]),
                       waits=tw + [tok1, ln_tp_last[0]])
        s1 = em.emit("vector", lambda v: v.tensor_scalar(
            t_m[:], t_sums[:], 1.0 / D, None, op0=ALU.mult), waits=[tok1])
        s2 = em.emit("vector", lambda v: v.tensor_scalar(
            t_v1[:], t_sumsq[:], 1.0 / D, EPS, op0=ALU.mult, op1=ALU.add),
            waits=[tok2])
        s3 = em.emit("vector", lambda v: v.tensor_tensor(
            t_v2[:], t_m[:], t_m[:], op=ALU.mult), waits=[s1])
        s4 = em.emit("vector", lambda v: v.scalar_tensor_tensor(
            t_v1[:], t_v2[:], -1.0, t_v1[:], op0=ALU.mult, op1=ALU.add),
            waits=[s2, s3])
        s5 = em.emit("scalar", lambda s: s.activation(
            t_std[:], t_v1[:], AF.Sqrt), waits=[s4])
        s6 = em.emit("vector", lambda v: v.reciprocal(t_r[:], t_std[:]),
                     waits=[s5])
        s7 = em.emit("vector", lambda v: v.scalar_tensor_tensor(
            t_negmr[:], t_m[:], -1.0, t_r[:], op0=ALU.mult, op1=ALU.mult),
            waits=[s6])
        tok5 = em.emit("vector", lambda v: v.tensor_scalar(
            lnt[:], src_ap, t_r[:, 0:1], t_negmr[:, 0:1],
            op0=ALU.mult, op1=ALU.add), waits=[s7, tok2])

        last_copy = None
        last_tp = None
        for grp in range(8):
            pk = grp % 2
            dstp = psB[:, pk * 512:(pk + 1) * 512].bitcast(BF16)

            def fn_tp(t, grp=grp, dstp=dstp):
                out = []
                for j in range(4):
                    ic = grp * 4 + j
                    out.append(t.transpose(dstp[:, j * 128:(j + 1) * 128],
                                           lnt[:, ic * 128:(ic + 1) * 128],
                                           identB))
                return out
            pw = [tok5, tok_const, pack_cons[pk]]
            if grp < 2:
                pw += [t for t in psum_guard if t is not None]
            tok_tp = em.emit("tensor", fn_tp, waits=pw, inc_by=4)
            last_tp = tok_tp

            def fn_cp(s, grp=grp, dstp=dstp, tbcol=tbcol):
                d = BIG.rearrange("p (c t) -> p c t", c=NIC)[
                    :, grp * 4:(grp + 1) * 4, tbcol * 128:(tbcol + 1) * 128]
                return s.copy(d, dstp[:, 0:512].rearrange(
                    "p (c t) -> p c t", c=4))
            last_copy = em.emit("scalar", fn_cp, waits=[tok_tp])
            pack_cons[pk] = last_copy
        ln_tp_last[0] = last_tp
        return last_copy

    a_end = res_toks

    scp = view(big1, 53248, 2048, F32)

    def attention_pipe(pb_, PT, rbc_sb, scale, causal, bufs_state):
        st = bufs_state
        st["pend"] = None

        def emit_tail(p):
            qb, span, ex, rc = p["qb"], p["span"], p["ex"], p["rc"]
            qbl = qb % 4

            def fn_rT(t, qbl=qbl, pk2=qb % 2):
                return t.transpose(
                    psB[0:1, 1024 + qbl * 128:1024 + (qbl + 1) * 128],
                    t_rc[pk2][:, 0:1], identF)
            rT = em.emit("tensor", fn_rT, waits=[rc, st["row_free"]],
                         inc="sPE")
            st["rT_prev"][qb % 2] = rT
            st["rows"].append(rT)
            nch = (span + 127) // 128
            pb = pb_[qb % 2]
            last_pt = None
            for p0 in range(0, nch, 4):
                n = min(4, nch - p0)
                pk = (p0 // 4) % 2
                bankc = 0 if pk == 0 else 1536

                def fn_pt(t, p0=p0, n=n, pb=pb, bankc=bankc):
                    return [t.transpose(
                        psB[:, bankc:bankc + 512].bitcast(BF16)[
                            :, j * 128:(j + 1) * 128],
                        pb[:, (p0 + j) * 128:(p0 + j + 1) * 128],
                        identB) for j in range(n)]
                tpk = em.emit("tensor", fn_pt,
                              waits=[ex, st["pt_pack_free"][pk]],
                              inc="sPE", inc_by=n)
                last_pt = tpk

                if pk == 0:
                    def fn_ptc(sc_, p0=p0, n=n, qbl=qbl, bankc=bankc, PT=PT):
                        d = PT.rearrange("p (c t) -> p c t", c=16)[
                            :, p0:p0 + n, qbl * 128:(qbl + 1) * 128]
                        srcv = psB[:, bankc:bankc + 512].bitcast(BF16)[
                            :, 0:n * 128].rearrange("p (c t) -> p c t", c=n)
                        return sc_.copy(d, srcv)
                    ptc = em.emit("scalar", fn_ptc, waits=[tpk], inc="sAC")
                else:
                    def fn_ptc(v_, p0=p0, n=n, qbl=qbl, bankc=bankc, PT=PT):
                        d = PT.rearrange("p (c t) -> p c t", c=16)[
                            :, p0:p0 + n, qbl * 128:(qbl + 1) * 128]
                        srcv = psB[:, bankc:bankc + 512].bitcast(BF16)[
                            :, 0:n * 128].rearrange("p (c t) -> p c t", c=n)
                        return v_.tensor_copy(d, srcv)
                    ptc = em.emit("vector", fn_ptc, waits=[tpk], inc="sDV")
                st["pt_pack_free"][pk] = ptc
                st["pt_tile"].append(ptc)
            st["p_free"][qb % 2] = last_pt

        def emit_tile(te, tile):
            def fn_rowcp(v, tile=tile):
                return v.tensor_copy(
                    t_row[0:1, (tile % 4) * 512:(tile % 4 + 1) * 512],
                    psB[0:1, 1024:1536])
            rowcp = em.emit("vector", fn_rowcp,
                            waits=st["rows"] + [st["trow_free"]], inc="sDV")

            def fn_rbc(t, tile=tile):
                return t.matmul(
                    psB[:, 1024:1536], onesr,
                    t_row[0:1, (tile % 4) * 512:(tile % 4 + 1) * 512],
                    start=True, stop=True)
            rbct = em.emit("tensor", fn_rbc,
                           waits=[rowcp, st["rbc_free"]], inc="sPE")
            st["trow_free"] = rbct
            rbccp = em.emit("vector",
                            (lambda v, rbc_sb=rbc_sb: v.tensor_copy(
                                rbc_sb, psB[:, 1024:1536])),
                            waits=[rbct, st["rbc_free"]], inc="sDV")
            st["row_free"] = rbccp
            nchT = te["nchT"]
            vsl = te["vsl"]

            def fn_av(t, nchT=nchT, vsl=vsl, PT=PT):
                return [t.matmul(
                    psB[:, 512:1024],
                    vsl.buf[:, c * 128:(c + 1) * 128],
                    PT.rearrange("p (c t) -> p c t", c=16)[:, c, :],
                    start=(c == 0), stop=(c == nchT - 1))
                    for c in range(nchT)]
            av = em.emit("tensor", fn_av,
                         waits=[vsl.prod_tok, st["ptzero"], st["av_prev"]]
                         + st["pt_tile"], inc="sPE", inc_by=nchT)
            vsl.cons_toks.append(av)
            avev = te["evac"](av, rbccp, rbc_sb)
            st["av_prev"] = avev
            st["rbc_free"] = avev
            st["rows"] = []
            st["pt_tile"] = []

        def push(task):
            if task.get("zero") is not None:
                tile = task["zero"]

                def fn_z(v, tile=tile, PT=PT):
                    return [v.memset(
                        PT.rearrange("p (c t) -> p c t", c=16)[
                            :, tile * 4 + cc:tile * 4 + cc + 1,
                            0:cc * 128], 0.0) for cc in range(1, 4)]
                st["ptzero"] = em.emit("vector", fn_z,
                                       waits=[st["av_prev"]], inc="sDV",
                                       inc_by=3)
                return
            qb, span = task["qb"], task["span"]
            ksl, qsl, qcol = task["ksl"], task["qsl"], task["qcol"]
            nfull, rem = span // 512, span % 512
            dc = qb // 4

            def fn_sc(t, qb=qb, nfull=nfull, rem=rem, dc=dc,
                      ksl=ksl, qsl=qsl, qcol=qcol, causal=causal):
                lhs = qsl.buf[:, qcol * 128:(qcol + 1) * 128]
                out = []
                for c in range(nfull):
                    out.append(t.matmul(
                        psA[:, c * 512:(c + 1) * 512], lhs,
                        ksl.buf[:, c * 512:(c + 1) * 512],
                        start=True,
                        stop=(not causal) or (c != dc)))
                if rem:
                    out.append(t.matmul(
                        psA[:, nfull * 512:nfull * 512 + rem], lhs,
                        ksl.buf[:, nfull * 512:nfull * 512 + rem],
                        start=True, stop=(nfull != dc)))
                if causal:
                    out.append(t.matmul(psA[:, qb * 128:qb * 128 + 128],
                                        maskT, identB, start=False,
                                        stop=True))
                return out
            n_sc = nfull + (1 if rem else 0) + (1 if causal else 0)
            sc = em.emit("tensor", fn_sc,
                         waits=[ksl.prod_tok, qsl.prod_tok,
                                st["cp_prev"], tok_const]
                         + task.get("extra", []),
                         inc="sPE", inc_by=n_sc)
            ksl.cons_toks.append(sc)
            qsl.cons_toks.append(sc)
            nchunks = nfull + (1 if rem else 0)
            nh = (span + 1023) // 1024
            cps = []
            mx_toks = []
            for h in range(nh):
                h0, h1 = h * 1024, min(span, (h + 1) * 1024)
                c2 = min(2 * h + 1, nchunks - 1)
                ctok = (sc[0], sc[1] - (n_sc - 1 - c2))
                if causal and qb // 4 <= c2:
                    ctok = sc
                cp = em.emit("scalar",
                             (lambda s_, h0=h0, h1=h1: s_.copy(
                                 scp[:, h0:h1], psA[:, h0:h1])),
                             waits=[ctok], inc="sAC")
                cps.append(cp)
                mx_toks.append(em.emit(
                    "vector",
                    (lambda v, h=h, h0=h0, h1=h1: v.reduce_max(
                        t_mxc[:, h:h + 1], scp[:, h0:h1], axis=AX.X)),
                    waits=[cp, st["exp_prev"]]))
            st["cp_prev"] = cps[-1]
            mx = em.emit("vector",
                         (lambda v, nh=nh: (
                             v.reduce_max(t_nsx[:], t_mxc[:, 0:nh],
                                          axis=AX.X, negate=True))),
                         waits=mx_toks + [st["exp_prev"]])
            pb = pb_[qb % 2]

            pk2 = qb % 2

            def fn_exp(s_, pb=pb, span=span, pk2=pk2):
                return s_.activation(pb[:, 0:span], scp[:, 0:span],
                                     AF.Exp, bias=t_nsx[:, 0:1],
                                     accum_out=t_sumsP[pk2][:])
            ex = em.emit("scalar", fn_exp,
                         waits=[mx, st["p_free"][pk2], st["rc_prev"][pk2]],
                         inc="sAC")
            st["exp_prev"] = ex
            rc = em.emit("vector",
                         (lambda v, pk2=pk2: v.reciprocal(
                             t_rc[pk2][:], t_sumsP[pk2][:])),
                         waits=[ex, st["rT_prev"][pk2]], inc="sDV")
            st["rc_prev"][pk2] = rc
            if st["pend"] is not None:
                emit_tail(st["pend"])
                if st["pend"]["tile_end"] is not None:
                    emit_tile(st["pend"]["tile_end"], st["pend"]["tile"])
            st["pend"] = {"qb": qb, "span": span, "ex": ex, "rc": rc,
                          "tile_end": task.get("tile_end"),
                          "tile": task.get("tile", 0)}

        def flush():
            if st["pend"] is not None:
                emit_tail(st["pend"])
                if st["pend"]["tile_end"] is not None:
                    emit_tile(st["pend"]["tile_end"], st["pend"]["tile"])
                st["pend"] = None
        return push, flush


    if "A" in phases:
        xbufs = [view(scr, PH0 + i * 8192, 4096) for i in range(2)]
        qk_stg = [view(scr, PH0 + 16384 + i * 4096, 2048) for i in range(2)]
        v_stg = [view(scr, PH0 + 24576 + i * 1024, 512) for i in range(2)]
        xring = Ring(em, "xa", xbufs)
        stQK = StoreRing("stQK")
        stV = StoreRing("stV")
        stAT = StoreRing("stAT")

        qk_psum_free = [None, None]
        qk_stg_free = [None, None]
        v_bank_free = [None] * 8
        v_stg_free = [None, None]

        for half in range(2):
            ln_last = None
            for tbl in range(8):
                tbg = half * 8 + tbl
                sl = xring.acquire()
                xring.produce(sl, "gpsimd",
                              (lambda g, tbg=tbg, sl=sl: g.dma_start(
                                  sl.buf, xf[tbg * 128:(tbg + 1) * 128, :])))
                guard = ()
                if half == 1 and tbl == 0:
                    guard = qk_psum_free + [t for t in v_bank_free
                                            if t is not None]
                tok = emit_ln_block(sl.buf, tbl, extra_w=[sl.prod_tok],
                                    psum_guard=guard)
                sl.cons_toks.append(tok)
                ln_last = tok
            lnfm_rdy = ln_last

            for W, dstD, qscl in ((wq_my, Qa, SCALE_A), (wk_my, Ka, None)):
                for fp in range(8):
                    pk = fp % 2
                    ps = psA if pk == 0 else psB
                    last_mm = None
                    for ic in range(NIC):
                        sl = wring.acquire()
                        wring.produce(sl, "gpsimd",
                                      (lambda g, W=W, fp=fp, ic=ic, sl=sl:
                                       g.dma_start(sl.buf[:, 0:256],
                                                   W[ic * 128:(ic + 1) * 128,
                                                     fp * 256:(fp + 1) * 256])))
                        pw = [lnfm_rdy]
                        if ic == 0:
                            pw.append(qk_psum_free[pk])
                            pw += [t for t in v_bank_free if t is not None]

                        def fn_mm(t, ic=ic, sl=sl, ps=ps):
                            out = []
                            for fc in range(2):
                                for tt in range(2):
                                    out.append(t.matmul(
                                        ps[:, fc * 1024 + tt * 512:
                                           fc * 1024 + (tt + 1) * 512],
                                        sl.buf[:, fc * 128:(fc + 1) * 128],
                                        bigv(ic, tt * 512, (tt + 1) * 512),
                                        start=(ic == 0), stop=(ic == NIC - 1)))
                            return out
                        last_mm = wring.consume(sl, "tensor", fn_mm,
                                                extra_waits=pw, inc="sPE",
                                                inc_by=4)
                        tick(2)
                    stg = qk_stg[pk]

                    def fn_ev(s, ps=ps, stg=stg, qscl=qscl):
                        if qscl is None:
                            return s.copy(stg, ps[:, 0:2048])
                        return s.activation(stg, ps[:, 0:2048], AF.Copy,
                                            scale=qscl)
                    ev = em.emit("scalar", fn_ev,
                                 waits=[last_mm, qk_stg_free[pk]], inc="sAC")
                    qk_psum_free[pk] = ev

                    def fn_st(sy, dstD=dstD, fp=fp, half=half, stg=stg):
                        return sy.dma_start(
                            dstD.rearrange("(a p) t -> p a t", p=128)[
                                :, 2 * fp:2 * fp + 2,
                                half * 1024:(half + 1) * 1024],
                            stg.rearrange("p (a t) -> p a t", a=2))
                    qk_stg_free[pk] = stQK.store(pk, fn_st, waits=[ev])

            for vc in range(4):
                last_mm = None
                for ic in range(NIC):
                    sl = wring.acquire()
                    wring.produce(sl, "gpsimd",
                                  (lambda g, vc=vc, ic=ic, sl=sl:
                                   g.dma_start(sl.buf,
                                               wv_my[ic * 128:(ic + 1) * 128,
                                                     vc * 512:(vc + 1) * 512])))
                    pw = [lnfm_rdy]
                    if ic == 0:
                        pw += [t for t in v_bank_free if t is not None]
                        pw += qk_psum_free

                    def fn_mm(t, ic=ic, sl=sl):
                        out = []
                        for tb in range(8):
                            out.append(t.matmul(
                                bank(tb), bigv(ic, tb * 128, (tb + 1) * 128),
                                sl.buf, start=(ic == 0),
                                stop=(ic == NIC - 1)))
                        return out
                    last_mm = wring.consume(sl, "tensor", fn_mm,
                                            extra_waits=pw, inc="sPE",
                                            inc_by=8)
                    tick()
                for tb in range(8):
                    pk = tb % 2
                    stg = v_stg[pk]

                    def fn_ev(v, tb=tb, stg=stg):
                        return v.tensor_copy(stg, bank(tb))
                    ev = em.emit("vector", fn_ev,
                                 waits=[last_mm, v_stg_free[pk]], inc="sDV")
                    v_bank_free[tb] = ev

                    def fn_st(sy, tb=tb, vc=vc, half=half, stg=stg):
                        r0 = half * 1024 + tb * 128
                        return sy.dma_start(
                            Va[r0:r0 + 128, vc * 512:(vc + 1) * 512], stg)
                    v_stg_free[pk] = stV.store(pk, fn_st, waits=[ev])

        if a_stop < 2:
            a_end = res_toks
        kb = [view(big1, i * 4096, 2048) for i in range(2)]
        qb_ = [view(big1, 8192 + i * 4096, 2048) for i in range(2)]
        vb = [view(big1, 16384 + i * 4096, 2048) for i in range(2)]
        pb_ = [view(big1, 24576 + i * 4096, 2048) for i in range(2)]
        PT = view(big1, 32768, 8192)
        astg = [view(big1, 49152 + i * 1024, 512) for i in range(2)]
        rbc_sb = view(big1, 51200, 512, F32)
        fA3 = fence()
        kring = Ring(em, "ka", kb, init_cons=fA3)
        qring = Ring(em, "qa", qb_, init_cons=fA3)
        vring = Ring(em, "va", vb, init_cons=fA3)
        all_qkv_l = stQK.all_toks() + stV.all_toks()

        pt_pack_free = [fA3, fA3]
        p_free = [fA3, fA3]
        exp_prev = fA3
        av_prev = fA3
        avst_free = [None, None]
        rbc_free = fA3
        row_free = fA3
        trow_free = fA3
        ptzero = fA3

        st_at = dict(pt_pack_free=[fA3, fA3], p_free=[fA3, fA3],
                     exp_prev=fA3, cp_prev=fA3, av_prev=fA3, rbc_free=fA3,
                     row_free=fA3, trow_free=fA3, ptzero=fA3,
                     rT_prev=[fA3, fA3], rc_prev=[fA3, fA3],
                     rows=[], pt_tile=[])
        push_a, flush_a = attention_pipe(pb_, PT, rbc_sb, SCALE_A, True,
                                         st_at)

        def mk_evac_a(h, tile):
            def evac(av, rbccp, rbc_sb):
                stg = astg[tile % 2]

                def fn_avev(v, stg=stg, rbc_sb=rbc_sb):
                    return v.tensor_tensor(stg, psB[:, 512:1024], rbc_sb,
                                           op=ALU.mult)
                avev = em.emit("vector", fn_avev,
                               waits=[av, rbccp, avst_free[tile % 2]],
                               inc="sDV")

                def fn_avst(sy, h=h, tile=tile, stg=stg):
                    return sy.dma_start(
                        attn_ctr[h * 128:(h + 1) * 128,
                                 tile * 512:(tile + 1) * 512], stg)
                avst_free[tile % 2] = stAT.store(tile % 2, fn_avst,
                                                 waits=[avev])
                return avev
            return evac

        for h in range(16 if a_stop >= 2 else 0):
            ksl = kring.acquire()
            kring.produce(ksl, "sync",
                          (lambda sy, h=h, ksl=ksl: sy.dma_start(
                              ksl.buf, Ka[h * 128:(h + 1) * 128, :])),
                          extra_waits=all_qkv_l)
            qsl = qring.acquire()
            qring.produce(qsl, "sync",
                          (lambda sy, h=h, qsl=qsl: sy.dma_start(
                              qsl.buf, Qa[h * 128:(h + 1) * 128, :])),
                          extra_waits=all_qkv_l)
            vsl = vring.acquire()

            def fn_vl(sy, h=h, vsl=vsl):
                for q4 in range(4):
                    sy.dma_start(
                        vsl.buf.rearrange("p (c d) -> p c d", c=16)[
                            :, q4 * 4:(q4 + 1) * 4, :],
                        Va.rearrange("(c p) v -> p c v", p=128)[
                            :, q4 * 4:(q4 + 1) * 4, h * 128:(h + 1) * 128]
                    ).then_inc(SEMH[vsl.sem], 16)
                return None
            vring.produce(vsl, "sync", fn_vl, extra_waits=all_qkv_l, n_dma=4)

            for tile in range(4):
                for qbl in range(4):
                    qb = tile * 4 + qbl
                    t = {"qb": qb, "span": (qb + 1) * 128, "ksl": ksl,
                         "qsl": qsl, "qcol": qb, "tile": tile, "zero": None}
                    if qbl == 3:
                        t["tile_end"] = {"nchT": (tile + 1) * 4, "vsl": vsl,
                                         "evac": mk_evac_a(h, tile)}
                    else:
                        t["tile_end"] = None
                    push_a(t)
                    tick(2)
                    if qbl == 0:
                        push_a({"zero": tile})
        if a_stop >= 2:
            flush_a()

        def fn_cc(g):
            return g.collective_compute(
                "AllGather", ALU.bypass,
                replica_groups=[[0, 1], [2, 3], [4, 5], [6, 7]],
                ins=[attn_ctr.opt()], outs=[attn_gth.opt()])
        cc_tok = None
        if a_stop >= 3:
            cc_tok = em.emit("gpsimd", fn_cc, waits=stAT.all_toks(),
                             inc="sCC")

        em.sem("qAFM")

        def fn_afm(g):
            cid = g.compute_val(nc.partition_id(engines=[mybir.EngineType.Pool]))
            off = g.snap((cid % 2) * 1024, min_val=0, max_val=1024)
            src = attn_gth.rearrange("(c p) t -> p c t", p=128)
            dst = BIG.rearrange("p (c t) -> p c t", c=NIC)
            for j in range(4):
                g.dma_start(dst[:, j * 8:(j + 1) * 8, :],
                            src[:, j * 8:(j + 1) * 8, bass.ds(off, 1024)]
                            ).then_inc(SEMH["qAFM"], 16)
            return None
        afm_tok = None
        if a_stop >= 4:
            afm_tok = em.emit("gpsimd", fn_afm, waits=[cc_tok] + fence(),
                              inc="qAFM", inc_by=16 * 4, manual=True)

        oev = [None] * 8
        oev[0] = fence()
        for oc in range(8 if a_stop >= 5 else 0):
            last_mm = None
            for fc in range(NIC):
                sl = wring.acquire()
                wring.produce(sl, "gpsimd",
                              (lambda g, oc=oc, fc=fc, sl=sl:
                               g.dma_start(sl.buf,
                                           wo[fc * 128:(fc + 1) * 128,
                                              oc * 512:(oc + 1) * 512])))
                pw = [afm_tok, tok_res_all]
                if fc == 0:
                    pw += [t for t in oev if t is not None]

                def fn_mm(t, fc=fc, sl=sl):
                    return [t.matmul(bank(tb),
                                     bigv(fc, tb * 128, (tb + 1) * 128),
                                     sl.buf, start=(fc == 0),
                                     stop=(fc == NIC - 1))
                            for tb in range(8)]
                last_mm = wring.consume(sl, "tensor", fn_mm, extra_waits=pw,
                                        inc="sPE", inc_by=8)
                tick(2)
            for tb in range(8):
                def fn_ev(v, tb=tb, oc=oc):
                    d = resv(tb)[:, oc * 512:(oc + 1) * 512]
                    return v.tensor_tensor(d, bank(tb), d, op=ALU.add)
                oev[tb] = em.emit("vector", fn_ev, waits=[last_mm], inc="sDV")
        a_end = [t for t in oev if t is not None]
        if a_stop < 5:
            a_end = res_toks + ([cc_tok] if cc_tok else
                                ([avst_free[0], avst_free[1]]
                                 if a_stop >= 2 else []))

    out_stg_free = [None, None]
    stOUT = StoreRing("stOUT")

    def emit_ffn(prev_toks, to_output):
        gwt_sb = view(scr, PH0, 256)
        ics_sb = view(scr, PH0 + 512, 4096, parts=8)
        ocs_sb = view(scr, PH0 + 8704, 4096, parts=8)
        bias_bc = view(scr, PH0 + 16896, 4096)
        routT = view(scr, PH0 + 25088, 1024, parts=8)
        routf = view(scr, PH0 + 27136, 8, F32)
        routb = view(scr, PH0 + 27168, 8)
        os_sb = [view(scr, PH0 + 27200 + i * 1024, 512) for i in range(8)]
        tu_sb = [view(scr, PH0 + 35392 + i * 2048, 512, F32)
                 for i in range(2)]
        outst = [view(scr, PH0 + 39488 + i * 2048, 512, F32)
                 for i in range(2)]

        rsem = rowsem()
        fF = fence()

        def fn_rows(g):
            g.dma_start(gwt_sb.rearrange("p (c e) -> p c e", c=NIC),
                        gwT.rearrange("(c p) e -> p c e", p=128)
                        ).then_inc(SEMH[rsem], 16)
            g.dma_start(ics_sb, ics[:, :]).then_inc(SEMH[rsem], 16)
            g.dma_start(ocs_sb, ocs[:, :]).then_inc(SEMH[rsem], 16)
            g.dma_start(bias_bc[0:1, :], fbias[:, :]).then_inc(SEMH[rsem], 16)
            return None
        rows_tok = em.emit("gpsimd", fn_rows, waits=list(prev_toks),
                           inc=rsem, inc_by=64, manual=True)

        bb_prev = fF
        bb_toks = []
        for j in range(8):
            def fn_bb(t, j=j):
                return t.matmul(psB[:, 1024:1536], onesr,
                                bias_bc[0:1, j * 512:(j + 1) * 512],
                                start=True, stop=True)
            t1 = em.emit("tensor", fn_bb, waits=[rows_tok, tok_const, bb_prev],
                         inc="sPE")

            def fn_bbc(s, j=j):
                return s.copy(bias_bc[:, j * 512:(j + 1) * 512],
                                     psB[:, 1024:1536])
            bb_prev = em.emit("scalar", fn_bbc, waits=[t1], inc="sAC")
            bb_toks.append(bb_prev)
        bias_rdy = bb_toks[-1]

        ln_toks = []
        for tb in range(NTB):
            ln_toks.append(emit_ln_block(resv(tb), tb,
                                         extra_w=list(prev_toks)))
        lnfm_rdy = ln_toks[-1]

        rt_prev = fF
        lg_free = fF
        rt_toks = []
        for tb in range(NTB):
            def fn_lg(t, tb=tb):
                return [t.matmul(
                    psB[:, 1024:1032],
                    bigv(ic, tb * 128, (tb + 1) * 128),
                    gwt_sb.rearrange("p (c e) -> p c e", c=NIC)[:, ic, :],
                    start=(ic == 0), stop=(ic == NIC - 1))
                    for ic in range(NIC)]
            lg = em.emit("tensor", fn_lg,
                         waits=[lnfm_rdy, rows_tok, bias_rdy, lg_free],
                         inc="sPE", inc_by=NIC)

            sm0 = em.emit("vector",
                          lambda v: v.reduce_max(t_mx[:], psB[:, 1024:1032],
                                                 axis=AX.X), waits=[lg])
            sm = em.emit("vector",
                         lambda v: v.tensor_scalar(t_nsx[:], t_mx[:], -1.0,
                                                   None, op0=ALU.mult),
                         waits=[sm0])

            def fn_ex(s):
                return s.activation(routf[:, 0:8], psB[:, 1024:1032], AF.Exp,
                                    bias=t_nsx[:, 0:1],
                                    accum_out=t_sumsP[0][:])
            ex = em.emit("scalar", fn_ex, waits=[sm], inc="sAC")
            lg_free = ex

            nr0 = em.emit("vector",
                          lambda v: v.reciprocal(t_rc[0][:], t_sumsP[0][:]),
                          waits=[ex])
            nr = em.emit("vector",
                         lambda v: v.tensor_scalar(routb[:, 0:8],
                                                   routf[:, 0:8],
                                                   t_rc[0][:, 0:1], None,
                                                   op0=ALU.mult),
                         waits=[nr0])

            def fn_rt(t):
                d = psB[:, 1536:2048].bitcast(BF16)[0:8, 0:128]
                return t.transpose(d, routb[:, 0:8], identB)
            rt = em.emit("tensor", fn_rt, waits=[nr, rt_prev], inc="sPE")

            def fn_rtc(s, tb=tb):
                return s.copy(routT[0:8, tb * 128:(tb + 1) * 128],
                              psB[:, 1536:2048].bitcast(BF16)[0:8, 0:128])
            rt_prev = em.emit("scalar", fn_rtc, waits=[rt], inc="sAC")
            rt_toks.append(rt_prev)
        routT_rdy = rt_toks[-1]

        is_free = [fF, fF]
        xs_last = None
        for ic in range(NIC):
            for tt in range(2):
                pk = (ic * 2 + tt) % 2
                ps = psA[:, pk * 512:(pk + 1) * 512]

                def fn_is(t, ic=ic, tt=tt, ps=ps):
                    return t.matmul(ps, ics_sb[:, ic * 128:(ic + 1) * 128],
                                    routT[0:8, tt * 512:(tt + 1) * 512],
                                    start=True, stop=True)
                ist = em.emit("tensor", fn_is,
                              waits=[routT_rdy, is_free[pk]], inc="sPE")

                def fn_xs(v, ic=ic, tt=tt, ps=ps):
                    d = bigv(ic, tt * 512, (tt + 1) * 512)
                    return v.tensor_tensor(d, ps, d, op=ALU.mult)
                xs_last = em.emit("vector", fn_xs, waits=[ist], inc="sDV")
                is_free[pk] = xs_last
        xs_rdy = xs_last

        badd = []
        for tb in range(NTB):
            def fn_ba(g, tb=tb):
                return g.tensor_tensor(resv(tb), resv(tb), bias_bc,
                                       op=ALU.add)
            badd.append(em.emit("gpsimd", fn_ba,
                                waits=[bias_rdy, ln_toks[tb]], inc="sGP"))

        ev_done = [None] * 8
        ev_done[0] = fF
        os_free = [fF] * 8
        tu_free = [fF, fF]
        out_toks = []
        for oc in range(8):
            os_toks = []
            for tb in range(NTB):
                pk = tb % 2
                ps = psB[:, 1024 + pk * 512:1536 + pk * 512]

                def fn_os(t, tb=tb, oc=oc, ps=ps):
                    return t.matmul(ps, routT[0:8, tb * 128:(tb + 1) * 128],
                                    ocs_sb[:, oc * 512:(oc + 1) * 512],
                                    start=True, stop=True)
                pw = [routT_rdy, ev_done[6 + pk], xs_rdy]
                if len(os_toks) >= 2:
                    pw.append(os_toks[-2][1])
                ot = em.emit("tensor", fn_os, waits=pw, inc="sPE")

                def fn_osc(s, tb=tb, ps=ps):
                    return s.copy(os_sb[tb], ps)
                oc_t = em.emit("scalar", fn_osc,
                               waits=[ot, os_free[tb]], inc="sAC")
                os_toks.append((ot, oc_t))
            os_all = os_toks[-1][1]

            last_mm = None
            for ic in range(NIC):
                sl = wring.acquire()
                wring.produce(sl, "gpsimd",
                              (lambda g, oc=oc, ic=ic, sl=sl:
                               g.dma_start(sl.buf,
                                           fT[ic * 128:(ic + 1) * 128,
                                              oc * 512:(oc + 1) * 512])))
                gs = sgnring.acquire()

                def fn_sgn(s_, sl=sl, gs=gs):
                    return s_.activation(gs.buf, sl.buf, AF.Sign)
                sgnring.produce(gs, "scalar", fn_sgn,
                                extra_waits=[sl.prod_tok] + list(prev_toks))
                sl.cons_toks.append(gs.prod_tok)
                pw = [xs_rdy, os_all]
                if ic == 0:
                    pw += [t for t in ev_done if t is not None]

                def fn_mm(t, ic=ic, gs=gs):
                    return [t.matmul(bank(tb),
                                     bigv(ic, tb * 128, (tb + 1) * 128),
                                     gs.buf, start=(ic == 0),
                                     stop=(ic == NIC - 1))
                            for tb in range(8)]
                last_mm = sgnring.consume(gs, "tensor", fn_mm,
                                          extra_waits=pw, inc="sPE",
                                          inc_by=8)
            for tb in range(8):
                pk = tb % 2
                tu = tu_sb[pk]

                def fn_e1(v, tb=tb, tu=tu):
                    return v.tensor_tensor(tu, bank(tb), os_sb[tb],
                                           op=ALU.mult)
                e1 = em.emit("vector", fn_e1,
                             waits=[last_mm, os_toks[tb][1], tu_free[pk]],
                             inc="sDV")
                os_free[tb] = e1
                if not to_output:
                    def fn_e2(g, tb=tb, oc=oc, tu=tu):
                        d = resv(tb)[:, oc * 512:(oc + 1) * 512]
                        return g.tensor_tensor(d, tu, d, op=ALU.add)
                    e2 = em.emit("gpsimd", fn_e2, waits=[e1, badd[tb]],
                                 inc="sGP")
                    tu_free[pk] = e2
                    ev_done[tb] = e2
                else:
                    ost = outst[pk]

                    def fn_e2(g, tb=tb, oc=oc, ost=ost, tu=tu):
                        return g.tensor_tensor(
                            ost, tu, resv(tb)[:, oc * 512:(oc + 1) * 512],
                            op=ALU.add)
                    e2 = em.emit("gpsimd", fn_e2,
                                 waits=[e1, badd[tb], out_stg_free[pk]],
                                 inc="sGP")
                    tu_free[pk] = e2

                    def fn_yst(sy, tb=tb, oc=oc, ost=ost):
                        return sy.dma_start(
                            y[tb * 128:(tb + 1) * 128,
                              oc * 512:(oc + 1) * 512], ost)
                    yt = stOUT.store(pk, fn_yst, waits=[e2])
                    out_stg_free[pk] = yt
                    out_toks.append(yt)
                    ev_done[tb] = e2
        return out_toks if to_output else [t for t in ev_done
                                           if t is not None]

    a_end = list(a_end) + p0_drain()
    if "B" in phases:
        b_end = emit_ffn(a_end, to_output=False)
    else:
        b_end = a_end

    if "C" in phases:
        stKV = StoreRing("stKV")
        stQB = StoreRing("stQB")
        stAB = StoreRing("stAB")
        sgnC = Ring(em, "gc", [view(scr, 55296 + i * 1024, 512)
                               for i in range(4)], dma=False)
        stg2 = [view(scr, SGN0 + i * 2048, 1024) for i in range(2)]
        svbc = view(scr, PH0 + 40960, 512)
        tuv2 = [view(scr, SGN0 + i * 2048, 512, F32)
                for i in range(2)]

        ln_toks = []
        for tb in range(NTB):
            ln_toks.append(emit_ln_block(resv(tb), tb, extra_w=list(b_end)))
        lnfm_rdy = ln_toks[-1]

        fC = fence()
        psum8_guard = list(fC)

        def str_ps(fc):
            src = psA if fc < 2 else psB
            return src[:, (fc % 2) * 1024:(fc % 2 + 1) * 1024]

        stg_free = [fC, fC]

        def grid_to_bcast(key, ob0):
            tg = []
            for j in range(4):
                def fn_tg(t, key=key, ob0=ob0, j=j):
                    return t.transpose(
                        psB[0:1, 1536 + j * 128:1536 + (j + 1) * 128],
                        sg_fin[key][:, ob0 + j:ob0 + j + 1], identF)
                tg.append(em.emit("tensor", fn_tg,
                                  waits=[tok_const] + psum8_guard
                                  + p0_marks["fin"], inc="sPE"))
            cb = em.emit("vector",
                         lambda v: v.tensor_copy(t_srowb[0:1, :],
                                                 psB[0:1, 1536:2048]),
                         waits=tg)
            bm = em.emit("tensor",
                         lambda t: t.matmul(psB[:, 1024:1536], onesr,
                                            t_srowb[0:1, :],
                                            start=True, stop=True),
                         waits=[cb, tok_const] + psum8_guard, inc="sPE")
            cc = em.emit("vector",
                         lambda v: v.tensor_copy(svbc, psB[:, 1024:1536]),
                         waits=[bm])
            psum8_guard.extend([cb, cc])
            return cc

        def feat_strips(SD, nst, key, Dst, stR):
            for st in range(nst):
                last_mm = None
                for ic in range(NIC):
                    sl = wring.acquire()
                    wring.produce(sl, "gpsimd",
                                  (lambda g, SD=SD, st=st, ic=ic, sl=sl:
                                   g.dma_start(sl.buf,
                                               SD[ic * 128:(ic + 1) * 128,
                                                  st * 512:(st + 1) * 512])))
                    gs = sgnC.acquire()

                    def fn_sgn(s_, sl=sl, gs=gs):
                        return s_.activation(gs.buf, sl.buf, AF.Sign)
                    sgnC.produce(gs, "scalar", fn_sgn,
                                 extra_waits=[sl.prod_tok] + list(b_end))
                    sl.cons_toks.append(gs.prod_tok)
                    pw = [lnfm_rdy]
                    if ic == 0:
                        pw = pw + psum8_guard

                    def fn_mm(t, ic=ic, gs=gs):
                        return [t.matmul(
                            str_ps(fc)[:, tt * 512:(tt + 1) * 512],
                            gs.buf[:, fc * 128:(fc + 1) * 128],
                            bigv(ic, tt * 512, (tt + 1) * 512),
                            start=(ic == 0), stop=(ic == NIC - 1))
                            for fc in range(4) for tt in range(2)]
                    last_mm = sgnC.consume(gs, "tensor", fn_mm,
                                              extra_waits=pw, inc="sPE",
                                              inc_by=8)
                for fc in range(4):
                    pk = fc % 2
                    stg = stg2[pk]

                    def fn_ev(e, fc=fc, st=st, key=key, stg=stg):
                        return e.activation(
                            stg, str_ps(fc), AF.Copy,
                            scale=sg_fin[key][:, st * 4 + fc:st * 4 + fc + 1])
                    ev = em.emit("scalar", fn_ev,
                                 waits=[last_mm, stg_free[pk]]
                                 + p0_marks["fin"], inc="sAC")
                    psum8_guard.append(ev)

                    def fn_st(sy, st=st, fc=fc, stg=stg, Dst=Dst):
                        r0 = st * 512 + fc * 128
                        return sy.dma_start(Dst[r0:r0 + 128, :], stg)
                    stg_free[pk] = stR.store(pk, fn_st, waits=[ev])

        feat_strips(kT, 2, "k", Kc, stKV)

        v_bank_free2 = [None] * 4
        vst_free = stg_free
        for st in range(2):
            svc = grid_to_bcast("v", st * 4)
            for tbp in range(2):
                last_mm = None
                for ic in range(NIC):
                    sl = wring.acquire()
                    wring.produce(sl, "gpsimd",
                                  (lambda g, st=st, ic=ic, sl=sl:
                                   g.dma_start(sl.buf,
                                               vT[ic * 128:(ic + 1) * 128,
                                                  st * 512:(st + 1) * 512])))
                    gs = sgnC.acquire()

                    def fn_sgn(s_, sl=sl, gs=gs):
                        return s_.activation(gs.buf, sl.buf, AF.Sign)
                    sgnC.produce(gs, "scalar", fn_sgn,
                                 extra_waits=[sl.prod_tok] + list(b_end))
                    sl.cons_toks.append(gs.prod_tok)
                    pw = [lnfm_rdy]
                    if ic == 0:
                        pw = pw + psum8_guard \
                            + [t for t in v_bank_free2 if t is not None]

                    def fn_mm(t, ic=ic, tbp=tbp, gs=gs):
                        return [t.matmul(
                            psA[:, j * 512:(j + 1) * 512],
                            bigv(ic, (tbp * 4 + j) * 128,
                                 (tbp * 4 + j + 1) * 128),
                            gs.buf,
                            start=(ic == 0), stop=(ic == NIC - 1))
                            for j in range(4)]
                    last_mm = sgnC.consume(gs, "tensor", fn_mm,
                                              extra_waits=pw, inc="sPE",
                                              inc_by=4)
                for j in range(4):
                    tb = tbp * 4 + j
                    pk = tb % 2
                    stg = stg2[pk]

                    def fn_ev(v, j=j, stg=stg):
                        return v.tensor_tensor(stg[:, 0:512],
                                               psA[:, j * 512:(j + 1) * 512],
                                               svbc, op=ALU.mult)
                    ev = em.emit("vector", fn_ev,
                                 waits=[last_mm, svc, vst_free[pk]])
                    v_bank_free2[j] = ev
                    psum8_guard.append(ev)

                    def fn_st(sy, tb=tb, st=st, stg=stg):
                        return sy.dma_start(Vc[tb * 128:(tb + 1) * 128,
                                               st * 512:(st + 1) * 512],
                                            stg[:, 0:512])
                    vst_free[pk] = stKV.store(pk, fn_st, waits=[ev])

        def fn_cc2(g):
            return g.collective_compute(
                "AllGather", ALU.bypass,
                replica_groups=[[0, 1], [2, 3], [4, 5], [6, 7]],
                ins=[Kc.opt()], outs=[Kg.opt()])
        em.emit("gpsimd", fn_cc2, waits=stKV.all_toks(), inc="sCC")

        def fn_cc3(g):
            return g.collective_compute(
                "AllGather", ALU.bypass,
                replica_groups=[[0, 1], [2, 3], [4, 5], [6, 7]],
                ins=[Vc.opt()], outs=[Vg.opt()])
        cc2 = em.emit("gpsimd", fn_cc3, waits=stKV.all_toks(), inc="sCC")

        feat_strips(qT, 8, "q", Qb, stQB)

        kb = [view(big1, i * 4096, 2048) for i in range(2)]
        vb = [view(big1, 8192 + i * 4096, 2048) for i in range(2)]
        qb2 = [view(big1, 16384 + i * 2048, 1024) for i in range(2)]
        pb_ = [view(big1, 20480 + i * 4096, 2048) for i in range(2)]
        PT = view(big1, 28672, 8192)
        astg = [view(big1, 45056 + i * 1024, 512) for i in range(2)]
        rbc_sb = view(big1, 47104, 512, F32)
        fC4 = fence()
        kringb = Ring(em, "kb", kb, init_cons=fC4)
        vringb = Ring(em, "vb", vb, init_cons=fC4)
        qringb = Ring(em, "qb", qb2, init_cons=fC4)
        all_qb_l = stQB.all_toks()

        pt_pack_free = [fC4, fC4]
        p_free = [fC4, fC4]
        exp_prev = fC4
        av_prev = fC4
        ab_stg_free = [None, None]
        rbc_free = fC4
        row_free = fC4
        trow_free = fC4

        st_cb = dict(pt_pack_free=[fC4, fC4], p_free=[fC4, fC4],
                     exp_prev=fC4, cp_prev=fC4, av_prev=fC4, rbc_free=fC4,
                     row_free=fC4, trow_free=fC4, ptzero=fC4,
                     rT_prev=[fC4, fC4], rc_prev=[fC4, fC4],
                     rows=[], pt_tile=[])
        push_b, flush_b = attention_pipe(pb_, PT, rbc_sb, SCALE_B, False,
                                         st_cb)

        ab_stg_free = [None, None]

        def mk_evac_b(hq, tile):
            def evac(av, rbccp, rbc_sb):
                stg = astg[tile % 2]

                def fn_avev(v, stg=stg, rbc_sb=rbc_sb):
                    return v.tensor_tensor(stg, psB[:, 512:1024], rbc_sb,
                                           op=ALU.mult)
                avev = em.emit("vector", fn_avev,
                               waits=[av, rbccp, ab_stg_free[tile % 2]],
                               inc="sDV")

                def fn_ast(sy, hq=hq, tile=tile, stg=stg):
                    return sy.dma_start(
                        Ab[hq * 128:(hq + 1) * 128,
                           tile * 512:(tile + 1) * 512], stg)
                ab_stg_free[tile % 2] = stAB.store(tile % 2, fn_ast,
                                                   waits=[avev])
                return avev
            return evac

        for kvh in range(8):
            ksl = kringb.acquire()

            def fn_kl(sy, kvh=kvh, ksl=ksl):
                sy.dma_start(ksl.buf[:, 0:1024],
                             Kg[kvh * 128:(kvh + 1) * 128, :]
                             ).then_inc(SEMH[ksl.sem], 16)
                sy.dma_start(ksl.buf[:, 1024:2048],
                             Kg[1024 + kvh * 128:1024 + (kvh + 1) * 128, :]
                             ).then_inc(SEMH[ksl.sem], 16)
                return None
            kringb.produce(ksl, "sync", fn_kl, extra_waits=[cc2], n_dma=2)
            vsl = vringb.acquire()

            def fn_vl(sy, kvh=kvh, vsl=vsl):
                for q4 in range(4):
                    sy.dma_start(
                        vsl.buf.rearrange("p (c d) -> p c d", c=16)[
                            :, q4 * 4:(q4 + 1) * 4, :],
                        Vg.rearrange("(c p) v -> p c v", p=128)[
                            :, q4 * 4:(q4 + 1) * 4,
                            kvh * 128:(kvh + 1) * 128]
                    ).then_inc(SEMH[vsl.sem], 16)
                return None
            vringb.produce(vsl, "sync", fn_vl, extra_waits=[cc2], n_dma=4)

            for qh in range(4):
                hq = kvh * 4 + qh
                qsl = qringb.acquire()
                qringb.produce(qsl, "sync",
                               (lambda sy, hq=hq, qsl=qsl: sy.dma_start(
                                   qsl.buf, Qb[hq * 128:(hq + 1) * 128, :])),
                               extra_waits=all_qb_l)
                for tile in range(2):
                    for qbl in range(4):
                        qb = tile * 4 + qbl
                        t = {"qb": qb, "span": 2048, "ksl": ksl,
                             "qsl": qsl, "qcol": qb, "tile": tile,
                             "zero": None, "tile_end": None}
                        if qbl == 3:
                            t["tile_end"] = {"nchT": 16, "vsl": vsl,
                                             "evac": mk_evac_b(hq, tile)}
                        push_b(t)
        flush_b()

        em.sem("qABM")

        def fn_abm(sy):
            for fc in range(NIC):
                sy.dma_start(BIG[:, fc * TOK:(fc + 1) * TOK],
                             Ab[fc * 128:(fc + 1) * 128, :]
                             ).then_inc(SEMH["qABM"], 16)
            return None
        abm_tok = em.emit("sync", fn_abm, waits=stAB.all_toks(),
                          inc="qABM", inc_by=16 * NIC, manual=True)

        fC5 = fence()
        psum8_guard.extend(list(fC5))
        oev = [None] * 8
        oev[0] = fC5
        o_bank_free = [None] * 4
        tuv2_init = stQB.all_toks() + stKV.all_toks() + list(fC5)
        tu_free2 = [tuv2_init, tuv2_init]
        for oc in range(8):
            svc = grid_to_bcast("o", oc * 4)
            for tbp in range(2):
                last_mm = None
                for ic in range(NIC):
                    sl = wring.acquire()
                    wring.produce(sl, "gpsimd",
                                  (lambda g, oc=oc, ic=ic, sl=sl:
                                   g.dma_start(sl.buf,
                                               oT[ic * 128:(ic + 1) * 128,
                                                  oc * 512:(oc + 1) * 512])))
                    gs = sgnC.acquire()

                    def fn_sgn(s_, sl=sl, gs=gs):
                        return s_.activation(gs.buf, sl.buf, AF.Sign)
                    sgnC.produce(gs, "scalar", fn_sgn,
                                 extra_waits=[sl.prod_tok] + list(b_end))
                    sl.cons_toks.append(gs.prod_tok)
                    pw = [abm_tok]
                    if ic == 0:
                        pw = pw + psum8_guard \
                            + [t for t in o_bank_free if t is not None]

                    def fn_mm(t, ic=ic, tbp=tbp, gs=gs):
                        return [t.matmul(
                            psA[:, j * 512:(j + 1) * 512],
                            bigv(ic, (tbp * 4 + j) * 128,
                                 (tbp * 4 + j + 1) * 128),
                            gs.buf,
                            start=(ic == 0), stop=(ic == NIC - 1))
                            for j in range(4)]
                    last_mm = sgnC.consume(gs, "tensor", fn_mm,
                                              extra_waits=pw, inc="sPE",
                                              inc_by=4)
                for j in range(4):
                    tb = tbp * 4 + j
                    pk = tb % 2
                    tu = tuv2[pk]

                    def fn_e1(v, j=j, tu=tu):
                        return v.tensor_tensor(tu,
                                               psA[:, j * 512:(j + 1) * 512],
                                               svbc, op=ALU.mult)
                    e1 = em.emit("vector", fn_e1,
                                 waits=[last_mm, svc, tu_free2[pk]])
                    o_bank_free[j] = e1
                    psum8_guard.append(e1)

                    def fn_e2(g, tb=tb, oc=oc, tu=tu):
                        d = resv(tb)[:, oc * 512:(oc + 1) * 512]
                        return g.tensor_tensor(d, tu, d, op=ALU.add)
                    e2 = em.emit("gpsimd", fn_e2, waits=[e1])
                    tu_free2[pk] = [e2]
                    oev[tb] = e2
        c_end = [t for t in oev if t is not None]
    else:
        c_end = b_end

    if "D" in phases:
        emit_ffn(c_end, to_output=True)
        final_l = stOUT.all_toks()
    else:
        em.sem("qDUMP")
        def fn_dump(g):
            for tb in range(NTB):
                g.dma_start(y[tb * 128:(tb + 1) * 128, :],
                            resv(tb)).then_inc(SEMH["qDUMP"], 16)
            return None
        em.emit("gpsimd", fn_dump, waits=list(c_end) + [tok_res_all],
                inc="qDUMP", inc_by=16 * NTB, manual=True)
        final_l = [("qDUMP", em.counts["qDUMP"])]

    em.emit("sync", lambda sy: None, waits=final_l)

    with nc.Block() as blk:
        def runner(eng_name):
            def go(eng):
                for idx, (fn, waits, inc, lineno) in enumerate(em.ops[eng_name]):
                    try:
                        for s, v in waits:
                            eng.wait_ge(SEMH[s], v)
                        last = fn(eng)
                        for ins_ in (last if isinstance(last, list)
                                     else ([last] if last is not None else [])):
                            try:
                                ins_.annotate(f"{eng_name}:{idx}:L{lineno}")
                            except Exception:
                                pass
                        if inc is not None:
                            name, by = inc
                            if isinstance(last, list):
                                assert len(last) == by, \
                                    f"inc_by {by} != {len(last)} insts"
                                for ins_ in last:
                                    ins_.then_inc(SEMH[name], 1)
                            else:
                                assert last is not None, \
                                    f"no inst for inc {name}"
                                last.then_inc(SEMH[name], by)
                    except Exception as ex:
                        import traceback
                        co = fn.__code__
                        raise RuntimeError(
                            f"step {idx} on {eng_name} "
                            f"({co.co_filename}:{co.co_firstlineno}): {ex}"
                        ) from ex
            return go
        blk.tensor(runner("tensor"))
        blk.vector(runner("vector"))
        blk.scalar(runner("scalar"))
        blk.gpsimd(runner("gpsimd"))
        blk.sync(runner("sync"))
    nc._em = em
    return nc


def kernel(**inputs):
    x = np.asarray(inputs["L0_attn_a_hidden_states"], dtype=np.float32)
    wq = np.asarray(inputs["L0_attn_a_wq"], dtype=np.float32)
    wkv = np.asarray(inputs["L0_attn_a_wkv"], dtype=np.float32)
    wo_ = np.asarray(inputs["L0_attn_a_wo"], dtype=np.float32)
    C = np.ascontiguousarray
    qT = C(np.asarray(inputs["L0_attn_b_q_weight"], dtype=np.float32).T)
    kT = C(np.asarray(inputs["L0_attn_b_k_weight"], dtype=np.float32).T)
    vT = C(np.asarray(inputs["L0_attn_b_v_weight"], dtype=np.float32).T)
    oT = C(np.asarray(inputs["L0_attn_b_o_weight"], dtype=np.float32).T)
    fT = C(np.asarray(inputs["L0_ffn_weight"], dtype=np.float32).T)
    gwT = C(np.asarray(inputs["L0_ffn_gate_w"], dtype=np.float32).T)
    icsv = np.asarray(inputs["L0_ffn_in_channel_scale"], dtype=np.float32)
    ocsv = np.asarray(inputs["L0_ffn_out_channel_scale"], dtype=np.float32)
    fb = np.asarray(inputs["L0_ffn_bias"], dtype=np.float32).reshape(1, -1)
    qN = C(np.asarray(inputs["L0_attn_b_q_weight"], dtype=np.float32))
    kN = C(np.asarray(inputs["L0_attn_b_k_weight"], dtype=np.float32))
    vN = C(np.asarray(inputs["L0_attn_b_v_weight"], dtype=np.float32))
    oN = C(np.asarray(inputs["L0_attn_b_o_weight"], dtype=np.float32))
    qscg = C(np.asarray(inputs["L0_attn_b_q_scale"],
                        dtype=np.float32).reshape(32, 128).T)
    kscg = C(np.asarray(inputs["L0_attn_b_k_scale"],
                        dtype=np.float32).reshape(8, 128).T)
    vscg = C(np.asarray(inputs["L0_attn_b_v_scale"],
                        dtype=np.float32).reshape(8, 128).T)
    oscg = C(np.asarray(inputs["L0_attn_b_o_scale"],
                        dtype=np.float32).reshape(32, 128).T)

    in_maps = []
    for core in range(8):
        b, g = core // 2, core % 2
        qcols = np.concatenate([np.arange(h * 192, h * 192 + 128)
                                for h in range(g * 16, (g + 1) * 16)])
        in_maps.append({
            "xf": C(x[b]),
            "xo": C(x[b, g * TOK:(g + 1) * TOK]),
            "wq_my": C(wq[:, qcols]),
            "wk_my": C(wkv[:, g * 2048:(g + 1) * 2048]),
            "wv_my": C(wkv[:, 4096 + g * 2048:4096 + (g + 1) * 2048]),
            "wo": wo_, "qT": qT, "kT": kT, "vT": vT, "oT": oT, "fT": fT,
            "gwT": gwT, "ics": icsv, "ocs": ocsv, "fbias": fb,
            "qN": qN, "kN": kN, "vN": vN, "oN": oN,
            "qscg": qscg, "kscg": kscg, "vscg": vscg, "oscg": oscg,
        })

    nc = build()
    r = run_bass_kernel_spmd(nc, in_maps, core_ids=list(range(8)))
    out = np.empty((B, S, D), dtype=np.float32)
    for core in range(8):
        b, g = core // 2, core % 2
        out[b, g * TOK:(g + 1) * TOK] = np.asarray(r.results[core]["y"])
    return out


# version 12
# speedup vs baseline: 1.1781x; 1.0509x over previous
import numpy as np
import ml_dtypes
import concourse.bass as bass
import concourse.mybir as mybir
from concourse.bass_utils import run_bass_kernel_spmd

F32 = mybir.dt.float32
BF16 = mybir.dt.bfloat16
U8 = mybir.dt.uint8
AF = mybir.ActivationFunctionType
ALU = mybir.AluOpType
AX = mybir.AxisListType
bfloat16 = ml_dtypes.bfloat16

B, S, D = 4, 2048, 4096
TOK = 1024
NIC = 32
NTB = 8
SCALE_A = float(192.0 ** -0.5)
SCALE_B = float(128.0 ** -0.5)
NEG_BIG = -1e30
EPS = 1e-5
ENGINES = ("tensor", "vector", "scalar", "gpsimd", "sync")


class Em:

    def __init__(self, nc):
        self.nc = nc
        self.ops = {e: [] for e in ENGINES}
        self.meta = {}
        self.counts = {}
        self.hwm = {e: {} for e in ENGINES}
        self.semh = {}

    def sem(self, name):
        if name not in self.counts:
            self.counts[name] = 0
            self.semh[name] = self.nc.alloc_semaphore(name)
        return self.semh[name]

    OWN = {"tensor": "sPE", "vector": "sDV", "scalar": "sAC",
           "gpsimd": "sGP"}

    def emit(self, eng, fn, waits=(), inc=None, inc_by=1, manual=False):
        own = self.OWN.get(eng)
        if own is not None:
            self.sem(own)
            waits = list(waits) + [(own, self.counts[own])]
            if inc is None:
                inc = own
        w = []
        flat = []
        for t in waits:
            if isinstance(t, list):
                flat.extend(t)
            else:
                flat.append(t)
        for t in flat:
            if t is None:
                continue
            s, v = t
            if v <= 0 or v <= self.hwm[eng].get(s, 0):
                continue
            self.hwm[eng][s] = v
            w.append((s, v))
        tok = None
        if inc is not None:
            self.sem(inc)
            self.counts[inc] += inc_by
            tok = (inc, self.counts[inc])
        self.ops[eng].append((fn, w, None if (inc is None or manual)
                              else (inc, inc_by),
                              getattr(fn, "__code__", None)
                              and fn.__code__.co_firstlineno))
        self.meta.setdefault(eng, []).append(
            (tok, getattr(fn, "__code__", None) and fn.__code__.co_firstlineno))
        return tok


class Slot:
    __slots__ = ("buf", "sem", "uses", "prod_tok", "cons_toks")

    def __init__(self, buf, sem):
        self.buf = buf
        self.sem = sem
        self.uses = 0
        self.prod_tok = None
        self.cons_toks = []


class Ring:
    def __init__(self, em, name, bufs, dma=True, init_cons=None):
        self.em = em
        for i in range(len(bufs)):
            em.sem(f"{name}{i}")
        self.slots = [Slot(b, f"{name}{i}") for i, b in enumerate(bufs)]
        if init_cons:
            for sl in self.slots:
                sl.cons_toks = list(init_cons)
        self.k = 0
        self.dma = dma

    def acquire(self):
        s = self.slots[self.k % len(self.slots)]
        self.k += 1
        return s

    def produce(self, s, eng, fn, extra_waits=(), n_dma=1):
        waits = list(s.cons_toks) + list(extra_waits)
        s.cons_toks = []
        s.uses += 1
        if self.dma and n_dma > 1:
            tok = self.em.emit(eng, fn, waits=waits, inc=s.sem,
                               inc_by=16 * n_dma, manual=True)
        else:
            tok = self.em.emit(eng, fn, waits=waits, inc=s.sem,
                               inc_by=16 if self.dma else 1)
        s.prod_tok = tok
        return tok

    def consume(self, s, eng, fn, extra_waits=(), inc=None, inc_by=1):
        tok = self.em.emit(eng, fn, waits=[s.prod_tok] + list(extra_waits),
                           inc=inc, inc_by=inc_by)
        if tok is not None:
            s.cons_toks.append(tok)
        return tok


def _u16(a):
    return np.ascontiguousarray(np.asarray(a).astype(bfloat16)).view(np.uint16)


def build(phases="ABCD", a_stop=9):
    nc = bass.Bass("TRN2", target_bir_lowering=False)
    em = Em(nc)
    SEMH = em.semh

    def inp(name, shape):
        return nc.dram_tensor(name, list(shape), F32, kind="ExternalInput").ap()

    xf = inp("xf", (S, D))
    xo = inp("xo", (TOK, D))
    wq_my = inp("wq_my", (D, 2048))
    wk_my = inp("wk_my", (D, 2048))
    wv_my = inp("wv_my", (D, 2048))
    wo = inp("wo", (D, D))
    qT = inp("qT", (D, D))
    kT = inp("kT", (D, 1024))
    vT = inp("vT", (D, 1024))
    oT = inp("oT", (D, D))
    fT = inp("fT", (D, D))
    gwT = inp("gwT", (D, 8))
    ics = inp("ics", (8, D))
    ocs = inp("ocs", (8, D))
    fbias = inp("fbias", (1, D))
    qN = inp("qN", (D, D))
    kN = inp("kN", (1024, D))
    vN = inp("vN", (1024, D))
    oN = inp("oN", (D, D))
    qscg = inp("qscg", (128, 32))
    kscg = inp("kscg", (128, 8))
    vscg = inp("vscg", (128, 8))
    oscg = inp("oscg", (128, 32))
    y = nc.dram_tensor("y", [TOK, D], F32, kind="ExternalOutput").ap()

    Qa = nc.dram_tensor("Qa", [2048, S], BF16).ap()
    Ka = nc.dram_tensor("Ka", [2048, S], BF16).ap()
    Va = nc.dram_tensor("Va", [S, 2048], BF16).ap()
    attn_ctr = nc.dram_tensor("attn_ctr", [2048, S], BF16).ap()
    attn_gth = nc.dram_tensor("attn_gth", [4096, S], BF16).ap()
    Qb = nc.dram_tensor("Qb", [D, TOK], BF16).ap()
    Kc = nc.dram_tensor("Kc", [1024, TOK], BF16).ap()
    Vc = nc.dram_tensor("Vc", [TOK, 1024], BF16).ap()
    Kg = nc.dram_tensor("Kg", [2048, TOK], BF16).ap()
    Vg = nc.dram_tensor("Vg", [2048, 1024], BF16).ap()
    Ab = nc.dram_tensor("Ab", [D, TOK], BF16).ap()

    identB_d = nc.inline_tensor(_u16(np.eye(128)), name="identB_c")
    identF_d = nc.inline_tensor(np.eye(128, dtype=np.float32), name="identF_c")
    maskT_d = nc.inline_tensor(
        _u16(np.tril(np.full((128, 128), NEG_BIG, np.float32), -1)),
        name="maskT_c")
    onesr_d = nc.inline_tensor(_u16(np.ones((1, 128))), name="onesr_c")
    onesc_d = nc.inline_tensor(_u16(np.ones((128, 1))), name="onesc_c")

    res_t = nc.alloc_sbuf_tensor("res", [128, NTB * D], BF16)
    big1_t = nc.alloc_sbuf_tensor("big1", [128, 65536], U8)
    scr_t = nc.alloc_sbuf_tensor("scr", [128, 73728], U8)
    identB = nc.alloc_sbuf_tensor("identB", [128, 128], BF16).ap()
    identF = nc.alloc_sbuf_tensor("identF", [128, 128], F32).ap()
    maskT = nc.alloc_sbuf_tensor("maskT", [128, 128], BF16).ap()
    onesr = nc.alloc_sbuf_tensor("onesr", [1, 128], BF16).ap()
    onesc = nc.alloc_sbuf_tensor("onesc", [128, 1], BF16).ap()
    res = res_t.ap()
    big1 = big1_t.ap()
    scr = scr_t.ap()

    def resv(tb):
        return res[:, tb * D:(tb + 1) * D]

    def view(base, off, cols, dt=BF16, parts=128):
        nb = cols * (4 if dt == F32 else 2)
        return base[0:parts, off:off + nb].bitcast(dt)

    BIG = view(big1, 0, 32 * TOK)

    def bigv(ic, c0, c1):
        return BIG[:, ic * TOK + c0:ic * TOK + c1]

    SLOT0, SGN0, STAT0, PH0 = 0, 8192, 12288, 28672
    wring = Ring(em, "w", [view(scr, SLOT0 + i * 1024, 512) for i in range(8)])
    sgnring = Ring(em, "g", [view(scr, SGN0 + i * 1024, 512) for i in range(4)],
                   dma=False)

    so = [STAT0]

    def stat(cols, dt=F32, parts=128):
        nb = cols * (4 if dt == F32 else 2)
        off = so[0]
        so[0] += (nb + 31) // 32 * 32
        assert so[0] <= PH0, "stat overflow"
        return scr[0:parts, off:off + nb].bitcast(dt)

    t_sums = stat(1)
    t_sumsq = stat(1)
    t_m = stat(1)
    t_v1 = stat(1)
    t_v2 = stat(1)
    t_std = stat(1)
    t_r = stat(1)
    t_negmr = stat(1)
    t_mx = stat(1)
    t_mxc = stat(4)
    t_nsx = stat(1)
    t_sumsP = [stat(1), stat(1)]
    t_rc = [stat(1), stat(1)]
    t_row = stat(2048, BF16, parts=1)
    t_srowb = stat(512, BF16, parts=1)
    sg_raw = {"q": stat(32), "k": stat(8), "v": stat(8), "o": stat(32)}
    sg_fin = {"q": stat(32), "k": stat(8), "v": stat(8), "o": stat(32)}
    scg_sb = {"q": stat(32), "k": stat(8), "v": stat(8), "o": stat(32)}
    t_red = stat(1)
    lnt = stat(4096, BF16)

    psA = nc.alloc_psum_tensor("psA", [128, 2048], F32).ap()
    psB = nc.alloc_psum_tensor("psB", [128, 2048], F32).ap()

    def bank(i):
        src = psA if i < 4 else psB
        j = i % 4
        return src[:, j * 512:(j + 1) * 512]

    def fn_consts(g):
        g.dma_start(identB[:], identB_d.ap().bitcast(BF16)).then_inc(SEMH["qC"], 16)
        g.dma_start(identF[:], identF_d.ap()).then_inc(SEMH["qC"], 16)
        g.dma_start(maskT[:], maskT_d.ap().bitcast(BF16)).then_inc(SEMH["qC"], 16)
        g.dma_start(onesr[:], onesr_d.ap().bitcast(BF16)).then_inc(SEMH["qC"], 16)
        g.dma_start(onesc[:], onesc_d.ap().bitcast(BF16)).then_inc(SEMH["qC"], 16)
        return None
    em.sem("qC")
    tok_const = em.emit("gpsimd", fn_consts, inc="qC", inc_by=16 * 5,
                        manual=True)

    em.sem("qRES")
    res_toks = []
    for tb in range(NTB):
        def fn_res(g, tb=tb):
            return g.dma_start(resv(tb), xo[tb * 128:(tb + 1) * 128, :])
        res_toks.append(em.emit("gpsimd", fn_res, inc="qRES", inc_by=16))
    tok_res_all = res_toks[-1]


    class StoreRing:

        def __init__(self, name, n=2):
            self.names = [f"{name}{i}" for i in range(n)]
            for nm in self.names:
                em.sem(nm)

        def store(self, k, fn, waits=()):
            nm = self.names[k % len(self.names)]
            return em.emit("sync", fn, waits=waits, inc=nm, inc_by=16)

        def all_toks(self):
            return [(nm, em.counts[nm]) for nm in self.names]

    def fence():
        return [(nm, em.counts.get(nm, 0))
                for nm in ("sPE", "sDV", "sAC", "sGP")]

    rowsem_n = [0]

    def rowsem():
        rowsem_n[0] += 1
        name = f"qR{rowsem_n[0]}"
        em.sem(name)
        return name

    p0_nat = Ring(em, "p0n", [view(scr, 67584 + i * 1024, 512)
                              for i in range(4)])
    p0_marks = {}

    def p0_gen():
        def s_units(WN, key, nblk):
            grid = sg_raw[key]

            def consume(u):
                sl, ob, piece = u

                def fn_red(e, sl=sl):
                    return e.tensor_reduce(t_red[:], sl.buf, axis=AX.X,
                                           op=ALU.add,
                                           apply_absolute_value=True)
                rt = p0_nat.consume(sl, "vector", fn_red)

                def fn_acc(e, ob=ob, piece=piece, grid=grid):
                    if piece == 0:
                        return e.tensor_copy(grid[:, ob:ob + 1], t_red[:])
                    return e.tensor_tensor(grid[:, ob:ob + 1],
                                           grid[:, ob:ob + 1],
                                           t_red[:], op=ALU.add)
                em.emit("vector", fn_acc, waits=[rt])

            pend = []
            for ob in range(nblk):
                for piece in range(8):
                    sl = p0_nat.acquire()
                    p0_nat.produce(sl, "gpsimd",
                                   (lambda e, WN=WN, ob=ob, piece=piece,
                                    sl=sl: e.dma_start(
                                        sl.buf,
                                        WN[ob * 128:(ob + 1) * 128,
                                           piece * 512:(piece + 1) * 512])))
                    pend.append((sl, ob, piece))
                    if len(pend) >= 3:
                        consume(pend.pop(0))
                    yield 1
            while pend:
                consume(pend.pop(0))

        for key, WN, nblk in (("q", qN, 32), ("k", kN, 8), ("v", vN, 8),
                              ("o", oN, 32)):
            yield from s_units(WN, key, nblk)
        p0_marks["s"] = [("sDV", em.counts["sDV"])]

    _p0 = p0_gen()
    p0_active = [True]

    def tick(n=1):
        for _ in range(n):
            if not p0_active[0]:
                return
            if next(_p0, None) is None:
                p0_active[0] = False
                return

    def p0_drain():
        while p0_active[0]:
            tick()
        rsemg = rowsem()

        def fn_g(g):
            g.dma_start(scg_sb["q"], qscg[:, :]).then_inc(SEMH[rsemg], 16)
            g.dma_start(scg_sb["k"], kscg[:, :]).then_inc(SEMH[rsemg], 16)
            g.dma_start(scg_sb["v"], vscg[:, :]).then_inc(SEMH[rsemg], 16)
            g.dma_start(scg_sb["o"], oscg[:, :]).then_inc(SEMH[rsemg], 16)
            return None
        gt = em.emit("sync", fn_g, inc=rsemg, inc_by=64, manual=True)
        fin = {}
        for key in ("q", "k", "v", "o"):
            t1 = em.emit("vector",
                         (lambda v, key=key: v.tensor_scalar(
                             sg_fin[key][:], sg_raw[key][:], 1.0 / D, 1e-8,
                             op0=ALU.mult, op1=ALU.max)),
                         waits=p0_marks["s"] + [gt])
            fin[key] = em.emit("vector",
                               (lambda v, key=key: v.tensor_tensor(
                                   sg_fin[key][:], sg_fin[key][:],
                                   scg_sb[key][:], op=ALU.mult)),
                               waits=[t1])
            if key == "q":
                fin[key] = em.emit("vector",
                                   (lambda v: v.tensor_scalar(
                                       sg_fin["q"][:], sg_fin["q"][:],
                                       SCALE_B, None, op0=ALU.mult)),
                                   waits=[fin[key]])
        p0_marks["fin"] = [fin[k] for k in fin]
        return p0_marks["s"] + p0_marks["fin"]

    pack_cons = [None, None]
    ln_tp_last = [None]

    def emit_ln_block(src_ap, tbcol, extra_w=(), psum_guard=()):
        tw = [t for t in extra_w if t is not None]
        tok1 = em.emit("vector",
                       lambda v: v.reduce_sum(t_sums[:], src_ap, axis=AX.X),
                       waits=tw)
        tok2 = em.emit("scalar",
                       lambda s: s.activation(lnt[:], src_ap, AF.Square,
                                              accum_out=t_sumsq[:]),
                       waits=tw + [tok1, ln_tp_last[0]])
        s1 = em.emit("vector", lambda v: v.tensor_scalar(
            t_m[:], t_sums[:], 1.0 / D, None, op0=ALU.mult), waits=[tok1])
        s2 = em.emit("vector", lambda v: v.tensor_scalar(
            t_v1[:], t_sumsq[:], 1.0 / D, EPS, op0=ALU.mult, op1=ALU.add),
            waits=[tok2])
        s3 = em.emit("vector", lambda v: v.tensor_tensor(
            t_v2[:], t_m[:], t_m[:], op=ALU.mult), waits=[s1])
        s4 = em.emit("vector", lambda v: v.scalar_tensor_tensor(
            t_v1[:], t_v2[:], -1.0, t_v1[:], op0=ALU.mult, op1=ALU.add),
            waits=[s2, s3])
        s5 = em.emit("scalar", lambda s: s.activation(
            t_std[:], t_v1[:], AF.Sqrt), waits=[s4])
        s6 = em.emit("vector", lambda v: v.reciprocal(t_r[:], t_std[:]),
                     waits=[s5])
        s7 = em.emit("vector", lambda v: v.scalar_tensor_tensor(
            t_negmr[:], t_m[:], -1.0, t_r[:], op0=ALU.mult, op1=ALU.mult),
            waits=[s6])
        tok5 = em.emit("vector", lambda v: v.tensor_scalar(
            lnt[:], src_ap, t_r[:, 0:1], t_negmr[:, 0:1],
            op0=ALU.mult, op1=ALU.add), waits=[s7, tok2])

        last_copy = None
        last_tp = None
        for grp in range(8):
            pk = grp % 2
            dstp = psB[:, pk * 512:(pk + 1) * 512].bitcast(BF16)

            def fn_tp(t, grp=grp, dstp=dstp):
                out = []
                for j in range(4):
                    ic = grp * 4 + j
                    out.append(t.transpose(dstp[:, j * 128:(j + 1) * 128],
                                           lnt[:, ic * 128:(ic + 1) * 128],
                                           identB))
                return out
            pw = [tok5, tok_const, pack_cons[pk]]
            if grp < 2:
                pw += [t for t in psum_guard if t is not None]
            tok_tp = em.emit("tensor", fn_tp, waits=pw, inc_by=4)
            last_tp = tok_tp

            def fn_cp(s, grp=grp, dstp=dstp, tbcol=tbcol):
                d = BIG.rearrange("p (c t) -> p c t", c=NIC)[
                    :, grp * 4:(grp + 1) * 4, tbcol * 128:(tbcol + 1) * 128]
                return s.copy(d, dstp[:, 0:512].rearrange(
                    "p (c t) -> p c t", c=4))
            last_copy = em.emit("scalar", fn_cp, waits=[tok_tp])
            pack_cons[pk] = last_copy
        ln_tp_last[0] = last_tp
        return last_copy

    a_end = res_toks

    def attention_pipe(pb_, PT, rbc_sb, scale, causal, bufs_state):
        st = bufs_state
        st["pend"] = None

        def emit_tail(p):
            qb, span, ex, rc = p["qb"], p["span"], p["ex"], p["rc"]
            qbl = qb % 4

            def fn_rT(t, qbl=qbl, pk2=qb % 2):
                return t.transpose(
                    psB[0:1, 1024 + qbl * 128:1024 + (qbl + 1) * 128],
                    t_rc[pk2][:, 0:1], identF)
            rT = em.emit("tensor", fn_rT, waits=[rc, st["row_free"]],
                         inc="sPE")
            st["rT_prev"][qb % 2] = rT
            st["rows"].append(rT)
            nch = (span + 127) // 128
            pb = pb_[qb % 2]
            last_pt = None
            for p0 in range(0, nch, 4):
                n = min(4, nch - p0)
                pk = (p0 // 4) % 2
                bankc = 0 if pk == 0 else 1536

                def fn_pt(t, p0=p0, n=n, pb=pb, bankc=bankc):
                    return [t.transpose(
                        psB[:, bankc:bankc + 512].bitcast(BF16)[
                            :, j * 128:(j + 1) * 128],
                        pb[:, (p0 + j) * 128:(p0 + j + 1) * 128],
                        identB) for j in range(n)]
                tpk = em.emit("tensor", fn_pt,
                              waits=[ex, st["pt_pack_free"][pk]],
                              inc="sPE", inc_by=n)
                last_pt = tpk

                if pk == 0:
                    def fn_ptc(sc_, p0=p0, n=n, qbl=qbl, bankc=bankc, PT=PT):
                        d = PT.rearrange("p (c t) -> p c t", c=16)[
                            :, p0:p0 + n, qbl * 128:(qbl + 1) * 128]
                        srcv = psB[:, bankc:bankc + 512].bitcast(BF16)[
                            :, 0:n * 128].rearrange("p (c t) -> p c t", c=n)
                        return sc_.copy(d, srcv)
                    ptc = em.emit("scalar", fn_ptc, waits=[tpk], inc="sAC")
                else:
                    def fn_ptc(v_, p0=p0, n=n, qbl=qbl, bankc=bankc, PT=PT):
                        d = PT.rearrange("p (c t) -> p c t", c=16)[
                            :, p0:p0 + n, qbl * 128:(qbl + 1) * 128]
                        srcv = psB[:, bankc:bankc + 512].bitcast(BF16)[
                            :, 0:n * 128].rearrange("p (c t) -> p c t", c=n)
                        return v_.tensor_copy(d, srcv)
                    ptc = em.emit("vector", fn_ptc, waits=[tpk], inc="sDV")
                st["pt_pack_free"][pk] = ptc
                st["pt_tile"].append(ptc)
            st["p_free"][qb % 2] = last_pt

        def emit_tile(te, tile):
            def fn_rowcp(v, tile=tile):
                return v.tensor_copy(
                    t_row[0:1, (tile % 4) * 512:(tile % 4 + 1) * 512],
                    psB[0:1, 1024:1536])
            rowcp = em.emit("vector", fn_rowcp,
                            waits=st["rows"] + [st["trow_free"]], inc="sDV")

            def fn_rbc(t, tile=tile):
                return t.matmul(
                    psB[:, 1024:1536], onesr,
                    t_row[0:1, (tile % 4) * 512:(tile % 4 + 1) * 512],
                    start=True, stop=True)
            rbct = em.emit("tensor", fn_rbc,
                           waits=[rowcp, st["rbc_free"]], inc="sPE")
            st["trow_free"] = rbct
            rbccp = em.emit("vector",
                            (lambda v, rbc_sb=rbc_sb: v.tensor_copy(
                                rbc_sb, psB[:, 1024:1536])),
                            waits=[rbct, st["rbc_free"]], inc="sDV")
            st["row_free"] = rbccp
            nchT = te["nchT"]
            vsl = te["vsl"]

            def fn_av(t, nchT=nchT, vsl=vsl, PT=PT):
                return [t.matmul(
                    psB[:, 512:1024],
                    vsl.buf[:, c * 128:(c + 1) * 128],
                    PT.rearrange("p (c t) -> p c t", c=16)[:, c, :],
                    start=(c == 0), stop=(c == nchT - 1))
                    for c in range(nchT)]
            av = em.emit("tensor", fn_av,
                         waits=[vsl.prod_tok, st["ptzero"], st["av_prev"]]
                         + st["pt_tile"], inc="sPE", inc_by=nchT)
            vsl.cons_toks.append(av)
            avev = te["evac"](av, rbccp, rbc_sb)
            st["av_prev"] = avev
            st["rbc_free"] = avev
            st["rows"] = []
            st["pt_tile"] = []

        def push(task):
            if task.get("zero") is not None:
                tile = task["zero"]

                def fn_z(v, tile=tile, PT=PT):
                    return [v.memset(
                        PT.rearrange("p (c t) -> p c t", c=16)[
                            :, tile * 4 + cc:tile * 4 + cc + 1,
                            0:cc * 128], 0.0) for cc in range(1, 4)]
                st["ptzero"] = em.emit("vector", fn_z,
                                       waits=[st["av_prev"]], inc="sDV",
                                       inc_by=3)
                return
            qb, span = task["qb"], task["span"]
            ksl, qsl, qcol = task["ksl"], task["qsl"], task["qcol"]
            nfull, rem = span // 512, span % 512
            dc = qb // 4

            def fn_sc(t, qb=qb, nfull=nfull, rem=rem, dc=dc,
                      ksl=ksl, qsl=qsl, qcol=qcol, causal=causal):
                lhs = qsl.buf[:, qcol * 128:(qcol + 1) * 128]
                out = []
                for c in range(nfull):
                    out.append(t.matmul(
                        psA[:, c * 512:(c + 1) * 512], lhs,
                        ksl.buf[:, c * 512:(c + 1) * 512],
                        start=True,
                        stop=(not causal) or (c != dc)))
                if rem:
                    out.append(t.matmul(
                        psA[:, nfull * 512:nfull * 512 + rem], lhs,
                        ksl.buf[:, nfull * 512:nfull * 512 + rem],
                        start=True, stop=(nfull != dc)))
                if causal:
                    out.append(t.matmul(psA[:, qb * 128:qb * 128 + 128],
                                        maskT, identB, start=False,
                                        stop=True))
                return out
            n_sc = nfull + (1 if rem else 0) + (1 if causal else 0)
            sc = em.emit("tensor", fn_sc,
                         waits=[ksl.prod_tok, qsl.prod_tok,
                                st["exp_prev"], tok_const]
                         + task.get("extra", []),
                         inc="sPE", inc_by=n_sc)
            ksl.cons_toks.append(sc)
            qsl.cons_toks.append(sc)
            nchunks = nfull + (1 if rem else 0)
            mx_toks = []
            for c in range(nchunks):
                c0, c1 = c * 512, min((c + 1) * 512, span)
                ctok = (sc[0], sc[1] - (n_sc - 1 - c))
                if causal and c == qb // 4:
                    ctok = sc
                mx_toks.append(em.emit(
                    "vector",
                    (lambda v, c=c, c0=c0, c1=c1: v.reduce_max(
                        t_mxc[:, c:c + 1], psA[:, c0:c1], axis=AX.X)),
                    waits=[ctok, st["exp_prev"]]))
            mx = em.emit("vector",
                         (lambda v, nchunks=nchunks: (
                             v.reduce_max(t_nsx[:], t_mxc[:, 0:nchunks],
                                          axis=AX.X, negate=True))),
                         waits=mx_toks + [st["exp_prev"]])
            pb = pb_[qb % 2]

            pk2 = qb % 2

            def fn_exp(s_, pb=pb, span=span, pk2=pk2):
                return s_.activation(pb[:, 0:span], psA[:, 0:span],
                                     AF.Exp, bias=t_nsx[:, 0:1],
                                     accum_out=t_sumsP[pk2][:])
            ex = em.emit("scalar", fn_exp,
                         waits=[mx, st["p_free"][pk2], st["rc_prev"][pk2]],
                         inc="sAC")
            st["exp_prev"] = ex
            rc = em.emit("vector",
                         (lambda v, pk2=pk2: v.reciprocal(
                             t_rc[pk2][:], t_sumsP[pk2][:])),
                         waits=[ex, st["rT_prev"][pk2]], inc="sDV")
            st["rc_prev"][pk2] = rc
            if st["pend"] is not None:
                emit_tail(st["pend"])
                if st["pend"]["tile_end"] is not None:
                    emit_tile(st["pend"]["tile_end"], st["pend"]["tile"])
            st["pend"] = {"qb": qb, "span": span, "ex": ex, "rc": rc,
                          "tile_end": task.get("tile_end"),
                          "tile": task.get("tile", 0)}

        def flush():
            if st["pend"] is not None:
                emit_tail(st["pend"])
                if st["pend"]["tile_end"] is not None:
                    emit_tile(st["pend"]["tile_end"], st["pend"]["tile"])
                st["pend"] = None
        return push, flush


    if "A" in phases:
        xbufs = [view(scr, PH0 + i * 8192, 4096) for i in range(2)]
        qk_stg = [view(scr, PH0 + 16384 + i * 4096, 2048) for i in range(2)]
        v_stg = [view(scr, PH0 + 24576 + i * 1024, 512) for i in range(2)]
        xring = Ring(em, "xa", xbufs)
        stQK = StoreRing("stQK")
        stV = StoreRing("stV")
        stAT = StoreRing("stAT")

        qk_psum_free = [None, None]
        qk_stg_free = [None, None]
        v_bank_free = [None] * 8
        v_stg_free = [None, None]

        for half in range(2):
            ln_last = None
            for tbl in range(8):
                tbg = half * 8 + tbl
                sl = xring.acquire()
                xring.produce(sl, "gpsimd",
                              (lambda g, tbg=tbg, sl=sl: g.dma_start(
                                  sl.buf, xf[tbg * 128:(tbg + 1) * 128, :])))
                guard = ()
                if half == 1 and tbl == 0:
                    guard = qk_psum_free + [t for t in v_bank_free
                                            if t is not None]
                tok = emit_ln_block(sl.buf, tbl, extra_w=[sl.prod_tok],
                                    psum_guard=guard)
                sl.cons_toks.append(tok)
                ln_last = tok
            lnfm_rdy = ln_last

            for W, dstD, qscl in ((wq_my, Qa, SCALE_A), (wk_my, Ka, None)):
                for fp in range(8):
                    pk = fp % 2
                    ps = psA if pk == 0 else psB
                    last_mm = None
                    for ic in range(NIC):
                        sl = wring.acquire()
                        wring.produce(sl, "gpsimd",
                                      (lambda g, W=W, fp=fp, ic=ic, sl=sl:
                                       g.dma_start(sl.buf[:, 0:256],
                                                   W[ic * 128:(ic + 1) * 128,
                                                     fp * 256:(fp + 1) * 256])))
                        pw = [lnfm_rdy]
                        if ic == 0:
                            pw.append(qk_psum_free[pk])
                            pw += [t for t in v_bank_free if t is not None]

                        def fn_mm(t, ic=ic, sl=sl, ps=ps):
                            out = []
                            for fc in range(2):
                                for tt in range(2):
                                    out.append(t.matmul(
                                        ps[:, fc * 1024 + tt * 512:
                                           fc * 1024 + (tt + 1) * 512],
                                        sl.buf[:, fc * 128:(fc + 1) * 128],
                                        bigv(ic, tt * 512, (tt + 1) * 512),
                                        start=(ic == 0), stop=(ic == NIC - 1)))
                            return out
                        last_mm = wring.consume(sl, "tensor", fn_mm,
                                                extra_waits=pw, inc="sPE",
                                                inc_by=4)
                        tick(2)
                    stg = qk_stg[pk]

                    def fn_ev(s, ps=ps, stg=stg, qscl=qscl):
                        if qscl is None:
                            return s.copy(stg, ps[:, 0:2048])
                        return s.activation(stg, ps[:, 0:2048], AF.Copy,
                                            scale=qscl)
                    ev = em.emit("scalar", fn_ev,
                                 waits=[last_mm, qk_stg_free[pk]], inc="sAC")
                    qk_psum_free[pk] = ev

                    def fn_st(sy, dstD=dstD, fp=fp, half=half, stg=stg):
                        return sy.dma_start(
                            dstD.rearrange("(a p) t -> p a t", p=128)[
                                :, 2 * fp:2 * fp + 2,
                                half * 1024:(half + 1) * 1024],
                            stg.rearrange("p (a t) -> p a t", a=2))
                    qk_stg_free[pk] = stQK.store(pk, fn_st, waits=[ev])

            for vc in range(4):
                last_mm = None
                for ic in range(NIC):
                    sl = wring.acquire()
                    wring.produce(sl, "gpsimd",
                                  (lambda g, vc=vc, ic=ic, sl=sl:
                                   g.dma_start(sl.buf,
                                               wv_my[ic * 128:(ic + 1) * 128,
                                                     vc * 512:(vc + 1) * 512])))
                    pw = [lnfm_rdy]
                    if ic == 0:
                        pw += [t for t in v_bank_free if t is not None]
                        pw += qk_psum_free

                    def fn_mm(t, ic=ic, sl=sl):
                        out = []
                        for tb in range(8):
                            out.append(t.matmul(
                                bank(tb), bigv(ic, tb * 128, (tb + 1) * 128),
                                sl.buf, start=(ic == 0),
                                stop=(ic == NIC - 1)))
                        return out
                    last_mm = wring.consume(sl, "tensor", fn_mm,
                                            extra_waits=pw, inc="sPE",
                                            inc_by=8)
                    tick()
                for tb in range(8):
                    pk = tb % 2
                    stg = v_stg[pk]

                    def fn_ev(v, tb=tb, stg=stg):
                        return v.tensor_copy(stg, bank(tb))
                    ev = em.emit("vector", fn_ev,
                                 waits=[last_mm, v_stg_free[pk]], inc="sDV")
                    v_bank_free[tb] = ev

                    def fn_st(sy, tb=tb, vc=vc, half=half, stg=stg):
                        r0 = half * 1024 + tb * 128
                        return sy.dma_start(
                            Va[r0:r0 + 128, vc * 512:(vc + 1) * 512], stg)
                    v_stg_free[pk] = stV.store(pk, fn_st, waits=[ev])

        if a_stop < 2:
            a_end = res_toks
        kb = [view(big1, i * 4096, 2048) for i in range(2)]
        qb_ = [view(big1, 8192 + i * 4096, 2048) for i in range(2)]
        vb = [view(big1, 16384 + i * 4096, 2048) for i in range(2)]
        pb_ = [view(big1, 24576 + i * 4096, 2048) for i in range(2)]
        PT = view(big1, 32768, 8192)
        astg = [view(big1, 49152 + i * 1024, 512) for i in range(2)]
        rbc_sb = view(big1, 51200, 512, F32)
        fA3 = fence()
        kring = Ring(em, "ka", kb, init_cons=fA3)
        qring = Ring(em, "qa", qb_, init_cons=fA3)
        vring = Ring(em, "va", vb, init_cons=fA3)
        all_qkv_l = stQK.all_toks() + stV.all_toks()

        pt_pack_free = [fA3, fA3]
        p_free = [fA3, fA3]
        exp_prev = fA3
        av_prev = fA3
        avst_free = [None, None]
        rbc_free = fA3
        row_free = fA3
        trow_free = fA3
        ptzero = fA3

        st_at = dict(pt_pack_free=[fA3, fA3], p_free=[fA3, fA3],
                     exp_prev=fA3, av_prev=fA3, rbc_free=fA3,
                     row_free=fA3, trow_free=fA3, ptzero=fA3,
                     rT_prev=[fA3, fA3], rc_prev=[fA3, fA3],
                     rows=[], pt_tile=[])
        push_a, flush_a = attention_pipe(pb_, PT, rbc_sb, SCALE_A, True,
                                         st_at)

        def mk_evac_a(h, tile):
            def evac(av, rbccp, rbc_sb):
                stg = astg[tile % 2]

                def fn_avev(v, stg=stg, rbc_sb=rbc_sb):
                    return v.tensor_tensor(stg, psB[:, 512:1024], rbc_sb,
                                           op=ALU.mult)
                avev = em.emit("vector", fn_avev,
                               waits=[av, rbccp, avst_free[tile % 2]],
                               inc="sDV")

                def fn_avst(sy, h=h, tile=tile, stg=stg):
                    return sy.dma_start(
                        attn_ctr[h * 128:(h + 1) * 128,
                                 tile * 512:(tile + 1) * 512], stg)
                avst_free[tile % 2] = stAT.store(tile % 2, fn_avst,
                                                 waits=[avev])
                return avev
            return evac

        for h in range(16 if a_stop >= 2 else 0):
            ksl = kring.acquire()
            kring.produce(ksl, "sync",
                          (lambda sy, h=h, ksl=ksl: sy.dma_start(
                              ksl.buf, Ka[h * 128:(h + 1) * 128, :])),
                          extra_waits=all_qkv_l)
            qsl = qring.acquire()
            qring.produce(qsl, "sync",
                          (lambda sy, h=h, qsl=qsl: sy.dma_start(
                              qsl.buf, Qa[h * 128:(h + 1) * 128, :])),
                          extra_waits=all_qkv_l)
            vsl = vring.acquire()

            def fn_vl(sy, h=h, vsl=vsl):
                for q4 in range(4):
                    sy.dma_start(
                        vsl.buf.rearrange("p (c d) -> p c d", c=16)[
                            :, q4 * 4:(q4 + 1) * 4, :],
                        Va.rearrange("(c p) v -> p c v", p=128)[
                            :, q4 * 4:(q4 + 1) * 4, h * 128:(h + 1) * 128]
                    ).then_inc(SEMH[vsl.sem], 16)
                return None
            vring.produce(vsl, "sync", fn_vl, extra_waits=all_qkv_l, n_dma=4)

            for tile in range(4):
                for qbl in range(4):
                    qb = tile * 4 + qbl
                    t = {"qb": qb, "span": (qb + 1) * 128, "ksl": ksl,
                         "qsl": qsl, "qcol": qb, "tile": tile, "zero": None}
                    if qbl == 3:
                        t["tile_end"] = {"nchT": (tile + 1) * 4, "vsl": vsl,
                                         "evac": mk_evac_a(h, tile)}
                    else:
                        t["tile_end"] = None
                    push_a(t)
                    tick(2)
                    if qbl == 0:
                        push_a({"zero": tile})
        if a_stop >= 2:
            flush_a()

        def fn_cc(g):
            return g.collective_compute(
                "AllGather", ALU.bypass,
                replica_groups=[[0, 1], [2, 3], [4, 5], [6, 7]],
                ins=[attn_ctr.opt()], outs=[attn_gth.opt()])
        cc_tok = None
        if a_stop >= 3:
            cc_tok = em.emit("gpsimd", fn_cc, waits=stAT.all_toks(),
                             inc="sCC")

        em.sem("qAFM")

        def fn_afm(g):
            cid = g.compute_val(nc.partition_id(engines=[mybir.EngineType.Pool]))
            off = g.snap((cid % 2) * 1024, min_val=0, max_val=1024)
            src = attn_gth.rearrange("(c p) t -> p c t", p=128)
            dst = BIG.rearrange("p (c t) -> p c t", c=NIC)
            for j in range(4):
                g.dma_start(dst[:, j * 8:(j + 1) * 8, :],
                            src[:, j * 8:(j + 1) * 8, bass.ds(off, 1024)]
                            ).then_inc(SEMH["qAFM"], 16)
            return None
        afm_tok = None
        if a_stop >= 4:
            afm_tok = em.emit("gpsimd", fn_afm, waits=[cc_tok] + fence(),
                              inc="qAFM", inc_by=16 * 4, manual=True)

        oev = [None] * 8
        oev[0] = fence()
        for oc in range(8 if a_stop >= 5 else 0):
            last_mm = None
            for fc in range(NIC):
                sl = wring.acquire()
                wring.produce(sl, "gpsimd",
                              (lambda g, oc=oc, fc=fc, sl=sl:
                               g.dma_start(sl.buf,
                                           wo[fc * 128:(fc + 1) * 128,
                                              oc * 512:(oc + 1) * 512])))
                pw = [afm_tok, tok_res_all]
                if fc == 0:
                    pw += [t for t in oev if t is not None]

                def fn_mm(t, fc=fc, sl=sl):
                    return [t.matmul(bank(tb),
                                     bigv(fc, tb * 128, (tb + 1) * 128),
                                     sl.buf, start=(fc == 0),
                                     stop=(fc == NIC - 1))
                            for tb in range(8)]
                last_mm = wring.consume(sl, "tensor", fn_mm, extra_waits=pw,
                                        inc="sPE", inc_by=8)
                tick(2)
            for tb in range(8):
                def fn_ev(v, tb=tb, oc=oc):
                    d = resv(tb)[:, oc * 512:(oc + 1) * 512]
                    return v.tensor_tensor(d, bank(tb), d, op=ALU.add)
                oev[tb] = em.emit("vector", fn_ev, waits=[last_mm], inc="sDV")
        a_end = [t for t in oev if t is not None]
        if a_stop < 5:
            a_end = res_toks + ([cc_tok] if cc_tok else
                                ([avst_free[0], avst_free[1]]
                                 if a_stop >= 2 else []))

    out_stg_free = [None, None]
    stOUT = StoreRing("stOUT")

    def emit_ffn(prev_toks, to_output):
        gwt_sb = view(scr, PH0, 256)
        ics_sb = view(scr, PH0 + 512, 4096, parts=8)
        ocs_sb = view(scr, PH0 + 8704, 4096, parts=8)
        bias_bc = view(scr, PH0 + 16896, 4096)
        routT = view(scr, PH0 + 25088, 1024, parts=8)
        routf = view(scr, PH0 + 27136, 8, F32)
        routb = view(scr, PH0 + 27168, 8)
        os_sb = [view(scr, PH0 + 27200 + i * 1024, 512) for i in range(8)]
        tu_sb = [view(scr, PH0 + 35392 + i * 2048, 512, F32)
                 for i in range(2)]
        outst = [view(scr, PH0 + 39488 + i * 2048, 512, F32)
                 for i in range(2)]

        rsem = rowsem()
        fF = fence()

        def fn_rows(g):
            g.dma_start(gwt_sb.rearrange("p (c e) -> p c e", c=NIC),
                        gwT.rearrange("(c p) e -> p c e", p=128)
                        ).then_inc(SEMH[rsem], 16)
            g.dma_start(ics_sb, ics[:, :]).then_inc(SEMH[rsem], 16)
            g.dma_start(ocs_sb, ocs[:, :]).then_inc(SEMH[rsem], 16)
            g.dma_start(bias_bc[0:1, :], fbias[:, :]).then_inc(SEMH[rsem], 16)
            return None
        rows_tok = em.emit("gpsimd", fn_rows, waits=list(prev_toks),
                           inc=rsem, inc_by=64, manual=True)

        bb_prev = fF
        bb_toks = []
        for j in range(8):
            def fn_bb(t, j=j):
                return t.matmul(psB[:, 1024:1536], onesr,
                                bias_bc[0:1, j * 512:(j + 1) * 512],
                                start=True, stop=True)
            t1 = em.emit("tensor", fn_bb, waits=[rows_tok, tok_const, bb_prev],
                         inc="sPE")

            def fn_bbc(s, j=j):
                return s.copy(bias_bc[:, j * 512:(j + 1) * 512],
                                     psB[:, 1024:1536])
            bb_prev = em.emit("scalar", fn_bbc, waits=[t1], inc="sAC")
            bb_toks.append(bb_prev)
        bias_rdy = bb_toks[-1]

        ln_toks = []
        for tb in range(NTB):
            ln_toks.append(emit_ln_block(resv(tb), tb,
                                         extra_w=list(prev_toks)))
        lnfm_rdy = ln_toks[-1]

        rt_prev = fF
        lg_free = fF
        rt_toks = []
        for tb in range(NTB):
            def fn_lg(t, tb=tb):
                return [t.matmul(
                    psB[:, 1024:1032],
                    bigv(ic, tb * 128, (tb + 1) * 128),
                    gwt_sb.rearrange("p (c e) -> p c e", c=NIC)[:, ic, :],
                    start=(ic == 0), stop=(ic == NIC - 1))
                    for ic in range(NIC)]
            lg = em.emit("tensor", fn_lg,
                         waits=[lnfm_rdy, rows_tok, bias_rdy, lg_free],
                         inc="sPE", inc_by=NIC)

            sm0 = em.emit("vector",
                          lambda v: v.reduce_max(t_mx[:], psB[:, 1024:1032],
                                                 axis=AX.X), waits=[lg])
            sm = em.emit("vector",
                         lambda v: v.tensor_scalar(t_nsx[:], t_mx[:], -1.0,
                                                   None, op0=ALU.mult),
                         waits=[sm0])

            def fn_ex(s):
                return s.activation(routf[:, 0:8], psB[:, 1024:1032], AF.Exp,
                                    bias=t_nsx[:, 0:1],
                                    accum_out=t_sumsP[0][:])
            ex = em.emit("scalar", fn_ex, waits=[sm], inc="sAC")
            lg_free = ex

            nr0 = em.emit("vector",
                          lambda v: v.reciprocal(t_rc[0][:], t_sumsP[0][:]),
                          waits=[ex])
            nr = em.emit("vector",
                         lambda v: v.tensor_scalar(routb[:, 0:8],
                                                   routf[:, 0:8],
                                                   t_rc[0][:, 0:1], None,
                                                   op0=ALU.mult),
                         waits=[nr0])

            def fn_rt(t):
                d = psB[:, 1536:2048].bitcast(BF16)[0:8, 0:128]
                return t.transpose(d, routb[:, 0:8], identB)
            rt = em.emit("tensor", fn_rt, waits=[nr, rt_prev], inc="sPE")

            def fn_rtc(s, tb=tb):
                return s.copy(routT[0:8, tb * 128:(tb + 1) * 128],
                              psB[:, 1536:2048].bitcast(BF16)[0:8, 0:128])
            rt_prev = em.emit("scalar", fn_rtc, waits=[rt], inc="sAC")
            rt_toks.append(rt_prev)
        routT_rdy = rt_toks[-1]

        is_free = [fF, fF]
        xs_last = None
        for ic in range(NIC):
            for tt in range(2):
                pk = (ic * 2 + tt) % 2
                ps = psA[:, pk * 512:(pk + 1) * 512]

                def fn_is(t, ic=ic, tt=tt, ps=ps):
                    return t.matmul(ps, ics_sb[:, ic * 128:(ic + 1) * 128],
                                    routT[0:8, tt * 512:(tt + 1) * 512],
                                    start=True, stop=True)
                ist = em.emit("tensor", fn_is,
                              waits=[routT_rdy, is_free[pk]], inc="sPE")

                def fn_xs(v, ic=ic, tt=tt, ps=ps):
                    d = bigv(ic, tt * 512, (tt + 1) * 512)
                    return v.tensor_tensor(d, ps, d, op=ALU.mult)
                xs_last = em.emit("vector", fn_xs, waits=[ist], inc="sDV")
                is_free[pk] = xs_last
        xs_rdy = xs_last

        badd = []
        for tb in range(NTB):
            def fn_ba(g, tb=tb):
                return g.tensor_tensor(resv(tb), resv(tb), bias_bc,
                                       op=ALU.add)
            badd.append(em.emit("gpsimd", fn_ba,
                                waits=[bias_rdy, ln_toks[tb]], inc="sGP"))

        ev_done = [None] * 8
        ev_done[0] = fF
        os_free = [fF] * 8
        tu_free = [fF, fF]
        out_toks = []
        for oc in range(8):
            os_toks = []
            for tb in range(NTB):
                pk = tb % 2
                ps = psB[:, 1024 + pk * 512:1536 + pk * 512]

                def fn_os(t, tb=tb, oc=oc, ps=ps):
                    return t.matmul(ps, routT[0:8, tb * 128:(tb + 1) * 128],
                                    ocs_sb[:, oc * 512:(oc + 1) * 512],
                                    start=True, stop=True)
                pw = [routT_rdy, ev_done[6 + pk], xs_rdy]
                if len(os_toks) >= 2:
                    pw.append(os_toks[-2][1])
                ot = em.emit("tensor", fn_os, waits=pw, inc="sPE")

                def fn_osc(s, tb=tb, ps=ps):
                    return s.copy(os_sb[tb], ps)
                oc_t = em.emit("scalar", fn_osc,
                               waits=[ot, os_free[tb]], inc="sAC")
                os_toks.append((ot, oc_t))
            os_all = os_toks[-1][1]

            last_mm = None
            for ic in range(NIC):
                sl = wring.acquire()
                wring.produce(sl, "gpsimd",
                              (lambda g, oc=oc, ic=ic, sl=sl:
                               g.dma_start(sl.buf,
                                           fT[ic * 128:(ic + 1) * 128,
                                              oc * 512:(oc + 1) * 512])))
                gs = sgnring.acquire()

                def fn_sgn(s_, sl=sl, gs=gs):
                    return s_.activation(gs.buf, sl.buf, AF.Sign)
                sgnring.produce(gs, "scalar", fn_sgn,
                                extra_waits=[sl.prod_tok] + list(prev_toks))
                sl.cons_toks.append(gs.prod_tok)
                pw = [xs_rdy, os_all]
                if ic == 0:
                    pw += [t for t in ev_done if t is not None]

                def fn_mm(t, ic=ic, gs=gs):
                    return [t.matmul(bank(tb),
                                     bigv(ic, tb * 128, (tb + 1) * 128),
                                     gs.buf, start=(ic == 0),
                                     stop=(ic == NIC - 1))
                            for tb in range(8)]
                last_mm = sgnring.consume(gs, "tensor", fn_mm,
                                          extra_waits=pw, inc="sPE",
                                          inc_by=8)
            for tb in range(8):
                pk = tb % 2
                tu = tu_sb[pk]

                def fn_e1(v, tb=tb, tu=tu):
                    return v.tensor_tensor(tu, bank(tb), os_sb[tb],
                                           op=ALU.mult)
                e1 = em.emit("vector", fn_e1,
                             waits=[last_mm, os_toks[tb][1], tu_free[pk]],
                             inc="sDV")
                os_free[tb] = e1
                if not to_output:
                    def fn_e2(g, tb=tb, oc=oc, tu=tu):
                        d = resv(tb)[:, oc * 512:(oc + 1) * 512]
                        return g.tensor_tensor(d, tu, d, op=ALU.add)
                    e2 = em.emit("gpsimd", fn_e2, waits=[e1, badd[tb]],
                                 inc="sGP")
                    tu_free[pk] = e2
                    ev_done[tb] = e2
                else:
                    ost = outst[pk]

                    def fn_e2(g, tb=tb, oc=oc, ost=ost, tu=tu):
                        return g.tensor_tensor(
                            ost, tu, resv(tb)[:, oc * 512:(oc + 1) * 512],
                            op=ALU.add)
                    e2 = em.emit("gpsimd", fn_e2,
                                 waits=[e1, badd[tb], out_stg_free[pk]],
                                 inc="sGP")
                    tu_free[pk] = e2

                    def fn_yst(sy, tb=tb, oc=oc, ost=ost):
                        return sy.dma_start(
                            y[tb * 128:(tb + 1) * 128,
                              oc * 512:(oc + 1) * 512], ost)
                    yt = stOUT.store(pk, fn_yst, waits=[e2])
                    out_stg_free[pk] = yt
                    out_toks.append(yt)
                    ev_done[tb] = e2
        return out_toks if to_output else [t for t in ev_done
                                           if t is not None]

    a_end = list(a_end) + p0_drain()
    if "B" in phases:
        b_end = emit_ffn(a_end, to_output=False)
    else:
        b_end = a_end

    if "C" in phases:
        stKV = StoreRing("stKV")
        stQB = StoreRing("stQB")
        stAB = StoreRing("stAB")
        sgnC = Ring(em, "gc", [view(scr, 55296 + i * 1024, 512)
                               for i in range(4)], dma=False)
        stg2 = [view(scr, SGN0 + i * 2048, 1024) for i in range(2)]
        svbc = view(scr, PH0 + 40960, 512)
        tuv2 = [view(scr, SGN0 + i * 2048, 512, F32)
                for i in range(2)]

        ln_toks = []
        for tb in range(NTB):
            ln_toks.append(emit_ln_block(resv(tb), tb, extra_w=list(b_end)))
        lnfm_rdy = ln_toks[-1]

        fC = fence()
        psum8_guard = list(fC)

        def str_ps(fc):
            src = psA if fc < 2 else psB
            return src[:, (fc % 2) * 1024:(fc % 2 + 1) * 1024]

        stg_free = [fC, fC]

        def grid_to_bcast(key, ob0):
            tg = []
            for j in range(4):
                def fn_tg(t, key=key, ob0=ob0, j=j):
                    return t.transpose(
                        psB[0:1, 1536 + j * 128:1536 + (j + 1) * 128],
                        sg_fin[key][:, ob0 + j:ob0 + j + 1], identF)
                tg.append(em.emit("tensor", fn_tg,
                                  waits=[tok_const] + psum8_guard
                                  + p0_marks["fin"], inc="sPE"))
            cb = em.emit("vector",
                         lambda v: v.tensor_copy(t_srowb[0:1, :],
                                                 psB[0:1, 1536:2048]),
                         waits=tg)
            bm = em.emit("tensor",
                         lambda t: t.matmul(psB[:, 1024:1536], onesr,
                                            t_srowb[0:1, :],
                                            start=True, stop=True),
                         waits=[cb, tok_const] + psum8_guard, inc="sPE")
            cc = em.emit("vector",
                         lambda v: v.tensor_copy(svbc, psB[:, 1024:1536]),
                         waits=[bm])
            psum8_guard.extend([cb, cc])
            return cc

        def feat_strips(SD, nst, key, Dst, stR):
            for st in range(nst):
                last_mm = None
                for ic in range(NIC):
                    sl = wring.acquire()
                    wring.produce(sl, "gpsimd",
                                  (lambda g, SD=SD, st=st, ic=ic, sl=sl:
                                   g.dma_start(sl.buf,
                                               SD[ic * 128:(ic + 1) * 128,
                                                  st * 512:(st + 1) * 512])))
                    gs = sgnC.acquire()

                    def fn_sgn(s_, sl=sl, gs=gs):
                        return s_.activation(gs.buf, sl.buf, AF.Sign)
                    sgnC.produce(gs, "scalar", fn_sgn,
                                 extra_waits=[sl.prod_tok] + list(b_end))
                    sl.cons_toks.append(gs.prod_tok)
                    pw = [lnfm_rdy]
                    if ic == 0:
                        pw = pw + psum8_guard

                    def fn_mm(t, ic=ic, gs=gs):
                        return [t.matmul(
                            str_ps(fc)[:, tt * 512:(tt + 1) * 512],
                            gs.buf[:, fc * 128:(fc + 1) * 128],
                            bigv(ic, tt * 512, (tt + 1) * 512),
                            start=(ic == 0), stop=(ic == NIC - 1))
                            for fc in range(4) for tt in range(2)]
                    last_mm = sgnC.consume(gs, "tensor", fn_mm,
                                              extra_waits=pw, inc="sPE",
                                              inc_by=8)
                for fc in range(4):
                    pk = fc % 2
                    stg = stg2[pk]

                    def fn_ev(e, fc=fc, st=st, key=key, stg=stg):
                        return e.activation(
                            stg, str_ps(fc), AF.Copy,
                            scale=sg_fin[key][:, st * 4 + fc:st * 4 + fc + 1])
                    ev = em.emit("scalar", fn_ev,
                                 waits=[last_mm, stg_free[pk]]
                                 + p0_marks["fin"], inc="sAC")
                    psum8_guard.append(ev)

                    def fn_st(sy, st=st, fc=fc, stg=stg, Dst=Dst):
                        r0 = st * 512 + fc * 128
                        return sy.dma_start(Dst[r0:r0 + 128, :], stg)
                    stg_free[pk] = stR.store(pk, fn_st, waits=[ev])

        feat_strips(kT, 2, "k", Kc, stKV)

        v_bank_free2 = [None] * 4
        vst_free = stg_free
        for st in range(2):
            svc = grid_to_bcast("v", st * 4)
            for tbp in range(2):
                last_mm = None
                for ic in range(NIC):
                    sl = wring.acquire()
                    wring.produce(sl, "gpsimd",
                                  (lambda g, st=st, ic=ic, sl=sl:
                                   g.dma_start(sl.buf,
                                               vT[ic * 128:(ic + 1) * 128,
                                                  st * 512:(st + 1) * 512])))
                    gs = sgnC.acquire()

                    def fn_sgn(s_, sl=sl, gs=gs):
                        return s_.activation(gs.buf, sl.buf, AF.Sign)
                    sgnC.produce(gs, "scalar", fn_sgn,
                                 extra_waits=[sl.prod_tok] + list(b_end))
                    sl.cons_toks.append(gs.prod_tok)
                    pw = [lnfm_rdy]
                    if ic == 0:
                        pw = pw + psum8_guard \
                            + [t for t in v_bank_free2 if t is not None]

                    def fn_mm(t, ic=ic, tbp=tbp, gs=gs):
                        return [t.matmul(
                            psA[:, j * 512:(j + 1) * 512],
                            bigv(ic, (tbp * 4 + j) * 128,
                                 (tbp * 4 + j + 1) * 128),
                            gs.buf,
                            start=(ic == 0), stop=(ic == NIC - 1))
                            for j in range(4)]
                    last_mm = sgnC.consume(gs, "tensor", fn_mm,
                                              extra_waits=pw, inc="sPE",
                                              inc_by=4)
                for j in range(4):
                    tb = tbp * 4 + j
                    pk = tb % 2
                    stg = stg2[pk]

                    def fn_ev(v, j=j, stg=stg):
                        return v.tensor_tensor(stg[:, 0:512],
                                               psA[:, j * 512:(j + 1) * 512],
                                               svbc, op=ALU.mult)
                    ev = em.emit("vector", fn_ev,
                                 waits=[last_mm, svc, vst_free[pk]])
                    v_bank_free2[j] = ev
                    psum8_guard.append(ev)

                    def fn_st(sy, tb=tb, st=st, stg=stg):
                        return sy.dma_start(Vc[tb * 128:(tb + 1) * 128,
                                               st * 512:(st + 1) * 512],
                                            stg[:, 0:512])
                    vst_free[pk] = stKV.store(pk, fn_st, waits=[ev])

        def fn_cc2(g):
            return g.collective_compute(
                "AllGather", ALU.bypass,
                replica_groups=[[0, 1], [2, 3], [4, 5], [6, 7]],
                ins=[Kc.opt()], outs=[Kg.opt()])
        cck = em.emit("gpsimd", fn_cc2, waits=stKV.all_toks(),
                      inc="sCC")

        def fn_cc3(g):
            return g.collective_compute(
                "AllGather", ALU.bypass,
                replica_groups=[[0, 1], [2, 3], [4, 5], [6, 7]],
                ins=[Vc.opt()], outs=[Vg.opt()])
        cc2 = em.emit("gpsimd", fn_cc3, waits=stKV.all_toks(), inc="sCC")

        feat_strips(qT, 8, "q", Qb, stQB)

        kb = [view(big1, i * 4096, 2048) for i in range(2)]
        vb = [view(big1, 8192 + i * 4096, 2048) for i in range(2)]
        qb2 = [view(big1, 16384 + i * 2048, 1024) for i in range(2)]
        pb_ = [view(big1, 20480 + i * 4096, 2048) for i in range(2)]
        PT = view(big1, 28672, 8192)
        astg = [view(big1, 45056 + i * 1024, 512) for i in range(2)]
        rbc_sb = view(big1, 47104, 512, F32)
        fC4 = fence()
        kringb = Ring(em, "kb", kb, init_cons=fC4)
        vringb = Ring(em, "vb", vb, init_cons=fC4)
        qringb = Ring(em, "qb", qb2, init_cons=fC4)
        all_qb_l = stQB.all_toks()

        pt_pack_free = [fC4, fC4]
        p_free = [fC4, fC4]
        exp_prev = fC4
        av_prev = fC4
        ab_stg_free = [None, None]
        rbc_free = fC4
        row_free = fC4
        trow_free = fC4

        st_cb = dict(pt_pack_free=[fC4, fC4], p_free=[fC4, fC4],
                     exp_prev=fC4, av_prev=fC4, rbc_free=fC4,
                     row_free=fC4, trow_free=fC4, ptzero=fC4,
                     rT_prev=[fC4, fC4], rc_prev=[fC4, fC4],
                     rows=[], pt_tile=[])
        push_b, flush_b = attention_pipe(pb_, PT, rbc_sb, SCALE_B, False,
                                         st_cb)

        ab_stg_free = [None, None]

        def mk_evac_b(hq, tile):
            def evac(av, rbccp, rbc_sb):
                stg = astg[tile % 2]

                def fn_avev(v, stg=stg, rbc_sb=rbc_sb):
                    return v.tensor_tensor(stg, psB[:, 512:1024], rbc_sb,
                                           op=ALU.mult)
                avev = em.emit("vector", fn_avev,
                               waits=[av, rbccp, ab_stg_free[tile % 2]],
                               inc="sDV")

                def fn_ast(sy, hq=hq, tile=tile, stg=stg):
                    return sy.dma_start(
                        Ab[hq * 128:(hq + 1) * 128,
                           tile * 512:(tile + 1) * 512], stg)
                ab_stg_free[tile % 2] = stAB.store(tile % 2, fn_ast,
                                                   waits=[avev])
                return avev
            return evac

        for kvh in range(8):
            ksl = kringb.acquire()

            def fn_kl(sy, kvh=kvh, ksl=ksl):
                sy.dma_start(ksl.buf[:, 0:1024],
                             Kg[kvh * 128:(kvh + 1) * 128, :]
                             ).then_inc(SEMH[ksl.sem], 16)
                sy.dma_start(ksl.buf[:, 1024:2048],
                             Kg[1024 + kvh * 128:1024 + (kvh + 1) * 128, :]
                             ).then_inc(SEMH[ksl.sem], 16)
                return None
            kringb.produce(ksl, "sync", fn_kl, extra_waits=[cck], n_dma=2)
            vsl = vringb.acquire()

            def fn_vl(sy, kvh=kvh, vsl=vsl):
                for q4 in range(4):
                    sy.dma_start(
                        vsl.buf.rearrange("p (c d) -> p c d", c=16)[
                            :, q4 * 4:(q4 + 1) * 4, :],
                        Vg.rearrange("(c p) v -> p c v", p=128)[
                            :, q4 * 4:(q4 + 1) * 4,
                            kvh * 128:(kvh + 1) * 128]
                    ).then_inc(SEMH[vsl.sem], 16)
                return None
            vringb.produce(vsl, "sync", fn_vl, extra_waits=[cc2], n_dma=4)

            for qh in range(4):
                hq = kvh * 4 + qh
                qsl = qringb.acquire()
                qringb.produce(qsl, "sync",
                               (lambda sy, hq=hq, qsl=qsl: sy.dma_start(
                                   qsl.buf, Qb[hq * 128:(hq + 1) * 128, :])),
                               extra_waits=all_qb_l)
                for tile in range(2):
                    for qbl in range(4):
                        qb = tile * 4 + qbl
                        t = {"qb": qb, "span": 2048, "ksl": ksl,
                             "qsl": qsl, "qcol": qb, "tile": tile,
                             "zero": None, "tile_end": None}
                        if qbl == 3:
                            t["tile_end"] = {"nchT": 16, "vsl": vsl,
                                             "evac": mk_evac_b(hq, tile)}
                        push_b(t)
        flush_b()

        em.sem("qABM")

        def fn_abm(sy):
            for fc in range(NIC):
                sy.dma_start(BIG[:, fc * TOK:(fc + 1) * TOK],
                             Ab[fc * 128:(fc + 1) * 128, :]
                             ).then_inc(SEMH["qABM"], 16)
            return None
        abm_tok = em.emit("sync", fn_abm, waits=stAB.all_toks(),
                          inc="qABM", inc_by=16 * NIC, manual=True)

        fC5 = fence()
        psum8_guard.extend(list(fC5))
        oev = [None] * 8
        oev[0] = fC5
        o_bank_free = [None] * 4
        tuv2_init = stQB.all_toks() + stKV.all_toks() + list(fC5)
        tu_free2 = [tuv2_init, tuv2_init]
        for oc in range(8):
            svc = grid_to_bcast("o", oc * 4)
            for tbp in range(2):
                last_mm = None
                for ic in range(NIC):
                    sl = wring.acquire()
                    wring.produce(sl, "gpsimd",
                                  (lambda g, oc=oc, ic=ic, sl=sl:
                                   g.dma_start(sl.buf,
                                               oT[ic * 128:(ic + 1) * 128,
                                                  oc * 512:(oc + 1) * 512])))
                    gs = sgnC.acquire()

                    def fn_sgn(s_, sl=sl, gs=gs):
                        return s_.activation(gs.buf, sl.buf, AF.Sign)
                    sgnC.produce(gs, "scalar", fn_sgn,
                                 extra_waits=[sl.prod_tok] + list(b_end))
                    sl.cons_toks.append(gs.prod_tok)
                    pw = [abm_tok]
                    if ic == 0:
                        pw = pw + psum8_guard \
                            + [t for t in o_bank_free if t is not None]

                    def fn_mm(t, ic=ic, tbp=tbp, gs=gs):
                        return [t.matmul(
                            psA[:, j * 512:(j + 1) * 512],
                            bigv(ic, (tbp * 4 + j) * 128,
                                 (tbp * 4 + j + 1) * 128),
                            gs.buf,
                            start=(ic == 0), stop=(ic == NIC - 1))
                            for j in range(4)]
                    last_mm = sgnC.consume(gs, "tensor", fn_mm,
                                              extra_waits=pw, inc="sPE",
                                              inc_by=4)
                for j in range(4):
                    tb = tbp * 4 + j
                    pk = tb % 2
                    tu = tuv2[pk]

                    def fn_e1(v, j=j, tu=tu):
                        return v.tensor_tensor(tu,
                                               psA[:, j * 512:(j + 1) * 512],
                                               svbc, op=ALU.mult)
                    e1 = em.emit("vector", fn_e1,
                                 waits=[last_mm, svc, tu_free2[pk]])
                    o_bank_free[j] = e1
                    psum8_guard.append(e1)

                    def fn_e2(g, tb=tb, oc=oc, tu=tu):
                        d = resv(tb)[:, oc * 512:(oc + 1) * 512]
                        return g.tensor_tensor(d, tu, d, op=ALU.add)
                    e2 = em.emit("gpsimd", fn_e2, waits=[e1])
                    tu_free2[pk] = [e2]
                    oev[tb] = e2
        c_end = [t for t in oev if t is not None]
    else:
        c_end = b_end

    if "D" in phases:
        emit_ffn(c_end, to_output=True)
        final_l = stOUT.all_toks()
    else:
        em.sem("qDUMP")
        def fn_dump(g):
            for tb in range(NTB):
                g.dma_start(y[tb * 128:(tb + 1) * 128, :],
                            resv(tb)).then_inc(SEMH["qDUMP"], 16)
            return None
        em.emit("gpsimd", fn_dump, waits=list(c_end) + [tok_res_all],
                inc="qDUMP", inc_by=16 * NTB, manual=True)
        final_l = [("qDUMP", em.counts["qDUMP"])]

    em.emit("sync", lambda sy: None, waits=final_l)

    with nc.Block() as blk:
        def runner(eng_name):
            def go(eng):
                for idx, (fn, waits, inc, lineno) in enumerate(em.ops[eng_name]):
                    try:
                        for s, v in waits:
                            eng.wait_ge(SEMH[s], v)
                        last = fn(eng)
                        for ins_ in (last if isinstance(last, list)
                                     else ([last] if last is not None else [])):
                            try:
                                ins_.annotate(f"{eng_name}:{idx}:L{lineno}")
                            except Exception:
                                pass
                        if inc is not None:
                            name, by = inc
                            if isinstance(last, list):
                                assert len(last) == by, \
                                    f"inc_by {by} != {len(last)} insts"
                                for ins_ in last:
                                    ins_.then_inc(SEMH[name], 1)
                            else:
                                assert last is not None, \
                                    f"no inst for inc {name}"
                                last.then_inc(SEMH[name], by)
                    except Exception as ex:
                        import traceback
                        co = fn.__code__
                        raise RuntimeError(
                            f"step {idx} on {eng_name} "
                            f"({co.co_filename}:{co.co_firstlineno}): {ex}"
                        ) from ex
            return go
        blk.tensor(runner("tensor"))
        blk.vector(runner("vector"))
        blk.scalar(runner("scalar"))
        blk.gpsimd(runner("gpsimd"))
        blk.sync(runner("sync"))
    nc._em = em
    return nc


def kernel(**inputs):
    x = np.asarray(inputs["L0_attn_a_hidden_states"], dtype=np.float32)
    wq = np.asarray(inputs["L0_attn_a_wq"], dtype=np.float32)
    wkv = np.asarray(inputs["L0_attn_a_wkv"], dtype=np.float32)
    wo_ = np.asarray(inputs["L0_attn_a_wo"], dtype=np.float32)
    C = np.ascontiguousarray
    qT = C(np.asarray(inputs["L0_attn_b_q_weight"], dtype=np.float32).T)
    kT = C(np.asarray(inputs["L0_attn_b_k_weight"], dtype=np.float32).T)
    vT = C(np.asarray(inputs["L0_attn_b_v_weight"], dtype=np.float32).T)
    oT = C(np.asarray(inputs["L0_attn_b_o_weight"], dtype=np.float32).T)
    fT = C(np.asarray(inputs["L0_ffn_weight"], dtype=np.float32).T)
    gwT = C(np.asarray(inputs["L0_ffn_gate_w"], dtype=np.float32).T)
    icsv = np.asarray(inputs["L0_ffn_in_channel_scale"], dtype=np.float32)
    ocsv = np.asarray(inputs["L0_ffn_out_channel_scale"], dtype=np.float32)
    fb = np.asarray(inputs["L0_ffn_bias"], dtype=np.float32).reshape(1, -1)
    qN = C(np.asarray(inputs["L0_attn_b_q_weight"], dtype=np.float32))
    kN = C(np.asarray(inputs["L0_attn_b_k_weight"], dtype=np.float32))
    vN = C(np.asarray(inputs["L0_attn_b_v_weight"], dtype=np.float32))
    oN = C(np.asarray(inputs["L0_attn_b_o_weight"], dtype=np.float32))
    qscg = C(np.asarray(inputs["L0_attn_b_q_scale"],
                        dtype=np.float32).reshape(32, 128).T)
    kscg = C(np.asarray(inputs["L0_attn_b_k_scale"],
                        dtype=np.float32).reshape(8, 128).T)
    vscg = C(np.asarray(inputs["L0_attn_b_v_scale"],
                        dtype=np.float32).reshape(8, 128).T)
    oscg = C(np.asarray(inputs["L0_attn_b_o_scale"],
                        dtype=np.float32).reshape(32, 128).T)

    in_maps = []
    for core in range(8):
        b, g = core // 2, core % 2
        qcols = np.concatenate([np.arange(h * 192, h * 192 + 128)
                                for h in range(g * 16, (g + 1) * 16)])
        in_maps.append({
            "xf": C(x[b]),
            "xo": C(x[b, g * TOK:(g + 1) * TOK]),
            "wq_my": C(wq[:, qcols]),
            "wk_my": C(wkv[:, g * 2048:(g + 1) * 2048]),
            "wv_my": C(wkv[:, 4096 + g * 2048:4096 + (g + 1) * 2048]),
            "wo": wo_, "qT": qT, "kT": kT, "vT": vT, "oT": oT, "fT": fT,
            "gwT": gwT, "ics": icsv, "ocs": ocsv, "fbias": fb,
            "qN": qN, "kN": kN, "vN": vN, "oN": oN,
            "qscg": qscg, "kscg": kscg, "vscg": vscg, "oscg": oscg,
        })

    nc = build()
    r = run_bass_kernel_spmd(nc, in_maps, core_ids=list(range(8)))
    out = np.empty((B, S, D), dtype=np.float32)
    for core in range(8):
        b, g = core // 2, core % 2
        out[b, g * TOK:(g + 1) * TOK] = np.asarray(r.results[core]["y"])
    return out


# version 14 (final)
# speedup vs baseline: 1.1902x; 1.0102x over previous
import numpy as np
import ml_dtypes
import concourse.bass as bass
import concourse.mybir as mybir
from concourse.bass_utils import run_bass_kernel_spmd

F32 = mybir.dt.float32
BF16 = mybir.dt.bfloat16
U8 = mybir.dt.uint8
AF = mybir.ActivationFunctionType
ALU = mybir.AluOpType
AX = mybir.AxisListType
bfloat16 = ml_dtypes.bfloat16

B, S, D = 4, 2048, 4096
TOK = 1024
NIC = 32
NTB = 8
SCALE_A = float(192.0 ** -0.5)
SCALE_B = float(128.0 ** -0.5)
NEG_BIG = -1e30
EPS = 1e-5
ENGINES = ("tensor", "vector", "scalar", "gpsimd", "sync")


class Em:

    def __init__(self, nc):
        self.nc = nc
        self.ops = {e: [] for e in ENGINES}
        self.meta = {}
        self.counts = {}
        self.hwm = {e: {} for e in ENGINES}
        self.semh = {}

    def sem(self, name):
        if name not in self.counts:
            self.counts[name] = 0
            self.semh[name] = self.nc.alloc_semaphore(name)
        return self.semh[name]

    OWN = {"tensor": "sPE", "vector": "sDV", "scalar": "sAC",
           "gpsimd": "sGP"}

    def emit(self, eng, fn, waits=(), inc=None, inc_by=1, manual=False):
        own = self.OWN.get(eng)
        if own is not None:
            self.sem(own)
            waits = list(waits) + [(own, self.counts[own])]
            if inc is None:
                inc = own
        w = []
        flat = []
        for t in waits:
            if isinstance(t, list):
                flat.extend(t)
            else:
                flat.append(t)
        for t in flat:
            if t is None:
                continue
            s, v = t
            if v <= 0 or v <= self.hwm[eng].get(s, 0):
                continue
            self.hwm[eng][s] = v
            w.append((s, v))
        tok = None
        if inc is not None:
            self.sem(inc)
            self.counts[inc] += inc_by
            tok = (inc, self.counts[inc])
        self.ops[eng].append((fn, w, None if (inc is None or manual)
                              else (inc, inc_by),
                              getattr(fn, "__code__", None)
                              and fn.__code__.co_firstlineno))
        self.meta.setdefault(eng, []).append(
            (tok, getattr(fn, "__code__", None) and fn.__code__.co_firstlineno))
        return tok


class Slot:
    __slots__ = ("buf", "sem", "uses", "prod_tok", "cons_toks")

    def __init__(self, buf, sem):
        self.buf = buf
        self.sem = sem
        self.uses = 0
        self.prod_tok = None
        self.cons_toks = []


class Ring:
    def __init__(self, em, name, bufs, dma=True, init_cons=None):
        self.em = em
        for i in range(len(bufs)):
            em.sem(f"{name}{i}")
        self.slots = [Slot(b, f"{name}{i}") for i, b in enumerate(bufs)]
        if init_cons:
            for sl in self.slots:
                sl.cons_toks = list(init_cons)
        self.k = 0
        self.dma = dma

    def acquire(self):
        s = self.slots[self.k % len(self.slots)]
        self.k += 1
        return s

    def produce(self, s, eng, fn, extra_waits=(), n_dma=1):
        waits = list(s.cons_toks) + list(extra_waits)
        s.cons_toks = []
        s.uses += 1
        if self.dma and n_dma > 1:
            tok = self.em.emit(eng, fn, waits=waits, inc=s.sem,
                               inc_by=16 * n_dma, manual=True)
        else:
            tok = self.em.emit(eng, fn, waits=waits, inc=s.sem,
                               inc_by=16 if self.dma else 1)
        s.prod_tok = tok
        return tok

    def consume(self, s, eng, fn, extra_waits=(), inc=None, inc_by=1):
        tok = self.em.emit(eng, fn, waits=[s.prod_tok] + list(extra_waits),
                           inc=inc, inc_by=inc_by)
        if tok is not None:
            s.cons_toks.append(tok)
        return tok


def _u16(a):
    return np.ascontiguousarray(np.asarray(a).astype(bfloat16)).view(np.uint16)


def build(phases="ABCD", a_stop=9):
    nc = bass.Bass("TRN2", target_bir_lowering=False)
    em = Em(nc)
    SEMH = em.semh

    def inp(name, shape):
        return nc.dram_tensor(name, list(shape), F32, kind="ExternalInput").ap()

    xf = inp("xf", (S, D))
    xo = inp("xo", (TOK, D))
    wq_my = inp("wq_my", (D, 2048))
    wk_my = inp("wk_my", (D, 2048))
    wv_my = inp("wv_my", (D, 2048))
    wo = inp("wo", (D, D))
    qT = inp("qT", (D, D))
    kT = inp("kT", (D, 1024))
    vT = inp("vT", (D, 1024))
    oT = inp("oT", (D, D))
    fT = inp("fT", (D, D))
    gwT = inp("gwT", (D, 8))
    ics = inp("ics", (8, D))
    ocs = inp("ocs", (8, D))
    fbias = inp("fbias", (1, D))
    qN = inp("qN", (D, D))
    kN = inp("kN", (1024, D))
    vN = inp("vN", (1024, D))
    oN = inp("oN", (D, D))
    qscg = inp("qscg", (128, 32))
    kscg = inp("kscg", (128, 8))
    vscg = inp("vscg", (128, 8))
    oscg = inp("oscg", (128, 32))
    y = nc.dram_tensor("y", [TOK, D], F32, kind="ExternalOutput").ap()

    Qa = nc.dram_tensor("Qa", [2048, S], BF16).ap()
    Ka = nc.dram_tensor("Ka", [2048, S], BF16).ap()
    Va = nc.dram_tensor("Va", [S, 2048], BF16).ap()
    attn_ctr = nc.dram_tensor("attn_ctr", [2048, S], BF16).ap()
    attn_gth = nc.dram_tensor("attn_gth", [4096, S], BF16).ap()
    Qb = nc.dram_tensor("Qb", [D, TOK], BF16).ap()
    Kc = nc.dram_tensor("Kc", [1024, TOK], BF16).ap()
    Vc = nc.dram_tensor("Vc", [TOK, 1024], BF16).ap()
    Kg = nc.dram_tensor("Kg", [2048, TOK], BF16).ap()
    Vg = nc.dram_tensor("Vg", [2048, 1024], BF16).ap()
    Ab = nc.dram_tensor("Ab", [D, TOK], BF16).ap()

    identB_d = nc.inline_tensor(_u16(np.eye(128)), name="identB_c")
    identF_d = nc.inline_tensor(np.eye(128, dtype=np.float32), name="identF_c")
    maskT_d = nc.inline_tensor(
        _u16(np.tril(np.full((128, 128), NEG_BIG, np.float32), -1)),
        name="maskT_c")
    onesr_d = nc.inline_tensor(_u16(np.ones((1, 128))), name="onesr_c")
    onesc_d = nc.inline_tensor(_u16(np.ones((128, 1))), name="onesc_c")

    res_t = nc.alloc_sbuf_tensor("res", [128, NTB * D], BF16)
    big1_t = nc.alloc_sbuf_tensor("big1", [128, 65536], U8)
    scr_t = nc.alloc_sbuf_tensor("scr", [128, 73728], U8)
    identB = nc.alloc_sbuf_tensor("identB", [128, 128], BF16).ap()
    identF = nc.alloc_sbuf_tensor("identF", [128, 128], F32).ap()
    maskT = nc.alloc_sbuf_tensor("maskT", [128, 128], BF16).ap()
    onesr = nc.alloc_sbuf_tensor("onesr", [1, 128], BF16).ap()
    onesc = nc.alloc_sbuf_tensor("onesc", [128, 1], BF16).ap()
    res = res_t.ap()
    big1 = big1_t.ap()
    scr = scr_t.ap()

    def resv(tb):
        return res[:, tb * D:(tb + 1) * D]

    def view(base, off, cols, dt=BF16, parts=128):
        nb = cols * (4 if dt == F32 else 2)
        return base[0:parts, off:off + nb].bitcast(dt)

    BIG = view(big1, 0, 32 * TOK)

    def bigv(ic, c0, c1):
        return BIG[:, ic * TOK + c0:ic * TOK + c1]

    SLOT0, SGN0, STAT0, PH0 = 0, 8192, 12288, 28672
    wring = Ring(em, "w", [view(scr, SLOT0 + i * 1024, 512) for i in range(8)])
    sgnring = Ring(em, "g", [view(scr, SGN0 + i * 1024, 512) for i in range(4)],
                   dma=False)

    so = [STAT0]

    def stat(cols, dt=F32, parts=128):
        nb = cols * (4 if dt == F32 else 2)
        off = so[0]
        so[0] += (nb + 31) // 32 * 32
        assert so[0] <= PH0, "stat overflow"
        return scr[0:parts, off:off + nb].bitcast(dt)

    t_sums = stat(1)
    t_sumsq = stat(1)
    t_m = stat(1)
    t_v1 = stat(1)
    t_v2 = stat(1)
    t_std = stat(1)
    t_r = stat(1)
    t_negmr = stat(1)
    t_mx = stat(1)
    t_mxc = stat(4)
    t_nsx = stat(1)
    t_sumsP = [stat(1), stat(1)]
    t_rc = [stat(1), stat(1)]
    t_row = stat(2048, BF16, parts=1)
    t_srowb = stat(512, BF16, parts=1)
    sg_raw = {"q": stat(32), "k": stat(8), "v": stat(8), "o": stat(32)}
    sg_fin = {"q": stat(32), "k": stat(8), "v": stat(8), "o": stat(32)}
    scg_sb = {"q": stat(32), "k": stat(8), "v": stat(8), "o": stat(32)}
    t_red = stat(1)
    lnt = stat(4096, BF16)

    psA = nc.alloc_psum_tensor("psA", [128, 2048], F32).ap()
    psB = nc.alloc_psum_tensor("psB", [128, 2048], F32).ap()

    def bank(i):
        src = psA if i < 4 else psB
        j = i % 4
        return src[:, j * 512:(j + 1) * 512]

    def fn_consts(g):
        g.dma_start(identB[:], identB_d.ap().bitcast(BF16)).then_inc(SEMH["qC"], 16)
        g.dma_start(identF[:], identF_d.ap()).then_inc(SEMH["qC"], 16)
        g.dma_start(maskT[:], maskT_d.ap().bitcast(BF16)).then_inc(SEMH["qC"], 16)
        g.dma_start(onesr[:], onesr_d.ap().bitcast(BF16)).then_inc(SEMH["qC"], 16)
        g.dma_start(onesc[:], onesc_d.ap().bitcast(BF16)).then_inc(SEMH["qC"], 16)
        return None
    em.sem("qC")
    tok_const = em.emit("gpsimd", fn_consts, inc="qC", inc_by=16 * 5,
                        manual=True)

    em.sem("qRES")
    res_toks = []
    for tb in range(NTB):
        def fn_res(g, tb=tb):
            return g.dma_start(resv(tb), xo[tb * 128:(tb + 1) * 128, :])
        res_toks.append(em.emit("gpsimd", fn_res, inc="qRES", inc_by=16))
    tok_res_all = res_toks[-1]


    class StoreRing:

        def __init__(self, name, n=2):
            self.names = [f"{name}{i}" for i in range(n)]
            for nm in self.names:
                em.sem(nm)

        def store(self, k, fn, waits=()):
            nm = self.names[k % len(self.names)]
            return em.emit("sync", fn, waits=waits, inc=nm, inc_by=16)

        def all_toks(self):
            return [(nm, em.counts[nm]) for nm in self.names]

    def fence():
        return [(nm, em.counts.get(nm, 0))
                for nm in ("sPE", "sDV", "sAC", "sGP")]

    rowsem_n = [0]

    def rowsem():
        rowsem_n[0] += 1
        name = f"qR{rowsem_n[0]}"
        em.sem(name)
        return name

    p0_nat = Ring(em, "p0n", [view(scr, 67584 + i * 1024, 512)
                              for i in range(4)])
    p0_marks = {}

    def p0_gen():
        def s_units(WN, key, nblk):
            grid = sg_raw[key]

            def consume(u):
                sl, ob, piece = u

                def fn_red(e, sl=sl):
                    return e.tensor_reduce(t_red[:], sl.buf, axis=AX.X,
                                           op=ALU.add,
                                           apply_absolute_value=True)
                rt = p0_nat.consume(sl, "vector", fn_red)

                def fn_acc(e, ob=ob, piece=piece, grid=grid):
                    if piece == 0:
                        return e.tensor_copy(grid[:, ob:ob + 1], t_red[:])
                    return e.tensor_tensor(grid[:, ob:ob + 1],
                                           grid[:, ob:ob + 1],
                                           t_red[:], op=ALU.add)
                em.emit("vector", fn_acc, waits=[rt])

            pend = []
            for ob in range(nblk):
                for piece in range(8):
                    sl = p0_nat.acquire()
                    p0_nat.produce(sl, "gpsimd",
                                   (lambda e, WN=WN, ob=ob, piece=piece,
                                    sl=sl: e.dma_start(
                                        sl.buf,
                                        WN[ob * 128:(ob + 1) * 128,
                                           piece * 512:(piece + 1) * 512])))
                    pend.append((sl, ob, piece))
                    if len(pend) >= 3:
                        consume(pend.pop(0))
                    yield 1
            while pend:
                consume(pend.pop(0))

        for key, WN, nblk in (("q", qN, 32), ("k", kN, 8), ("v", vN, 8),
                              ("o", oN, 32)):
            yield from s_units(WN, key, nblk)
        p0_marks["s"] = [("sDV", em.counts["sDV"])]

    _p0 = p0_gen()
    p0_active = [True]

    def tick(n=1):
        for _ in range(n):
            if not p0_active[0]:
                return
            if next(_p0, None) is None:
                p0_active[0] = False
                return

    def p0_drain():
        while p0_active[0]:
            tick()
        rsemg = rowsem()

        def fn_g(g):
            g.dma_start(scg_sb["q"], qscg[:, :]).then_inc(SEMH[rsemg], 16)
            g.dma_start(scg_sb["k"], kscg[:, :]).then_inc(SEMH[rsemg], 16)
            g.dma_start(scg_sb["v"], vscg[:, :]).then_inc(SEMH[rsemg], 16)
            g.dma_start(scg_sb["o"], oscg[:, :]).then_inc(SEMH[rsemg], 16)
            return None
        gt = em.emit("sync", fn_g, inc=rsemg, inc_by=64, manual=True)
        fin = {}
        for key in ("q", "k", "v", "o"):
            t1 = em.emit("vector",
                         (lambda v, key=key: v.tensor_scalar(
                             sg_fin[key][:], sg_raw[key][:], 1.0 / D, 1e-8,
                             op0=ALU.mult, op1=ALU.max)),
                         waits=p0_marks["s"] + [gt])
            fin[key] = em.emit("vector",
                               (lambda v, key=key: v.tensor_tensor(
                                   sg_fin[key][:], sg_fin[key][:],
                                   scg_sb[key][:], op=ALU.mult)),
                               waits=[t1])
            if key == "q":
                fin[key] = em.emit("vector",
                                   (lambda v: v.tensor_scalar(
                                       sg_fin["q"][:], sg_fin["q"][:],
                                       SCALE_B, None, op0=ALU.mult)),
                                   waits=[fin[key]])
        p0_marks["fin"] = [fin[k] for k in fin]
        return p0_marks["s"] + p0_marks["fin"]

    pack_cons = [None, None]
    ln_tp_last = [None]

    def emit_ln_block(src_ap, tbcol, extra_w=(), psum_guard=()):
        tw = [t for t in extra_w if t is not None]
        tok1 = em.emit("vector",
                       lambda v: v.reduce_sum(t_sums[:], src_ap, axis=AX.X),
                       waits=tw)
        tok2 = em.emit("scalar",
                       lambda s: s.activation(lnt[:], src_ap, AF.Square,
                                              accum_out=t_sumsq[:]),
                       waits=tw + [tok1, ln_tp_last[0]])
        s1 = em.emit("vector", lambda v: v.tensor_scalar(
            t_m[:], t_sums[:], 1.0 / D, None, op0=ALU.mult), waits=[tok1])
        s2 = em.emit("vector", lambda v: v.tensor_scalar(
            t_v1[:], t_sumsq[:], 1.0 / D, EPS, op0=ALU.mult, op1=ALU.add),
            waits=[tok2])
        s3 = em.emit("vector", lambda v: v.tensor_tensor(
            t_v2[:], t_m[:], t_m[:], op=ALU.mult), waits=[s1])
        s4 = em.emit("vector", lambda v: v.scalar_tensor_tensor(
            t_v1[:], t_v2[:], -1.0, t_v1[:], op0=ALU.mult, op1=ALU.add),
            waits=[s2, s3])
        s5 = em.emit("scalar", lambda s: s.activation(
            t_std[:], t_v1[:], AF.Sqrt), waits=[s4])
        s6 = em.emit("vector", lambda v: v.reciprocal(t_r[:], t_std[:]),
                     waits=[s5])
        s7 = em.emit("vector", lambda v: v.scalar_tensor_tensor(
            t_negmr[:], t_m[:], -1.0, t_r[:], op0=ALU.mult, op1=ALU.mult),
            waits=[s6])
        tok5 = em.emit("vector", lambda v: v.tensor_scalar(
            lnt[:], src_ap, t_r[:, 0:1], t_negmr[:, 0:1],
            op0=ALU.mult, op1=ALU.add), waits=[s7, tok2])

        last_copy = None
        last_tp = None
        for grp in range(8):
            pk = grp % 2
            dstp = psB[:, pk * 512:(pk + 1) * 512].bitcast(BF16)

            def fn_tp(t, grp=grp, dstp=dstp):
                out = []
                for j in range(4):
                    ic = grp * 4 + j
                    out.append(t.transpose(dstp[:, j * 128:(j + 1) * 128],
                                           lnt[:, ic * 128:(ic + 1) * 128],
                                           identB))
                return out
            pw = [tok5, tok_const, pack_cons[pk]]
            if grp < 2:
                pw += [t for t in psum_guard if t is not None]
            tok_tp = em.emit("tensor", fn_tp, waits=pw, inc_by=4)
            last_tp = tok_tp

            def fn_cp(s, grp=grp, dstp=dstp, tbcol=tbcol):
                d = BIG.rearrange("p (c t) -> p c t", c=NIC)[
                    :, grp * 4:(grp + 1) * 4, tbcol * 128:(tbcol + 1) * 128]
                return s.copy(d, dstp[:, 0:512].rearrange(
                    "p (c t) -> p c t", c=4))
            last_copy = em.emit("scalar", fn_cp, waits=[tok_tp])
            pack_cons[pk] = last_copy
        ln_tp_last[0] = last_tp
        return last_copy

    a_end = res_toks

    def attention_pipe(pb_, PT, rbc_sb, scale, causal, bufs_state):
        st = bufs_state
        st["pend"] = None

        def emit_tail(p):
            qb, span, ex, rc = p["qb"], p["span"], p["ex"], p["rc"]
            qbl = qb % 4

            def fn_rT(t, qbl=qbl, pk2=qb % 2):
                return t.transpose(
                    psB[0:1, 1024 + qbl * 128:1024 + (qbl + 1) * 128],
                    t_rc[pk2][:, 0:1], identF)
            rT = em.emit("tensor", fn_rT, waits=[rc, st["row_free"]],
                         inc="sPE")
            st["rT_prev"][qb % 2] = rT
            st["rows"].append(rT)
            nch = (span + 127) // 128
            pb = pb_[qb % 2]
            last_pt = None
            for p0 in range(0, nch, 4):
                n = min(4, nch - p0)
                pk = (p0 // 4) % 2
                bankc = 0 if pk == 0 else 1536

                def fn_pt(t, p0=p0, n=n, pb=pb, bankc=bankc):
                    return [t.transpose(
                        psB[:, bankc:bankc + 512].bitcast(BF16)[
                            :, j * 128:(j + 1) * 128],
                        pb[:, (p0 + j) * 128:(p0 + j + 1) * 128],
                        identB) for j in range(n)]
                tpk = em.emit("tensor", fn_pt,
                              waits=[ex, st["pt_pack_free"][pk]],
                              inc="sPE", inc_by=n)
                last_pt = tpk

                if pk == 0:
                    def fn_ptc(sc_, p0=p0, n=n, qbl=qbl, bankc=bankc, PT=PT):
                        d = PT.rearrange("p (c t) -> p c t", c=16)[
                            :, p0:p0 + n, qbl * 128:(qbl + 1) * 128]
                        srcv = psB[:, bankc:bankc + 512].bitcast(BF16)[
                            :, 0:n * 128].rearrange("p (c t) -> p c t", c=n)
                        return sc_.copy(d, srcv)
                    ptc = em.emit("scalar", fn_ptc, waits=[tpk], inc="sAC")
                else:
                    def fn_ptc(v_, p0=p0, n=n, qbl=qbl, bankc=bankc, PT=PT):
                        d = PT.rearrange("p (c t) -> p c t", c=16)[
                            :, p0:p0 + n, qbl * 128:(qbl + 1) * 128]
                        srcv = psB[:, bankc:bankc + 512].bitcast(BF16)[
                            :, 0:n * 128].rearrange("p (c t) -> p c t", c=n)
                        return v_.tensor_copy(d, srcv)
                    ptc = em.emit("vector", fn_ptc, waits=[tpk], inc="sDV")
                st["pt_pack_free"][pk] = ptc
                st["pt_tile"].append(ptc)
            st["p_free"][qb % 2] = last_pt

        def emit_tile(te, tile):
            def fn_rowcp(v, tile=tile):
                return v.tensor_copy(
                    t_row[0:1, (tile % 4) * 512:(tile % 4 + 1) * 512],
                    psB[0:1, 1024:1536])
            rowcp = em.emit("vector", fn_rowcp,
                            waits=st["rows"] + [st["trow_free"]], inc="sDV")

            def fn_rbc(t, tile=tile):
                return t.matmul(
                    psB[:, 1024:1536], onesr,
                    t_row[0:1, (tile % 4) * 512:(tile % 4 + 1) * 512],
                    start=True, stop=True)
            rbct = em.emit("tensor", fn_rbc,
                           waits=[rowcp, st["rbc_free"]], inc="sPE")
            st["trow_free"] = rbct
            rbccp = em.emit("vector",
                            (lambda v, rbc_sb=rbc_sb: v.tensor_copy(
                                rbc_sb, psB[:, 1024:1536])),
                            waits=[rbct, st["rbc_free"]], inc="sDV")
            st["row_free"] = rbccp
            nchT = te["nchT"]
            vsl = te["vsl"]

            def fn_av(t, nchT=nchT, vsl=vsl, PT=PT):
                return [t.matmul(
                    psB[:, 512:1024],
                    vsl.buf[:, c * 128:(c + 1) * 128],
                    PT.rearrange("p (c t) -> p c t", c=16)[:, c, :],
                    start=(c == 0), stop=(c == nchT - 1))
                    for c in range(nchT)]
            av = em.emit("tensor", fn_av,
                         waits=[vsl.prod_tok, st["ptzero"], st["av_prev"]]
                         + st["pt_tile"], inc="sPE", inc_by=nchT)
            vsl.cons_toks.append(av)
            avev = te["evac"](av, rbccp, rbc_sb)
            st["av_prev"] = avev
            st["rbc_free"] = avev
            st["rows"] = []
            st["pt_tile"] = []

        def push(task):
            if task.get("zero") is not None:
                tile = task["zero"]

                def fn_z(v, tile=tile, PT=PT):
                    return [v.memset(
                        PT.rearrange("p (c t) -> p c t", c=16)[
                            :, tile * 4 + cc:tile * 4 + cc + 1,
                            0:cc * 128], 0.0) for cc in range(1, 4)]
                st["ptzero"] = em.emit("vector", fn_z,
                                       waits=[st["av_prev"]], inc="sDV",
                                       inc_by=3)
                return
            qb, span = task["qb"], task["span"]
            ksl, qsl, qcol = task["ksl"], task["qsl"], task["qcol"]
            nfull, rem = span // 512, span % 512
            dc = qb // 4

            def fn_sc(t, qb=qb, nfull=nfull, rem=rem, dc=dc,
                      ksl=ksl, qsl=qsl, qcol=qcol, causal=causal):
                lhs = qsl.buf[:, qcol * 128:(qcol + 1) * 128]
                out = []
                for c in range(nfull):
                    out.append(t.matmul(
                        psA[:, c * 512:(c + 1) * 512], lhs,
                        ksl.buf[:, c * 512:(c + 1) * 512],
                        start=True,
                        stop=(not causal) or (c != dc)))
                if rem:
                    out.append(t.matmul(
                        psA[:, nfull * 512:nfull * 512 + rem], lhs,
                        ksl.buf[:, nfull * 512:nfull * 512 + rem],
                        start=True, stop=(nfull != dc)))
                if causal:
                    out.append(t.matmul(psA[:, qb * 128:qb * 128 + 128],
                                        maskT, identB, start=False,
                                        stop=True))
                return out
            n_sc = nfull + (1 if rem else 0) + (1 if causal else 0)
            sc = em.emit("tensor", fn_sc,
                         waits=[ksl.prod_tok, qsl.prod_tok,
                                st["exp_prev"], tok_const]
                         + task.get("extra", []),
                         inc="sPE", inc_by=n_sc)
            ksl.cons_toks.append(sc)
            qsl.cons_toks.append(sc)
            nchunks = nfull + (1 if rem else 0)
            mx_toks = []
            for c in range(nchunks):
                c0, c1 = c * 512, min((c + 1) * 512, span)
                ctok = (sc[0], sc[1] - (n_sc - 1 - c))
                if causal and c == qb // 4:
                    ctok = sc
                mx_toks.append(em.emit(
                    "vector",
                    (lambda v, c=c, c0=c0, c1=c1: v.reduce_max(
                        t_mxc[:, c:c + 1], psA[:, c0:c1], axis=AX.X)),
                    waits=[ctok, st["exp_prev"]]))
            mx = em.emit("vector",
                         (lambda v, nchunks=nchunks: (
                             v.reduce_max(t_nsx[:], t_mxc[:, 0:nchunks],
                                          axis=AX.X, negate=True))),
                         waits=mx_toks + [st["exp_prev"]])
            pb = pb_[qb % 2]

            pk2 = qb % 2

            def fn_exp(s_, pb=pb, span=span, pk2=pk2):
                return s_.activation(pb[:, 0:span], psA[:, 0:span],
                                     AF.Exp, bias=t_nsx[:, 0:1],
                                     accum_out=t_sumsP[pk2][:])
            ex = em.emit("scalar", fn_exp,
                         waits=[mx, st["p_free"][pk2], st["rc_prev"][pk2]],
                         inc="sAC")
            st["exp_prev"] = ex
            rc = em.emit("vector",
                         (lambda v, pk2=pk2: v.reciprocal(
                             t_rc[pk2][:], t_sumsP[pk2][:])),
                         waits=[ex, st["rT_prev"][pk2]], inc="sDV")
            st["rc_prev"][pk2] = rc
            if st["pend"] is not None:
                emit_tail(st["pend"])
                if st["pend"]["tile_end"] is not None:
                    emit_tile(st["pend"]["tile_end"], st["pend"]["tile"])
            st["pend"] = {"qb": qb, "span": span, "ex": ex, "rc": rc,
                          "tile_end": task.get("tile_end"),
                          "tile": task.get("tile", 0)}

        def flush():
            if st["pend"] is not None:
                emit_tail(st["pend"])
                if st["pend"]["tile_end"] is not None:
                    emit_tile(st["pend"]["tile_end"], st["pend"]["tile"])
                st["pend"] = None
        return push, flush


    if "A" in phases:
        xbufs = [view(scr, PH0 + i * 8192, 4096) for i in range(2)]
        qk_stg = [view(scr, PH0 + 16384 + i * 4096, 2048) for i in range(2)]
        v_stg = [view(scr, PH0 + 24576 + i * 1024, 512) for i in range(2)]
        xring = Ring(em, "xa", xbufs)
        stQK = StoreRing("stQK")
        stV = StoreRing("stV")
        stAT = StoreRing("stAT")

        qk_psum_free = [None, None]
        qk_stg_free = [None, None]
        v_bank_free = [None] * 8
        v_stg_free = [None, None]

        for half in range(2):
            ln_last = None
            for tbl in range(8):
                tbg = half * 8 + tbl
                sl = xring.acquire()
                xring.produce(sl, "gpsimd",
                              (lambda g, tbg=tbg, sl=sl: g.dma_start(
                                  sl.buf, xf[tbg * 128:(tbg + 1) * 128, :])))
                guard = ()
                if half == 1 and tbl == 0:
                    guard = qk_psum_free + [t for t in v_bank_free
                                            if t is not None]
                tok = emit_ln_block(sl.buf, tbl, extra_w=[sl.prod_tok],
                                    psum_guard=guard)
                sl.cons_toks.append(tok)
                ln_last = tok
            lnfm_rdy = ln_last

            for W, dstD, qscl in ((wq_my, Qa, SCALE_A), (wk_my, Ka, None)):
                for fp in range(8):
                    pk = fp % 2
                    ps = psA if pk == 0 else psB
                    last_mm = None
                    for ic in range(NIC):
                        sl = wring.acquire()
                        wring.produce(sl, "gpsimd",
                                      (lambda g, W=W, fp=fp, ic=ic, sl=sl:
                                       g.dma_start(sl.buf[:, 0:256],
                                                   W[ic * 128:(ic + 1) * 128,
                                                     fp * 256:(fp + 1) * 256])))
                        pw = [lnfm_rdy]
                        if ic == 0:
                            pw.append(qk_psum_free[pk])
                            pw += [t for t in v_bank_free if t is not None]

                        def fn_mm(t, ic=ic, sl=sl, ps=ps):
                            out = []
                            for fc in range(2):
                                for tt in range(2):
                                    out.append(t.matmul(
                                        ps[:, fc * 1024 + tt * 512:
                                           fc * 1024 + (tt + 1) * 512],
                                        sl.buf[:, fc * 128:(fc + 1) * 128],
                                        bigv(ic, tt * 512, (tt + 1) * 512),
                                        start=(ic == 0), stop=(ic == NIC - 1)))
                            return out
                        last_mm = wring.consume(sl, "tensor", fn_mm,
                                                extra_waits=pw, inc="sPE",
                                                inc_by=4)
                    stg = qk_stg[pk]

                    def fn_ev(s, ps=ps, stg=stg, qscl=qscl):
                        if qscl is None:
                            return s.copy(stg, ps[:, 0:2048])
                        return s.activation(stg, ps[:, 0:2048], AF.Copy,
                                            scale=qscl)
                    ev = em.emit("scalar", fn_ev,
                                 waits=[last_mm, qk_stg_free[pk]], inc="sAC")
                    qk_psum_free[pk] = ev

                    def fn_st(sy, dstD=dstD, fp=fp, half=half, stg=stg):
                        return sy.dma_start(
                            dstD.rearrange("(a p) t -> p a t", p=128)[
                                :, 2 * fp:2 * fp + 2,
                                half * 1024:(half + 1) * 1024],
                            stg.rearrange("p (a t) -> p a t", a=2))
                    qk_stg_free[pk] = stQK.store(pk, fn_st, waits=[ev])

            for vc in range(4):
                last_mm = None
                for ic in range(NIC):
                    sl = wring.acquire()
                    wring.produce(sl, "gpsimd",
                                  (lambda g, vc=vc, ic=ic, sl=sl:
                                   g.dma_start(sl.buf,
                                               wv_my[ic * 128:(ic + 1) * 128,
                                                     vc * 512:(vc + 1) * 512])))
                    pw = [lnfm_rdy]
                    if ic == 0:
                        pw += [t for t in v_bank_free if t is not None]
                        pw += qk_psum_free

                    def fn_mm(t, ic=ic, sl=sl):
                        out = []
                        for tb in range(8):
                            out.append(t.matmul(
                                bank(tb), bigv(ic, tb * 128, (tb + 1) * 128),
                                sl.buf, start=(ic == 0),
                                stop=(ic == NIC - 1)))
                        return out
                    last_mm = wring.consume(sl, "tensor", fn_mm,
                                            extra_waits=pw, inc="sPE",
                                            inc_by=8)
                    tick()
                for tb in range(8):
                    pk = tb % 2
                    stg = v_stg[pk]

                    def fn_ev(v, tb=tb, stg=stg):
                        return v.tensor_copy(stg, bank(tb))
                    ev = em.emit("vector", fn_ev,
                                 waits=[last_mm, v_stg_free[pk]], inc="sDV")
                    v_bank_free[tb] = ev

                    def fn_st(sy, tb=tb, vc=vc, half=half, stg=stg):
                        r0 = half * 1024 + tb * 128
                        return sy.dma_start(
                            Va[r0:r0 + 128, vc * 512:(vc + 1) * 512], stg)
                    v_stg_free[pk] = stV.store(pk, fn_st, waits=[ev])

        if a_stop < 2:
            a_end = res_toks
        kb = [view(big1, i * 4096, 2048) for i in range(2)]
        qb_ = [view(big1, 8192 + i * 4096, 2048) for i in range(2)]
        vb = [view(big1, 16384 + i * 4096, 2048) for i in range(2)]
        pb_ = [view(big1, 24576 + i * 4096, 2048) for i in range(2)]
        PT = view(big1, 32768, 8192)
        astg = [view(big1, 49152 + i * 1024, 512) for i in range(2)]
        rbc_sb = view(big1, 51200, 512, F32)
        fA3 = fence()
        kring = Ring(em, "ka", kb, init_cons=fA3)
        qring = Ring(em, "qa", qb_, init_cons=fA3)
        vring = Ring(em, "va", vb, init_cons=fA3)
        all_qkv_l = stQK.all_toks() + stV.all_toks()

        pt_pack_free = [fA3, fA3]
        p_free = [fA3, fA3]
        exp_prev = fA3
        av_prev = fA3
        avst_free = [None, None]
        rbc_free = fA3
        row_free = fA3
        trow_free = fA3
        ptzero = fA3

        st_at = dict(pt_pack_free=[fA3, fA3], p_free=[fA3, fA3],
                     exp_prev=fA3, av_prev=fA3, rbc_free=fA3,
                     row_free=fA3, trow_free=fA3, ptzero=fA3,
                     rT_prev=[fA3, fA3], rc_prev=[fA3, fA3],
                     rows=[], pt_tile=[])
        push_a, flush_a = attention_pipe(pb_, PT, rbc_sb, SCALE_A, True,
                                         st_at)

        def mk_evac_a(h, tile):
            def evac(av, rbccp, rbc_sb):
                stg = astg[tile % 2]

                def fn_avev(v, stg=stg, rbc_sb=rbc_sb):
                    return v.tensor_tensor(stg, psB[:, 512:1024], rbc_sb,
                                           op=ALU.mult)
                avev = em.emit("vector", fn_avev,
                               waits=[av, rbccp, avst_free[tile % 2]],
                               inc="sDV")

                def fn_avst(sy, h=h, tile=tile, stg=stg):
                    return sy.dma_start(
                        attn_ctr[h * 128:(h + 1) * 128,
                                 tile * 512:(tile + 1) * 512], stg)
                avst_free[tile % 2] = stAT.store(tile % 2, fn_avst,
                                                 waits=[avev])
                return avev
            return evac

        for h in range(16 if a_stop >= 2 else 0):
            ksl = kring.acquire()
            kring.produce(ksl, "sync",
                          (lambda sy, h=h, ksl=ksl: sy.dma_start(
                              ksl.buf, Ka[h * 128:(h + 1) * 128, :])),
                          extra_waits=all_qkv_l)
            qsl = qring.acquire()
            qring.produce(qsl, "sync",
                          (lambda sy, h=h, qsl=qsl: sy.dma_start(
                              qsl.buf, Qa[h * 128:(h + 1) * 128, :])),
                          extra_waits=all_qkv_l)
            vsl = vring.acquire()

            def fn_vl(sy, h=h, vsl=vsl):
                for q4 in range(4):
                    sy.dma_start(
                        vsl.buf.rearrange("p (c d) -> p c d", c=16)[
                            :, q4 * 4:(q4 + 1) * 4, :],
                        Va.rearrange("(c p) v -> p c v", p=128)[
                            :, q4 * 4:(q4 + 1) * 4, h * 128:(h + 1) * 128]
                    ).then_inc(SEMH[vsl.sem], 16)
                return None
            vring.produce(vsl, "sync", fn_vl, extra_waits=all_qkv_l, n_dma=4)

            for tile in range(4):
                for qbl in range(4):
                    qb = tile * 4 + qbl
                    t = {"qb": qb, "span": (qb + 1) * 128, "ksl": ksl,
                         "qsl": qsl, "qcol": qb, "tile": tile, "zero": None}
                    if qbl == 3:
                        t["tile_end"] = {"nchT": (tile + 1) * 4, "vsl": vsl,
                                         "evac": mk_evac_a(h, tile)}
                    else:
                        t["tile_end"] = None
                    push_a(t)
                    tick(2)
                    if qbl == 0:
                        push_a({"zero": tile})
        if a_stop >= 2:
            flush_a()

        def fn_cc(g):
            return g.collective_compute(
                "AllGather", ALU.bypass,
                replica_groups=[[0, 1], [2, 3], [4, 5], [6, 7]],
                ins=[attn_ctr.opt()], outs=[attn_gth.opt()])
        cc_tok = None
        if a_stop >= 3:
            cc_tok = em.emit("gpsimd", fn_cc, waits=stAT.all_toks(),
                             inc="sCC")

        em.sem("qAFM")

        def fn_afm(g):
            cid = g.compute_val(nc.partition_id(engines=[mybir.EngineType.Pool]))
            off = g.snap((cid % 2) * 1024, min_val=0, max_val=1024)
            src = attn_gth.rearrange("(c p) t -> p c t", p=128)
            dst = BIG.rearrange("p (c t) -> p c t", c=NIC)
            for j in range(4):
                g.dma_start(dst[:, j * 8:(j + 1) * 8, :],
                            src[:, j * 8:(j + 1) * 8, bass.ds(off, 1024)]
                            ).then_inc(SEMH["qAFM"], 16)
            return None
        afm_tok = None
        if a_stop >= 4:
            afm_tok = em.emit("gpsimd", fn_afm, waits=[cc_tok] + fence(),
                              inc="qAFM", inc_by=16 * 4, manual=True)

        oev = [None] * 8
        oev[0] = fence()
        for oc in range(8 if a_stop >= 5 else 0):
            last_mm = None
            for fc in range(NIC):
                sl = wring.acquire()
                wring.produce(sl, "gpsimd",
                              (lambda g, oc=oc, fc=fc, sl=sl:
                               g.dma_start(sl.buf,
                                           wo[fc * 128:(fc + 1) * 128,
                                              oc * 512:(oc + 1) * 512])))
                pw = [afm_tok, tok_res_all]
                if fc == 0:
                    pw += [t for t in oev if t is not None]

                def fn_mm(t, fc=fc, sl=sl):
                    return [t.matmul(bank(tb),
                                     bigv(fc, tb * 128, (tb + 1) * 128),
                                     sl.buf, start=(fc == 0),
                                     stop=(fc == NIC - 1))
                            for tb in range(8)]
                last_mm = wring.consume(sl, "tensor", fn_mm, extra_waits=pw,
                                        inc="sPE", inc_by=8)
                tick(2)
            for tb in range(8):
                def fn_ev(v, tb=tb, oc=oc):
                    d = resv(tb)[:, oc * 512:(oc + 1) * 512]
                    return v.tensor_tensor(d, bank(tb), d, op=ALU.add)
                oev[tb] = em.emit("vector", fn_ev, waits=[last_mm], inc="sDV")
        a_end = [t for t in oev if t is not None]
        if a_stop < 5:
            a_end = res_toks + ([cc_tok] if cc_tok else
                                ([avst_free[0], avst_free[1]]
                                 if a_stop >= 2 else []))

    out_stg_free = [None, None]
    stOUT = StoreRing("stOUT")

    def emit_ffn(prev_toks, to_output):
        gwt_sb = view(scr, PH0, 256)
        ics_sb = view(scr, PH0 + 512, 4096, parts=8)
        ocs_sb = view(scr, PH0 + 8704, 4096, parts=8)
        bias_bc = view(scr, PH0 + 16896, 4096)
        routT = view(scr, PH0 + 25088, 1024, parts=8)
        routf = view(scr, PH0 + 27136, 8, F32)
        routb = view(scr, PH0 + 27168, 8)
        os_sb = [view(scr, PH0 + 27200 + i * 1024, 512) for i in range(8)]
        tu_sb = [view(scr, PH0 + 35392 + i * 2048, 512, F32)
                 for i in range(2)]
        outst = [view(scr, PH0 + 39488 + i * 2048, 512, F32)
                 for i in range(2)]

        rsem = rowsem()
        fF = fence()

        def fn_rows(g):
            g.dma_start(gwt_sb.rearrange("p (c e) -> p c e", c=NIC),
                        gwT.rearrange("(c p) e -> p c e", p=128)
                        ).then_inc(SEMH[rsem], 16)
            g.dma_start(ics_sb, ics[:, :]).then_inc(SEMH[rsem], 16)
            g.dma_start(ocs_sb, ocs[:, :]).then_inc(SEMH[rsem], 16)
            g.dma_start(bias_bc[0:1, :], fbias[:, :]).then_inc(SEMH[rsem], 16)
            return None
        rows_tok = em.emit("gpsimd", fn_rows, waits=list(prev_toks),
                           inc=rsem, inc_by=64, manual=True)

        bb_prev = fF
        bb_toks = []
        for j in range(8):
            def fn_bb(t, j=j):
                return t.matmul(psB[:, 1024:1536], onesr,
                                bias_bc[0:1, j * 512:(j + 1) * 512],
                                start=True, stop=True)
            t1 = em.emit("tensor", fn_bb, waits=[rows_tok, tok_const, bb_prev],
                         inc="sPE")

            def fn_bbc(s, j=j):
                return s.copy(bias_bc[:, j * 512:(j + 1) * 512],
                                     psB[:, 1024:1536])
            bb_prev = em.emit("scalar", fn_bbc, waits=[t1], inc="sAC")
            bb_toks.append(bb_prev)
        bias_rdy = bb_toks[-1]

        ln_toks = []
        for tb in range(NTB):
            ln_toks.append(emit_ln_block(resv(tb), tb,
                                         extra_w=list(prev_toks)))
        lnfm_rdy = ln_toks[-1]

        rt_prev = fF
        lg_free = fF
        rt_toks = []
        for tb in range(NTB):
            def fn_lg(t, tb=tb):
                return [t.matmul(
                    psB[:, 1024:1032],
                    bigv(ic, tb * 128, (tb + 1) * 128),
                    gwt_sb.rearrange("p (c e) -> p c e", c=NIC)[:, ic, :],
                    start=(ic == 0), stop=(ic == NIC - 1))
                    for ic in range(NIC)]
            lg = em.emit("tensor", fn_lg,
                         waits=[lnfm_rdy, rows_tok, bias_rdy, lg_free],
                         inc="sPE", inc_by=NIC)

            sm0 = em.emit("vector",
                          lambda v: v.reduce_max(t_mx[:], psB[:, 1024:1032],
                                                 axis=AX.X), waits=[lg])
            sm = em.emit("vector",
                         lambda v: v.tensor_scalar(t_nsx[:], t_mx[:], -1.0,
                                                   None, op0=ALU.mult),
                         waits=[sm0])

            def fn_ex(s):
                return s.activation(routf[:, 0:8], psB[:, 1024:1032], AF.Exp,
                                    bias=t_nsx[:, 0:1],
                                    accum_out=t_sumsP[0][:])
            ex = em.emit("scalar", fn_ex, waits=[sm], inc="sAC")
            lg_free = ex

            nr0 = em.emit("vector",
                          lambda v: v.reciprocal(t_rc[0][:], t_sumsP[0][:]),
                          waits=[ex])
            nr = em.emit("vector",
                         lambda v: v.tensor_scalar(routb[:, 0:8],
                                                   routf[:, 0:8],
                                                   t_rc[0][:, 0:1], None,
                                                   op0=ALU.mult),
                         waits=[nr0])

            def fn_rt(t):
                d = psB[:, 1536:2048].bitcast(BF16)[0:8, 0:128]
                return t.transpose(d, routb[:, 0:8], identB)
            rt = em.emit("tensor", fn_rt, waits=[nr, rt_prev], inc="sPE")

            def fn_rtc(s, tb=tb):
                return s.copy(routT[0:8, tb * 128:(tb + 1) * 128],
                              psB[:, 1536:2048].bitcast(BF16)[0:8, 0:128])
            rt_prev = em.emit("scalar", fn_rtc, waits=[rt], inc="sAC")
            rt_toks.append(rt_prev)
        routT_rdy = rt_toks[-1]

        is_free = [fF, fF]
        xs_last = None
        for ic in range(NIC):
            for tt in range(2):
                pk = (ic * 2 + tt) % 2
                ps = psA[:, pk * 512:(pk + 1) * 512]

                def fn_is(t, ic=ic, tt=tt, ps=ps):
                    return t.matmul(ps, ics_sb[:, ic * 128:(ic + 1) * 128],
                                    routT[0:8, tt * 512:(tt + 1) * 512],
                                    start=True, stop=True)
                ist = em.emit("tensor", fn_is,
                              waits=[routT_rdy, is_free[pk]], inc="sPE")

                def fn_xs(v, ic=ic, tt=tt, ps=ps):
                    d = bigv(ic, tt * 512, (tt + 1) * 512)
                    return v.tensor_tensor(d, ps, d, op=ALU.mult)
                xs_last = em.emit("vector", fn_xs, waits=[ist], inc="sDV")
                is_free[pk] = xs_last
        xs_rdy = xs_last

        badd = []
        for tb in range(NTB):
            def fn_ba(g, tb=tb):
                return g.tensor_tensor(resv(tb), resv(tb), bias_bc,
                                       op=ALU.add)
            badd.append(em.emit("gpsimd", fn_ba,
                                waits=[bias_rdy, ln_toks[tb]], inc="sGP"))

        ev_done = [None] * 8
        ev_done[0] = fF
        os_free = [fF] * 8
        tu_free = [fF, fF]
        out_toks = []
        for oc in range(8):
            os_toks = []
            for tb in range(NTB):
                pk = tb % 2
                ps = psB[:, 1024 + pk * 512:1536 + pk * 512]

                def fn_os(t, tb=tb, oc=oc, ps=ps):
                    return t.matmul(ps, routT[0:8, tb * 128:(tb + 1) * 128],
                                    ocs_sb[:, oc * 512:(oc + 1) * 512],
                                    start=True, stop=True)
                pw = [routT_rdy, ev_done[6 + pk], xs_rdy]
                if len(os_toks) >= 2:
                    pw.append(os_toks[-2][1])
                ot = em.emit("tensor", fn_os, waits=pw, inc="sPE")

                def fn_osc(s, tb=tb, ps=ps):
                    return s.copy(os_sb[tb], ps)
                oc_t = em.emit("scalar", fn_osc,
                               waits=[ot, os_free[tb]], inc="sAC")
                os_toks.append((ot, oc_t))
            os_all = os_toks[-1][1]

            last_mm = None
            for ic in range(NIC):
                sl = wring.acquire()
                wring.produce(sl, "gpsimd",
                              (lambda g, oc=oc, ic=ic, sl=sl:
                               g.dma_start(sl.buf,
                                           fT[ic * 128:(ic + 1) * 128,
                                              oc * 512:(oc + 1) * 512])))
                gs = sgnring.acquire()

                def fn_sgn(s_, sl=sl, gs=gs):
                    return s_.activation(gs.buf, sl.buf, AF.Sign)
                sgnring.produce(gs, "scalar", fn_sgn,
                                extra_waits=[sl.prod_tok] + list(prev_toks))
                sl.cons_toks.append(gs.prod_tok)
                pw = [xs_rdy, os_all]
                if ic == 0:
                    pw += [t for t in ev_done if t is not None]

                def fn_mm(t, ic=ic, gs=gs):
                    return [t.matmul(bank(tb),
                                     bigv(ic, tb * 128, (tb + 1) * 128),
                                     gs.buf, start=(ic == 0),
                                     stop=(ic == NIC - 1))
                            for tb in range(8)]
                last_mm = sgnring.consume(gs, "tensor", fn_mm,
                                          extra_waits=pw, inc="sPE",
                                          inc_by=8)
            for tb in range(8):
                pk = tb % 2
                tu = tu_sb[pk]

                def fn_e1(v, tb=tb, tu=tu):
                    return v.tensor_tensor(tu, bank(tb), os_sb[tb],
                                           op=ALU.mult)
                e1 = em.emit("vector", fn_e1,
                             waits=[last_mm, os_toks[tb][1], tu_free[pk]],
                             inc="sDV")
                os_free[tb] = e1
                if not to_output:
                    def fn_e2(g, tb=tb, oc=oc, tu=tu):
                        d = resv(tb)[:, oc * 512:(oc + 1) * 512]
                        return g.tensor_tensor(d, tu, d, op=ALU.add)
                    e2 = em.emit("gpsimd", fn_e2, waits=[e1, badd[tb]],
                                 inc="sGP")
                    tu_free[pk] = e2
                    ev_done[tb] = e2
                else:
                    ost = outst[pk]

                    def fn_e2(g, tb=tb, oc=oc, ost=ost, tu=tu):
                        return g.tensor_tensor(
                            ost, tu, resv(tb)[:, oc * 512:(oc + 1) * 512],
                            op=ALU.add)
                    e2 = em.emit("gpsimd", fn_e2,
                                 waits=[e1, badd[tb], out_stg_free[pk]],
                                 inc="sGP")
                    tu_free[pk] = e2

                    def fn_yst(sy, tb=tb, oc=oc, ost=ost):
                        return sy.dma_start(
                            y[tb * 128:(tb + 1) * 128,
                              oc * 512:(oc + 1) * 512], ost)
                    yt = stOUT.store(pk, fn_yst, waits=[e2])
                    out_stg_free[pk] = yt
                    out_toks.append(yt)
                    ev_done[tb] = e2
        return out_toks if to_output else [t for t in ev_done
                                           if t is not None]

    a_end = list(a_end) + p0_drain()
    if "B" in phases:
        b_end = emit_ffn(a_end, to_output=False)
    else:
        b_end = a_end

    if "C" in phases:
        stKV = StoreRing("stKV")
        stQB = StoreRing("stQB")
        stAB = StoreRing("stAB")
        sgnC = Ring(em, "gc", [view(scr, 55296 + i * 1024, 512)
                               for i in range(4)], dma=False)
        stg2 = [view(scr, SGN0 + i * 2048, 1024) for i in range(2)]
        svbc = view(scr, PH0 + 40960, 512)
        tuv2 = [view(scr, SGN0 + i * 2048, 512, F32)
                for i in range(2)]

        ln_toks = []
        for tb in range(NTB):
            ln_toks.append(emit_ln_block(resv(tb), tb, extra_w=list(b_end)))
        lnfm_rdy = ln_toks[-1]

        fC = fence()
        psum8_guard = list(fC)

        def str_ps(fc):
            src = psA if fc < 2 else psB
            return src[:, (fc % 2) * 1024:(fc % 2 + 1) * 1024]

        stg_free = [fC, fC]

        def grid_to_bcast(key, ob0):
            tg = []
            for j in range(4):
                def fn_tg(t, key=key, ob0=ob0, j=j):
                    return t.transpose(
                        psB[0:1, 1536 + j * 128:1536 + (j + 1) * 128],
                        sg_fin[key][:, ob0 + j:ob0 + j + 1], identF)
                tg.append(em.emit("tensor", fn_tg,
                                  waits=[tok_const] + psum8_guard
                                  + p0_marks["fin"], inc="sPE"))
            cb = em.emit("vector",
                         lambda v: v.tensor_copy(t_srowb[0:1, :],
                                                 psB[0:1, 1536:2048]),
                         waits=tg)
            bm = em.emit("tensor",
                         lambda t: t.matmul(psB[:, 1024:1536], onesr,
                                            t_srowb[0:1, :],
                                            start=True, stop=True),
                         waits=[cb, tok_const] + psum8_guard, inc="sPE")
            cc = em.emit("vector",
                         lambda v: v.tensor_copy(svbc, psB[:, 1024:1536]),
                         waits=[bm])
            psum8_guard.extend([cb, cc])
            return cc

        def feat_strips(SD, nst, key, Dst, stR):
            for st in range(nst):
                last_mm = None
                for ic in range(NIC):
                    sl = wring.acquire()
                    wring.produce(sl, "gpsimd",
                                  (lambda g, SD=SD, st=st, ic=ic, sl=sl:
                                   g.dma_start(sl.buf,
                                               SD[ic * 128:(ic + 1) * 128,
                                                  st * 512:(st + 1) * 512])))
                    gs = sgnC.acquire()

                    def fn_sgn(s_, sl=sl, gs=gs):
                        return s_.activation(gs.buf, sl.buf, AF.Sign)
                    sgnC.produce(gs, "scalar", fn_sgn,
                                 extra_waits=[sl.prod_tok] + list(b_end))
                    sl.cons_toks.append(gs.prod_tok)
                    pw = [lnfm_rdy]
                    if ic == 0:
                        pw = pw + psum8_guard

                    def fn_mm(t, ic=ic, gs=gs):
                        return [t.matmul(
                            str_ps(fc)[:, tt * 512:(tt + 1) * 512],
                            gs.buf[:, fc * 128:(fc + 1) * 128],
                            bigv(ic, tt * 512, (tt + 1) * 512),
                            start=(ic == 0), stop=(ic == NIC - 1))
                            for fc in range(4) for tt in range(2)]
                    last_mm = sgnC.consume(gs, "tensor", fn_mm,
                                              extra_waits=pw, inc="sPE",
                                              inc_by=8)
                for fc in range(4):
                    pk = fc % 2
                    stg = stg2[pk]

                    def fn_ev(e, fc=fc, st=st, key=key, stg=stg):
                        return e.activation(
                            stg, str_ps(fc), AF.Copy,
                            scale=sg_fin[key][:, st * 4 + fc:st * 4 + fc + 1])
                    ev = em.emit("scalar", fn_ev,
                                 waits=[last_mm, stg_free[pk]]
                                 + p0_marks["fin"], inc="sAC")
                    psum8_guard.append(ev)

                    def fn_st(sy, st=st, fc=fc, stg=stg, Dst=Dst):
                        r0 = st * 512 + fc * 128
                        return sy.dma_start(Dst[r0:r0 + 128, :], stg)
                    stg_free[pk] = stR.store(pk, fn_st, waits=[ev])

        feat_strips(kT, 2, "k", Kc, stKV)

        v_bank_free2 = [None] * 4
        vst_free = stg_free
        for st in range(2):
            svc = grid_to_bcast("v", st * 4)
            for tbp in range(2):
                last_mm = None
                for ic in range(NIC):
                    sl = wring.acquire()
                    wring.produce(sl, "gpsimd",
                                  (lambda g, st=st, ic=ic, sl=sl:
                                   g.dma_start(sl.buf,
                                               vT[ic * 128:(ic + 1) * 128,
                                                  st * 512:(st + 1) * 512])))
                    gs = sgnC.acquire()

                    def fn_sgn(s_, sl=sl, gs=gs):
                        return s_.activation(gs.buf, sl.buf, AF.Sign)
                    sgnC.produce(gs, "scalar", fn_sgn,
                                 extra_waits=[sl.prod_tok] + list(b_end))
                    sl.cons_toks.append(gs.prod_tok)
                    pw = [lnfm_rdy]
                    if ic == 0:
                        pw = pw + psum8_guard \
                            + [t for t in v_bank_free2 if t is not None]

                    def fn_mm(t, ic=ic, tbp=tbp, gs=gs):
                        return [t.matmul(
                            psA[:, j * 512:(j + 1) * 512],
                            bigv(ic, (tbp * 4 + j) * 128,
                                 (tbp * 4 + j + 1) * 128),
                            gs.buf,
                            start=(ic == 0), stop=(ic == NIC - 1))
                            for j in range(4)]
                    last_mm = sgnC.consume(gs, "tensor", fn_mm,
                                              extra_waits=pw, inc="sPE",
                                              inc_by=4)
                for j in range(4):
                    tb = tbp * 4 + j
                    pk = tb % 2
                    stg = stg2[pk]

                    def fn_ev(v, j=j, stg=stg):
                        return v.tensor_tensor(stg[:, 0:512],
                                               psA[:, j * 512:(j + 1) * 512],
                                               svbc, op=ALU.mult)
                    ev = em.emit("vector", fn_ev,
                                 waits=[last_mm, svc, vst_free[pk]])
                    v_bank_free2[j] = ev
                    psum8_guard.append(ev)

                    def fn_st(sy, tb=tb, st=st, stg=stg):
                        return sy.dma_start(Vc[tb * 128:(tb + 1) * 128,
                                               st * 512:(st + 1) * 512],
                                            stg[:, 0:512])
                    vst_free[pk] = stKV.store(pk, fn_st, waits=[ev])

        def fn_cc2(g):
            return g.collective_compute(
                "AllGather", ALU.bypass,
                replica_groups=[[0, 1], [2, 3], [4, 5], [6, 7]],
                ins=[Kc.opt()], outs=[Kg.opt()])
        cck = em.emit("gpsimd", fn_cc2, waits=stKV.all_toks(),
                      inc="sCC")

        def fn_cc3(g):
            return g.collective_compute(
                "AllGather", ALU.bypass,
                replica_groups=[[0, 1], [2, 3], [4, 5], [6, 7]],
                ins=[Vc.opt()], outs=[Vg.opt()])
        cc2 = em.emit("gpsimd", fn_cc3, waits=stKV.all_toks(), inc="sCC")

        feat_strips(qT, 8, "q", Qb, stQB)

        kb = [view(big1, i * 4096, 2048) for i in range(2)]
        vb = [view(big1, 8192 + i * 4096, 2048) for i in range(2)]
        qb2 = [view(big1, 16384 + i * 2048, 1024) for i in range(2)]
        pb_ = [view(big1, 20480 + i * 4096, 2048) for i in range(2)]
        PT = view(big1, 28672, 8192)
        astg = [view(big1, 45056 + i * 1024, 512) for i in range(2)]
        rbc_sb = view(big1, 47104, 512, F32)
        fC4 = fence()
        kringb = Ring(em, "kb", kb, init_cons=fC4)
        vringb = Ring(em, "vb", vb, init_cons=fC4)
        qringb = Ring(em, "qb", qb2, init_cons=fC4)
        all_qb_l = stQB.all_toks()

        pt_pack_free = [fC4, fC4]
        p_free = [fC4, fC4]
        exp_prev = fC4
        av_prev = fC4
        ab_stg_free = [None, None]
        rbc_free = fC4
        row_free = fC4
        trow_free = fC4

        st_cb = dict(pt_pack_free=[fC4, fC4], p_free=[fC4, fC4],
                     exp_prev=fC4, av_prev=fC4, rbc_free=fC4,
                     row_free=fC4, trow_free=fC4, ptzero=fC4,
                     rT_prev=[fC4, fC4], rc_prev=[fC4, fC4],
                     rows=[], pt_tile=[])
        push_b, flush_b = attention_pipe(pb_, PT, rbc_sb, SCALE_B, False,
                                         st_cb)

        ab_stg_free = [None, None]

        def mk_evac_b(hq, tile):
            def evac(av, rbccp, rbc_sb):
                stg = astg[tile % 2]

                def fn_avev(v, stg=stg, rbc_sb=rbc_sb):
                    return v.tensor_tensor(stg, psB[:, 512:1024], rbc_sb,
                                           op=ALU.mult)
                avev = em.emit("vector", fn_avev,
                               waits=[av, rbccp, ab_stg_free[tile % 2]],
                               inc="sDV")

                def fn_ast(sy, hq=hq, tile=tile, stg=stg):
                    return sy.dma_start(
                        Ab[hq * 128:(hq + 1) * 128,
                           tile * 512:(tile + 1) * 512], stg)
                ab_stg_free[tile % 2] = stAB.store(tile % 2, fn_ast,
                                                   waits=[avev])
                return avev
            return evac

        for kvh in range(8):
            ksl = kringb.acquire()

            def fn_kl(sy, kvh=kvh, ksl=ksl):
                sy.dma_start(ksl.buf[:, 0:1024],
                             Kg[kvh * 128:(kvh + 1) * 128, :]
                             ).then_inc(SEMH[ksl.sem], 16)
                sy.dma_start(ksl.buf[:, 1024:2048],
                             Kg[1024 + kvh * 128:1024 + (kvh + 1) * 128, :]
                             ).then_inc(SEMH[ksl.sem], 16)
                return None
            kringb.produce(ksl, "sync", fn_kl, extra_waits=[cck], n_dma=2)
            vsl = vringb.acquire()

            def fn_vl(sy, kvh=kvh, vsl=vsl):
                for q4 in range(4):
                    sy.dma_start(
                        vsl.buf.rearrange("p (c d) -> p c d", c=16)[
                            :, q4 * 4:(q4 + 1) * 4, :],
                        Vg.rearrange("(c p) v -> p c v", p=128)[
                            :, q4 * 4:(q4 + 1) * 4,
                            kvh * 128:(kvh + 1) * 128]
                    ).then_inc(SEMH[vsl.sem], 16)
                return None
            vringb.produce(vsl, "sync", fn_vl, extra_waits=[cc2], n_dma=4)

            for qh in range(4):
                hq = kvh * 4 + qh
                qsl = qringb.acquire()
                qringb.produce(qsl, "sync",
                               (lambda sy, hq=hq, qsl=qsl: sy.dma_start(
                                   qsl.buf, Qb[hq * 128:(hq + 1) * 128, :])),
                               extra_waits=all_qb_l)
                for tile in range(2):
                    for qbl in range(4):
                        qb = tile * 4 + qbl
                        t = {"qb": qb, "span": 2048, "ksl": ksl,
                             "qsl": qsl, "qcol": qb, "tile": tile,
                             "zero": None, "tile_end": None}
                        if qbl == 3:
                            t["tile_end"] = {"nchT": 16, "vsl": vsl,
                                             "evac": mk_evac_b(hq, tile)}
                        push_b(t)
        flush_b()

        em.sem("qABM")

        def fn_abm(sy):
            for fc in range(NIC):
                sy.dma_start(BIG[:, fc * TOK:(fc + 1) * TOK],
                             Ab[fc * 128:(fc + 1) * 128, :]
                             ).then_inc(SEMH["qABM"], 16)
            return None
        abm_tok = em.emit("sync", fn_abm, waits=stAB.all_toks(),
                          inc="qABM", inc_by=16 * NIC, manual=True)

        fC5 = fence()
        psum8_guard.extend(list(fC5))
        oev = [None] * 8
        oev[0] = fC5
        o_bank_free = [None] * 4
        tuv2_init = stQB.all_toks() + stKV.all_toks() + list(fC5)
        tu_free2 = [tuv2_init, tuv2_init]
        for oc in range(8):
            svc = grid_to_bcast("o", oc * 4)
            for tbp in range(2):
                last_mm = None
                for ic in range(NIC):
                    sl = wring.acquire()
                    wring.produce(sl, "gpsimd",
                                  (lambda g, oc=oc, ic=ic, sl=sl:
                                   g.dma_start(sl.buf,
                                               oT[ic * 128:(ic + 1) * 128,
                                                  oc * 512:(oc + 1) * 512])))
                    gs = sgnC.acquire()

                    def fn_sgn(s_, sl=sl, gs=gs):
                        return s_.activation(gs.buf, sl.buf, AF.Sign)
                    sgnC.produce(gs, "scalar", fn_sgn,
                                 extra_waits=[sl.prod_tok] + list(b_end))
                    sl.cons_toks.append(gs.prod_tok)
                    pw = [abm_tok]
                    if ic == 0:
                        pw = pw + psum8_guard \
                            + [t for t in o_bank_free if t is not None]

                    def fn_mm(t, ic=ic, tbp=tbp, gs=gs):
                        return [t.matmul(
                            psA[:, j * 512:(j + 1) * 512],
                            bigv(ic, (tbp * 4 + j) * 128,
                                 (tbp * 4 + j + 1) * 128),
                            gs.buf,
                            start=(ic == 0), stop=(ic == NIC - 1))
                            for j in range(4)]
                    last_mm = sgnC.consume(gs, "tensor", fn_mm,
                                              extra_waits=pw, inc="sPE",
                                              inc_by=4)
                for j in range(4):
                    tb = tbp * 4 + j
                    pk = tb % 2
                    tu = tuv2[pk]

                    def fn_e1(v, j=j, tu=tu):
                        return v.tensor_tensor(tu,
                                               psA[:, j * 512:(j + 1) * 512],
                                               svbc, op=ALU.mult)
                    e1 = em.emit("vector", fn_e1,
                                 waits=[last_mm, svc, tu_free2[pk]])
                    o_bank_free[j] = e1
                    psum8_guard.append(e1)

                    def fn_e2(g, tb=tb, oc=oc, tu=tu):
                        d = resv(tb)[:, oc * 512:(oc + 1) * 512]
                        return g.tensor_tensor(d, tu, d, op=ALU.add)
                    e2 = em.emit("gpsimd", fn_e2, waits=[e1])
                    tu_free2[pk] = [e2]
                    oev[tb] = e2
        c_end = [t for t in oev if t is not None]
    else:
        c_end = b_end

    if "D" in phases:
        emit_ffn(c_end, to_output=True)
        final_l = stOUT.all_toks()
    else:
        em.sem("qDUMP")
        def fn_dump(g):
            for tb in range(NTB):
                g.dma_start(y[tb * 128:(tb + 1) * 128, :],
                            resv(tb)).then_inc(SEMH["qDUMP"], 16)
            return None
        em.emit("gpsimd", fn_dump, waits=list(c_end) + [tok_res_all],
                inc="qDUMP", inc_by=16 * NTB, manual=True)
        final_l = [("qDUMP", em.counts["qDUMP"])]

    em.emit("sync", lambda sy: None, waits=final_l)

    with nc.Block() as blk:
        def runner(eng_name):
            def go(eng):
                for idx, (fn, waits, inc, lineno) in enumerate(em.ops[eng_name]):
                    try:
                        for s, v in waits:
                            eng.wait_ge(SEMH[s], v)
                        last = fn(eng)
                        for ins_ in (last if isinstance(last, list)
                                     else ([last] if last is not None else [])):
                            try:
                                ins_.annotate(f"{eng_name}:{idx}:L{lineno}")
                            except Exception:
                                pass
                        if inc is not None:
                            name, by = inc
                            if isinstance(last, list):
                                assert len(last) == by, \
                                    f"inc_by {by} != {len(last)} insts"
                                for ins_ in last:
                                    ins_.then_inc(SEMH[name], 1)
                            else:
                                assert last is not None, \
                                    f"no inst for inc {name}"
                                last.then_inc(SEMH[name], by)
                    except Exception as ex:
                        import traceback
                        co = fn.__code__
                        raise RuntimeError(
                            f"step {idx} on {eng_name} "
                            f"({co.co_filename}:{co.co_firstlineno}): {ex}"
                        ) from ex
            return go
        blk.tensor(runner("tensor"))
        blk.vector(runner("vector"))
        blk.scalar(runner("scalar"))
        blk.gpsimd(runner("gpsimd"))
        blk.sync(runner("sync"))
    nc._em = em
    return nc


def kernel(**inputs):
    x = np.asarray(inputs["L0_attn_a_hidden_states"], dtype=np.float32)
    wq = np.asarray(inputs["L0_attn_a_wq"], dtype=np.float32)
    wkv = np.asarray(inputs["L0_attn_a_wkv"], dtype=np.float32)
    wo_ = np.asarray(inputs["L0_attn_a_wo"], dtype=np.float32)
    C = np.ascontiguousarray
    qT = C(np.asarray(inputs["L0_attn_b_q_weight"], dtype=np.float32).T)
    kT = C(np.asarray(inputs["L0_attn_b_k_weight"], dtype=np.float32).T)
    vT = C(np.asarray(inputs["L0_attn_b_v_weight"], dtype=np.float32).T)
    oT = C(np.asarray(inputs["L0_attn_b_o_weight"], dtype=np.float32).T)
    fT = C(np.asarray(inputs["L0_ffn_weight"], dtype=np.float32).T)
    gwT = C(np.asarray(inputs["L0_ffn_gate_w"], dtype=np.float32).T)
    icsv = np.asarray(inputs["L0_ffn_in_channel_scale"], dtype=np.float32)
    ocsv = np.asarray(inputs["L0_ffn_out_channel_scale"], dtype=np.float32)
    fb = np.asarray(inputs["L0_ffn_bias"], dtype=np.float32).reshape(1, -1)
    qN = C(np.asarray(inputs["L0_attn_b_q_weight"], dtype=np.float32))
    kN = C(np.asarray(inputs["L0_attn_b_k_weight"], dtype=np.float32))
    vN = C(np.asarray(inputs["L0_attn_b_v_weight"], dtype=np.float32))
    oN = C(np.asarray(inputs["L0_attn_b_o_weight"], dtype=np.float32))
    qscg = C(np.asarray(inputs["L0_attn_b_q_scale"],
                        dtype=np.float32).reshape(32, 128).T)
    kscg = C(np.asarray(inputs["L0_attn_b_k_scale"],
                        dtype=np.float32).reshape(8, 128).T)
    vscg = C(np.asarray(inputs["L0_attn_b_v_scale"],
                        dtype=np.float32).reshape(8, 128).T)
    oscg = C(np.asarray(inputs["L0_attn_b_o_scale"],
                        dtype=np.float32).reshape(32, 128).T)

    in_maps = []
    for core in range(8):
        b, g = core // 2, core % 2
        qcols = np.concatenate([np.arange(h * 192, h * 192 + 128)
                                for h in range(g * 16, (g + 1) * 16)])
        in_maps.append({
            "xf": C(x[b]),
            "xo": C(x[b, g * TOK:(g + 1) * TOK]),
            "wq_my": C(wq[:, qcols]),
            "wk_my": C(wkv[:, g * 2048:(g + 1) * 2048]),
            "wv_my": C(wkv[:, 4096 + g * 2048:4096 + (g + 1) * 2048]),
            "wo": wo_, "qT": qT, "kT": kT, "vT": vT, "oT": oT, "fT": fT,
            "gwT": gwT, "ics": icsv, "ocs": ocsv, "fbias": fb,
            "qN": qN, "kN": kN, "vN": vN, "oN": oN,
            "qscg": qscg, "kscg": kscg, "vscg": vscg, "oscg": oscg,
        })

    nc = build()
    r = run_bass_kernel_spmd(nc, in_maps, core_ids=list(range(8)))
    out = np.empty((B, S, D), dtype=np.float32)
    for core in range(8):
        b, g = core // 2, core % 2
        out[b, g * TOK:(g + 1) * TOK] = np.asarray(r.results[core]["y"])
    return out
